# version 03
# baseline (speedup 1.0000x reference)
import contextlib
import re
import numpy as np
import ml_dtypes
import concourse.bass as bass
import concourse.mybir as mybir
from concourse.bass_utils import run_bass_kernel_spmd

F32 = mybir.dt.float32
BF16 = mybir.dt.bfloat16
I32 = mybir.dt.int32
AF = mybir.ActivationFunctionType
ALU = mybir.AluOpType
AX = mybir.AxisListType

B, S, D = 4, 2048, 2048
H, DH = 16, 128
NB = 16
T = 1024
NC = 16
HG = 8192
NE = 16
HM = 1408
HMT = 11
TOPK = 4
INVSQ = float(1.0 / np.sqrt(128.0))
LN_EPS = 1e-5
RMS_EPS = float(np.finfo(np.float32).eps)

ENGINES = ("tensor", "vector", "scalar", "sync", "gpsimd")

EV_KV0, EV_QNT, EV_KV1 = 0, 1, 2
NFLAG = 16


def _ceil_div(a, b):
    return (a + b - 1) // b


def js_of(kc):
    return max(0, _ceil_div(kc - 1, 2))


class Ctx:

    @staticmethod
    def canon(name):
        if name.startswith("bar"):
            return "bar"
        name = name.replace("_at1", "_at0")
        name = re.sub(r"^(ln_at0|lnF|lnM)", "LNx", name)
        return name

    def __init__(self, nc):
        self.nc = nc
        self.sems = {}
        self.counts = {}
        self.pbase = {}
        self.cur = None
        self.eng = None
        self._rn = 0

    def rn(self, p):
        self._rn += 1
        return f"{p}{self._rn}"

    def h(self, name):
        name = self.canon(name)
        if name not in self.sems:
            self.sems[name] = self.nc.alloc_semaphore(name=name)
        self.counts.setdefault(name, 0)
        return self.sems[name]

    def snapshot(self):
        return dict(self.counts)

    def start_pass(self, name, eng, base):
        self.cur = name
        self.eng = eng
        self.pbase = dict(base)
        self.counts = dict(base)

    def on(self, name):
        return self.cur == name

    def bump(self, name, k):
        name = self.canon(name)
        self.h(name)
        self.counts[name] += k
        return self.counts[name]

    def val(self, name):
        name = self.canon(name)
        self.h(name)
        return self.counts[name]

    def base(self, name):
        return self.pbase.get(self.canon(name), 0)

    def dma(self, q, out, in_, sem, **kw):
        self.h(sem)
        if self.on(q):
            self.eng.dma_start(out=out, in_=in_, **kw).then_inc(self.sems[self.canon(sem)], 16)
        self.bump(sem, 16)

    def wait(self, engname, sem, val=None):
        self.h(sem)
        if self.on(engname):
            tgt = self.val(sem) if val is None else self.base(sem) + val
            self.eng.wait_ge(self.sems[self.canon(sem)], tgt)

    def op(self, engname, fn, sem=None, k=1):
        if sem is not None:
            self.h(sem)
        if self.on(engname):
            ins = fn(self.eng)
            if sem is not None:
                ins.then_inc(self.sems[self.canon(sem)], k)
        if sem is not None:
            self.bump(sem, k)

    def mark(self, engname, sem, k=1):
        self.h(sem)
        if self.on(engname):
            self.eng.sem_inc(self.sems[self.canon(sem)], k)
        self.bump(sem, k)

    def barrier(self, name):
        s = self.h(name)
        for _ in ENGINES:
            self.bump(name, 1)
        if self.cur is not None:
            self.eng.sem_inc(s, 1)
            self.eng.wait_ge(s, self.val(name))

    def poll_flag(self, slot, fvals_sb, spoll_sb, sh_flags, donesem):
        self.h(donesem)
        if self.on("gpsimd"):
            g = self.eng
            with g.register() as loopv, g.register() as tmp, g.register() as want:
                g.reg_load(want, fvals_sb[0:1, slot : slot + 1])
                g.reg_mov(loopv, 1)
                with g.While(loopv):
                    g.dma_start(
                        out=spoll_sb[0:1, 0:1], in_=sh_flags[0:1, slot : slot + 1]
                    ).then_inc(self.sems["pollsem"], 16, skip_validation=True)
                    g.reg_load(tmp, spoll_sb[0:1, 0:1])
                    g.reg_alu(tmp, tmp, want, op=ALU.is_equal)
                    g.reg_alu(loopv, tmp, 1, op=ALU.not_equal)
            g.sem_inc(self.sems[self.canon(donesem)], 1)
        self.bump(donesem, 1)

    def write_flag(self, slot_p0, slot_p1, fvals_sb, sh_flags, pid_sb=None):
        if self.on("gpsimd"):
            g = self.eng
            with g.register(self.rn("wfpr")) as pr:
                g.reg_load(pr, pid_sb[0:1, 0:1])
                g.reg_alu(pr, pr, 1, op=ALU.bitwise_and)
                with g.If_eq(pr, 0):
                    g.dma_start(
                        out=sh_flags[0:1, slot_p0 : slot_p0 + 1],
                        in_=fvals_sb[0:1, slot_p0 : slot_p0 + 1],
                    ).then_inc(self.sems["pollsem"], 16, skip_validation=True)
                with g.If_eq(pr, 1):
                    g.dma_start(
                        out=sh_flags[0:1, slot_p1 : slot_p1 + 1],
                        in_=fvals_sb[0:1, slot_p1 : slot_p1 + 1],
                    ).then_inc(self.sems["pollsem"], 16, skip_validation=True)

    def poll_flag_pair(self, ev2, fvals_sb, spoll_sb, sh_flags, donesem, pid_sb=None):
        self.h(donesem)
        if self.on("gpsimd"):
            g = self.eng
            with g.register(self.rn("pfp")) as pr:
                g.reg_load(pr, pid_sb[0:1, 0:1])
                g.reg_alu(pr, pr, 1, op=ALU.bitwise_and)
                for mypar, slot in ((0, ev2 + 1), (1, ev2)):
                    with g.If_eq(pr, mypar):
                        with (
                            g.register(self.rn("pfl")) as loopv,
                            g.register(self.rn("pft")) as tmp,
                            g.register(self.rn("pfw")) as want,
                        ):
                            g.reg_load(want, fvals_sb[0:1, slot : slot + 1])
                            g.reg_mov(loopv, 1)
                            with g.While(loopv):
                                g.nop(cycle_cnt=4800)
                                g.dma_start(
                                    out=spoll_sb[0:1, 0:1],
                                    in_=sh_flags[0:1, slot : slot + 1],
                                ).then_inc(self.sems["pollsem"], 16, skip_validation=True)
                                g.reg_load(tmp, spoll_sb[0:1, 0:1])
                                g.reg_alu(tmp, tmp, want, op=ALU.is_equal)
                                g.reg_alu(loopv, tmp, 1, op=ALU.not_equal)
            g.sem_inc(self.sems[self.canon(donesem)], 1)
        self.bump(donesem, 1)

    def ifpar_dma(self, q, out_p0, out_p1, in_, sem, pid_sb=None):
        self.h(sem)
        if self.on(q):
            e = self.eng
            with e.register(self.rn("ipr")) as pr:
                e.reg_load(pr, pid_sb[0:1, 0:1])
                e.reg_alu(pr, pr, 1, op=ALU.bitwise_and)
                with e.If_eq(pr, 0):
                    e.dma_start(out=out_p0, in_=in_).then_inc(self.sems[self.canon(sem)], 16)
                with e.If_eq(pr, 1):
                    e.dma_start(out=out_p1, in_=in_).then_inc(self.sems[self.canon(sem)], 16)
        self.bump(sem, 16)



def own_blocks(p):
    return [g for g in range(NB) if g % 2 == p]


def make_dmask(p):
    m = np.zeros((NB, 128, 256), np.float32)
    blocks = own_blocks(p)
    for kc in range(NB):
        js = js_of(kc)
        for jj in range(2):
            j = js + jj
            col = jj * 128
            if j >= 8:
                continue
            g = blocks[j]
            if g < kc:
                pass
            elif g == kc:
                rk = np.arange(128)[:, None]
                rq = np.arange(128)[None, :]
                m[kc, :, col : col + 128] = (rk <= rq).astype(np.float32)
            else:
                m[kc, :, col : col + 128] = 1.0
    return m.astype(ml_dtypes.bfloat16)



def build_program(phases, detect_races=True):
    nc = bass.Bass("TRN2", target_bir_lowering=False, debug=False, num_devices=8,
                   detect_race_conditions=detect_races)
    dt = {}

    def din(name, shape, dtype=F32):
        dt[name] = nc.dram_tensor(name, list(shape), dtype, kind="ExternalInput")
        return dt[name]

    def dout(name, shape, dtype=F32):
        dt[name] = nc.dram_tensor(name, list(shape), dtype, kind="ExternalOutput")
        return dt[name]

    need = set(phases)

    din("fvals", (1, NFLAG), I32)
    din("ident_f32", (128, 128), F32)
    din("ident_bf", (128, 128), BF16)
    din("ones_bf", (128, 1), BF16)
    din("onesrow_f32", (1, 128), F32)

    if "load_x" in need:
        din("x_own", (T, D))
    if "loadT" in need:
        din("xT_in", (D, T))
    if "storeT" in need:
        dout("xT_out", (D, T))
    if "qdump" in need:
        dt["qgu_dump"] = nc.dram_tensor("qgu_dump", [2 * HG, D], BF16, kind="ExternalOutput")
        dt["qpr_dump"] = nc.dram_tensor("qpr_dump", [D, HG], BF16, kind="ExternalOutput")
    if "store" in need:
        dout("y_own", (T, D))

    if "l0attn" in need or "l1attn" in need:
        din("dmask", (NB, 128, 256), BF16)
    if "l0attn" in need:
        din("L0_attn_W_Q", (H, D, DH))
        din("L0_attn_W_K", (H, D, DH))
        din("L0_attn_W_V", (H, D, DH))
        din("L0_attn_W_O", (H, DH, D))
        din("bqT", (128, H))
        din("bkT", (128, H))
        din("bvT", (128, H))
        din("boT", (128, NC))
    if "l1attn" in need:
        din("wqkvT", (D, 3 * D))
        din("woutT", (D, D))
        din("bqkvT", (128, 3 * H))
        din("boutT", (128, NC))
    if "quant" in need:
        din("qgu_my", (HG, D))
        din("qpr_my", (D // 2, HG))
    if "moe" in need or "moe_noln" in need:
        din("gate_wT", (D, NE))
        din("gate_b_col", (NE, 1))
        din("sel16", (NE, NE * 128))
        din("L1_ffn_w1_shared", (D, HM))
        din("L1_ffn_w2_shared", (HM, D))
        din("L1_ffn_w3_shared", (D, HM))
        din("L1_ffn_w1_routed", (NE, D, HM))
        din("L1_ffn_w2_routed", (NE, HM, D))
        din("L1_ffn_w3_routed", (NE, D, HM))

    sh_flags = nc.dram_tensor("sh_flags", [1, NFLAG + 8], I32, addr_space="Shared")
    kv = {}
    for layer in (0, 1):
        if ("l0attn" if layer == 0 else "l1attn") in need:
            kv[layer] = (
                nc.dram_tensor(f"kvK{layer}", [2, H, 128, T], BF16, addr_space="Shared"),
                nc.dram_tensor(f"kvV{layer}", [2, H, 8, 128, 128], BF16, addr_space="Shared"),
            )
    if "quant" in need or "ffn0" in need:
        qgu_hat = nc.dram_tensor("qgu_hat", [2 * HG, D], BF16, addr_space="Shared")
        qpr_hat = nc.dram_tensor("qpr_hat", [D, HG], BF16, addr_space="Shared")
    else:
        qgu_hat = qpr_hat = None

    cx = Ctx(nc)
    cx.h("pollsem")

    with (
        nc.sbuf_tensor("xr", [128, NC * T], F32) as xr,
        nc.sbuf_tensor("xb", [128, NC * T], BF16) as xb,
        nc.sbuf_tensor("fvals_sb", [1, NFLAG], I32) as fvals_sb,
        nc.sbuf_tensor("spoll", [1, 1], I32) as spoll,
        nc.sbuf_tensor("pid_sb", [1, 1], mybir.dt.uint32) as pid_sb,
        nc.sbuf_tensor("epsc", [1, 2], F32) as epsc,
        nc.sbuf_tensor("ident_f32_sb", [128, 128], F32) as idf,
        nc.sbuf_tensor("ident_bf_sb", [128, 128], BF16) as idb,
        nc.sbuf_tensor("ones_bf_sb", [128, 1], BF16) as onesb,
        nc.sbuf_tensor("onesrow_sb", [1, 128], F32) as onesr,
        nc.Block() as block,
    ):
        bufs = dict(
            xr=xr, xb=xb, fvals=fvals_sb, spoll=spoll, idf=idf, idb=idb,
            onesb=onesb, onesr=onesr, dt=dt, kv=kv, sh_flags=sh_flags,
            qgu_hat=qgu_hat, qpr_hat=qpr_hat, pid_sb=pid_sb, epsc=epsc,
        )

        def run_phase(emitfn, pbufs):
            base = cx.snapshot()
            for engname in ENGINES:
                eng = getattr(nc, engname)

                def seg(e, _emitfn=emitfn, _n=engname, _pb=pbufs):
                    cx.start_pass(_n, e, base)
                    _emitfn(cx, bufs, _pb)

                getattr(block, engname)(seg)

        def emit_prologue(cx, b, pb):
            cx.op("gpsimd", lambda g: g.memset(b["spoll"][:], 0), "clds0")
            cx.op("gpsimd", lambda g: g.memset(b["epsc"][0:1, 0:1], LN_EPS), "clds0")
            cx.op("gpsimd", lambda g: g.memset(b["epsc"][0:1, 1:2], RMS_EPS), "clds0")
            cx.dma("sync", b["pid_sb"][:], nc.partition_id_tensor.ap(), "clds")
            cx.dma("sync", b["fvals"][:], dt["fvals"].ap(), "clds")
            cx.dma("sync", b["idf"][:], dt["ident_f32"].ap(), "clds")
            cx.dma("sync", b["idb"][:], dt["ident_bf"].ap(), "clds")
            cx.dma("sync", b["onesb"][:], dt["ones_bf"].ap(), "clds")
            cx.dma("sync", b["onesr"][:], dt["onesrow_f32"].ap(), "clds")
            for e in ENGINES:
                cx.wait(e, "clds")
            cx.barrier("bar_prologue")

        run_phase(emit_prologue, {})

        for ph in phases:
            if ph == "load_x":
                with (
                    nc.sbuf_tensor("sx0", [128, D], F32) as sx0,
                    nc.sbuf_tensor("sx1", [128, D], F32) as sx1,
                    nc.psum_tensor("ptr0", [128, 128], F32) as ptr0,
                    nc.psum_tensor("ptr1", [128, 128], F32) as ptr1,
                    nc.psum_tensor("ptr2", [128, 128], F32) as ptr2,
                    nc.psum_tensor("ptr3", [128, 128], F32) as ptr3,
                ):
                    pb = dict(sx=[sx0, sx1], ptr=[ptr0, ptr1, ptr2, ptr3])
                    run_phase(emit_load_x, pb)
            elif ph == "loadT":
                run_phase(emit_loadT, {})
            elif ph == "storeT":
                run_phase(emit_storeT, {})
            elif ph == "store":
                with (
                    nc.sbuf_tensor("sy0", [128, D], F32) as sy0,
                    nc.sbuf_tensor("sy1", [128, D], F32) as sy1,
                    nc.psum_tensor("pso0", [128, 128], F32) as pso0,
                    nc.psum_tensor("pso1", [128, 128], F32) as pso1,
                    nc.psum_tensor("pso2", [128, 128], F32) as pso2,
                    nc.psum_tensor("pso3", [128, 128], F32) as pso3,
                ):
                    pb = dict(sy=[sy0, sy1], pso=[pso0, pso1, pso2, pso3])
                    run_phase(emit_store, pb)
            elif ph in ("l0attn", "l1attn"):
                layer = 0 if ph == "l0attn" else 1
                emit_attention_phase(nc, cx, block, bufs, run_phase, layer)
            elif ph == "quant":
                emit_quant_phase(nc, cx, block, bufs, run_phase)
            elif ph == "ffn0":
                emit_ffn0_phase(nc, cx, block, bufs, run_phase)
            elif ph == "moe":
                emit_moe_phase(nc, cx, block, bufs, run_phase)
            elif ph == "moe_noln":
                emit_moe_phase(nc, cx, block, bufs, run_phase, skip_ln=True)
            elif ph == "qdump":
                def emit_qd(cx, b, pb):
                    cx.wait("sync", "qrdy")
                    cx.dma("sync", dt["qgu_dump"].ap(), b["qgu_hat"].ap(), "final_store")
                    cx.dma("sync", dt["qpr_dump"].ap(), b["qpr_hat"].ap(), "final_store")
                    cx.wait("sync", "final_store")
                run_phase(emit_qd, {})
            else:
                raise ValueError(ph)

        def emit_epilogue(cx, b, pb):
            cx.barrier("bar_epilogue")

        run_phase(emit_epilogue, {})

    return nc



def emit_load_x(cx, b, pb):
    x_own = b["dt"]["x_own"]
    for j in range(8):
        sx = pb["sx"][j % 2]
        if j >= 2:
            cx.wait("sync", "p0done", val=16 * (j - 1))
        cx.dma("sync", sx[:], x_own[j * 128 : (j + 1) * 128, :], "p0ld")
        cx.wait("tensor", "p0ld")
        for dc in range(NC):
            pt = pb["ptr"][dc % 4]
            n = j * NC + dc
            if n >= 4:
                cx.wait("tensor", "p0ev", val=2 * (n - 3))
            cx.op("tensor", lambda t, _sx=sx, _pt=pt, _dc=dc: t.transpose(
                _pt[:], _sx[:, _dc * 128 : (_dc + 1) * 128], b["idf"][:]), "p0tr")
            cx.wait("vector", "p0tr")
            cx.op("vector", lambda v, _pt=pt, _dc=dc, _j=j: v.tensor_copy(
                b["xr"][:, _dc * T + _j * 128 : _dc * T + (_j + 1) * 128], _pt[:]), "p0ev")
            cx.op("vector", lambda v, _pt=pt, _dc=dc, _j=j: v.tensor_copy(
                b["xb"][:, _dc * T + _j * 128 : _dc * T + (_j + 1) * 128], _pt[:]), "p0ev")
        cx.wait("sync", "p0ev")
        cx.mark("sync", "p0done", 16)
    cx.barrier("bar_load_x")


def emit_loadT(cx, b, pb):
    xT = b["dt"]["xT_in"]
    cx.dma("sync", b["xr"].ap().rearrange("p (c t) -> p c t", t=T),
           xT.ap().rearrange("(c p) t -> p c t", p=128), "pTld")
    cx.wait("vector", "pTld")
    for dc in range(NC):
        cx.op("vector", lambda v, _dc=dc: v.tensor_copy(
            b["xb"][:, _dc * T : (_dc + 1) * T], b["xr"][:, _dc * T : (_dc + 1) * T]), "pTev")
    cx.barrier("bar_loadT")


def emit_storeT(cx, b, pb):
    cx.barrier("bar_storeT0")
    xT = b["dt"]["xT_out"]
    cx.dma("sync", xT.ap().rearrange("(c p) t -> p c t", p=128),
           b["xr"].ap().rearrange("p (c t) -> p c t", t=T), "final_store")
    cx.wait("sync", "final_store")


def emit_store(cx, b, pb):
    cx.barrier("bar_store0")
    y = b["dt"]["y_own"]
    for j in range(8):
        sy = pb["sy"][j % 2]
        if j >= 2:
            cx.wait("vector", "final_store", val=16 * j)
        for dc in range(NC):
            ps = pb["pso"][dc % 4]
            n = j * NC + dc
            if n >= 4:
                cx.wait("tensor", "p6ev", val=n - 3)
            cx.op("tensor", lambda t, _ps=ps, _dc=dc, _j=j: t.transpose(
                _ps[:], b["xr"][:, _dc * T + _j * 128 : _dc * T + (_j + 1) * 128],
                b["idf"][:]), "p6tr")
            cx.wait("vector", "p6tr")
            cx.op("vector", lambda v, _ps=ps, _sy=sy, _dc=dc: v.tensor_copy(
                _sy[:, _dc * 128 : (_dc + 1) * 128], _ps[:]), "p6ev")
        cx.wait("sync", "p6ev")
        cx.dma("sync", y[j * 128 : (j + 1) * 128, :], sy[:], "final_store")
    cx.wait("sync", "final_store")



def emit_attention_phase(nc, cx, block, b, run_phase, layer):
    dt = b["dt"]
    if layer == 0:
        def wap(part, h):
            t = dt[{"q": "L0_attn_W_Q", "k": "L0_attn_W_K", "v": "L0_attn_W_V"}[part]]
            return t.ap()[h].rearrange("(c p) e -> p c e", p=128)

        def wo_ap(h, dtg):
            return dt["L0_attn_W_O"].ap()[h][:, dtg * 512 : (dtg + 1) * 512]

        bias_col = {"q": ("bqT",), "k": ("bkT",), "v": ("bvT",)}
        bo_name = "boT"
        ev = EV_KV0
    else:
        def wap(part, h):
            sec = {"q": 0, "k": 1, "v": 2}[part]
            col = sec * D + h * 128
            return dt["wqkvT"].ap()[:, col : col + 128].rearrange("(c p) e -> p c e", p=128)

        def wo_ap(h, dtg):
            return dt["woutT"].ap()[h * 128 : (h + 1) * 128, dtg * 512 : (dtg + 1) * 512]

        bias_col = {"q": ("bqkvT", 0), "k": ("bqkvT", 1), "v": ("bqkvT", 2)}
        bo_name = "boutT"
        ev = EV_KV0 if layer == 0 else EV_KV1
    kvK, kvV = b["kv"][layer]
    sfx = f"_at{layer}"

    def bias_ap(part, h):
        if layer == 0:
            return dt_sb[{"q": "bq", "k": "bk", "v": "bv"}[part]][:, h : h + 1]
        sec = {"q": 0, "k": 1, "v": 2}[part]
        return dt_sb["bqkv"][:, sec * H + h : sec * H + h + 1]

    with (
        nc.sbuf_tensor(f"QT{sfx}", [128, H * T], BF16) as QT,
        nc.sbuf_tensor(f"biasq{sfx}", [128, 3 * H], F32) as biasq,
        nc.sbuf_tensor(f"biaso{sfx}", [128, NC], F32) as biaso,
        nc.sbuf_tensor(f"dmask{sfx}", [128, NB * 256], BF16) as dmask_sb,
    ):
        dt_sb = dict(bq=biasq[:, 0:H], bk=biasq[:, H : 2 * H], bv=biasq[:, 2 * H : 3 * H],
                     bqkv=biasq)

        def emit_A(cx, b, pb):
            wsl = pb["wsl"]
            ppj = pb["ppj"]
            pvt = pb["pvt"]
            KT, VT, Vtm = pb["KT"], pb["VT"], pb["Vtm"]
            if layer == 0:
                cx.dma("sync", biasq[:, 0:H], dt["bqT"].ap(), "bld" + sfx)
                cx.dma("sync", biasq[:, H : 2 * H], dt["bkT"].ap(), "bld" + sfx)
                cx.dma("sync", biasq[:, 2 * H : 3 * H], dt["bvT"].ap(), "bld" + sfx)
            else:
                cx.dma("sync", biasq[:], dt["bqkvT"].ap(), "bld" + sfx)
            cx.dma("sync", biaso[:], dt[bo_name].ap(), "bld" + sfx)
            cx.dma("sync", dmask_sb.ap().rearrange("p (k q) -> p k q", q=256),
                   dt["dmask"].ap().rearrange("k p q -> p k q"), "bld" + sfx)
            cx.wait("scalar", "bld" + sfx)

            npj = [0]
            for h in range(H):
                slot = wsl[h % 2]
                if h >= 2:
                    cx.wait("gpsimd", "pjmm" + sfx, val=3 * (h - 1))
                for part in ("q", "k", "v"):
                    cx.dma("gpsimd", slot[part].ap().rearrange("p (c e) -> p c e", e=128),
                           wap(part, h), f"wld{h % 2}" + sfx)
                for pi, part in enumerate(("q", "k", "v")):
                    pp = ppj[pi % 2]
                    idx = h * 3 + pi
                    if idx >= 2:
                        cx.wait("tensor", "pjev" + sfx, val=idx - 1)
                    cx.wait("tensor", f"wld{h % 2}" + sfx, val=48 * (h // 2 + 1))
                    for dc in range(NC):
                        for th in range(2):
                            last = dc == NC - 1 and th == 1
                            cx.op("tensor", lambda t, _pp=pp, _s=slot[part], _dc=dc, _th=th: t.matmul(
                                _pp[:, _th * 512 : (_th + 1) * 512],
                                _s[:, _dc * 128 : (_dc + 1) * 128],
                                b["xb"][:, _dc * T + _th * 512 : _dc * T + (_th + 1) * 512],
                                start=(_dc == 0), stop=(_dc == NC - 1)),
                                sem="pjmm" + sfx if last else None)
                    dest = {"q": QT[:, h * T : (h + 1) * T], "k": KT[:], "v": VT[:]}[part]
                    cx.wait("scalar", "pjmm" + sfx)
                    if part in ("k", "v") and h >= 1:
                        cx.wait("scalar", "kxwr" + sfx, val=32 * h)
                    cx.op("scalar", lambda s, _d=dest, _pp=pp, _pt=part, _h=h: s.activation(
                        _d, _pp[:], AF.Identity, bias=bias_ap(_pt, _h)), "pjev" + sfx)
                for j in range(8):
                    pv = pvt[j % 2]
                    if j >= 2:
                        cx.wait("tensor", "vtev" + sfx, val=8 * h + j - 1)
                    cx.wait("tensor", "pjev" + sfx)
                    cx.op("tensor", lambda t, _pv=pv, _j=j: t.transpose(
                        _pv[:], VT[:, _j * 128 : (_j + 1) * 128], b["idb"][:]), "vttr" + sfx)
                    cx.wait("scalar", "vttr" + sfx)
                    cx.op("scalar", lambda s, _pv=pv, _j=j: s.copy(
                        Vtm[:, _j * 128 : (_j + 1) * 128], _pv[:]), "vtev" + sfx)
                cx.wait("sync", "pjev" + sfx)
                cx.wait("sync", "vtev" + sfx)
                cx.ifpar_dma("sync", kvK[0, h], kvK[1, h], KT[:], "kxwr" + sfx, b["pid_sb"])
                cx.ifpar_dma("sync",
                             kvV[0, h].rearrange("li p r -> p li r"),
                             kvV[1, h].rearrange("li p r -> p li r"),
                             Vtm.ap().rearrange("p (j e) -> p j e", e=128), "kxwr" + sfx,
                             b["pid_sb"])
            cx.wait("gpsimd", "kxwr" + sfx)
            cx.write_flag(2 * ev, 2 * ev + 1, b["fvals"], b["sh_flags"], b["pid_sb"])
            cx.poll_flag_pair(2 * ev, b["fvals"], b["spoll"], b["sh_flags"], "kvrdy" + sfx, b["pid_sb"])
            cx.barrier("barA" + sfx)

        with (
            nc.sbuf_tensor(f"wq{sfx}0", [128, NC * 128], BF16) as w00,
            nc.sbuf_tensor(f"wk{sfx}0", [128, NC * 128], BF16) as w01,
            nc.sbuf_tensor(f"wv{sfx}0", [128, NC * 128], BF16) as w02,
            nc.sbuf_tensor(f"wq{sfx}1", [128, NC * 128], BF16) as w10,
            nc.sbuf_tensor(f"wk{sfx}1", [128, NC * 128], BF16) as w11,
            nc.sbuf_tensor(f"wv{sfx}1", [128, NC * 128], BF16) as w12,
            nc.sbuf_tensor(f"KT{sfx}", [128, T], BF16) as KT,
            nc.sbuf_tensor(f"VT{sfx}", [128, T], BF16) as VT,
            nc.sbuf_tensor(f"Vtm{sfx}", [128, T], BF16) as Vtm,
            nc.psum_tensor(f"ppj{sfx}0", [128, T], F32) as ppj0,
            nc.psum_tensor(f"ppj{sfx}1", [128, T], F32) as ppj1,
            nc.psum_tensor(f"pvt{sfx}0", [128, 128], BF16) as pvt0,
            nc.psum_tensor(f"pvt{sfx}1", [128, 128], BF16) as pvt1,
        ):
            pb = dict(
                wsl=[dict(q=w00, k=w01, v=w02), dict(q=w10, k=w11, v=w12)],
                ppj=[ppj0, ppj1], pvt=[pvt0, pvt1], KT=KT, VT=VT, Vtm=Vtm,
            )
            run_phase(emit_A, pb)

        with (
            nc.sbuf_tensor(f"zsb{sfx}", [128, H * T], BF16) as zsb,
            nc.sbuf_tensor(f"ktf{sfx}0", [128, S], BF16) as ktf0,
            nc.sbuf_tensor(f"ktf{sfx}1", [128, S], BF16) as ktf1,
            nc.sbuf_tensor(f"vtf{sfx}0", [128, S], BF16) as vtf0,
            nc.sbuf_tensor(f"vtf{sfx}1", [128, S], BF16) as vtf1,
        ):

            def emit_B(cx, b, pb):
                PT = pb["PT"]
                pst, pzt = pb["pst"], pb["pzt"]
                prs0, prs1 = pb["prs0"], pb["prs1"]
                pbc = pb["pbc"]
                rsv, rb = pb["rsv"], pb["rb"]
                ktf, vtf = [ktf0, ktf1], [vtf0, vtf1]
                nkc = [0]
                cx.wait("sync", "kvrdy" + sfx)
                for h in range(H):
                    kb, vb = ktf[h % 2], vtf[h % 2]
                    if h >= 2:
                        cx.wait("sync", "bcmm" + sfx, val=h - 1)
                    for p_ in range(2):
                        cx.dma("sync",
                               kb.ap().rearrange("p (li half) -> p li half", half=256)[:, :, p_ * 128 : (p_ + 1) * 128],
                               kvK[p_, h].rearrange("p (li r) -> p li r", r=128),
                               "kvrd" + sfx)
                        cx.dma("sync",
                               vb.ap().rearrange("p (li half) -> p li half", half=256)[:, :, p_ * 128 : (p_ + 1) * 128],
                               kvV[p_, h].rearrange("li p r -> p li r"),
                               "kvrd" + sfx)
                    cx.wait("tensor", "kvrd" + sfx)
                    if h >= 1:
                        cx.wait("tensor", "ztev" + sfx, val=h)
                        cx.wait("tensor", "rn1" + sfx, val=2 * h)
                    for kc in range(NB):
                        a = js_of(kc) * 128
                        segs = [(a, 512), (512, 1024)] if a < 512 else [(a, 1024)]
                        if nkc[0] >= 1:
                            cx.wait("tensor", "expk" + sfx, val=nkc[0])
                            cx.wait("scalar", "avk" + sfx, val=nkc[0] - 1)
                        for si, (s0, s1) in enumerate(segs):
                            cx.op("tensor", lambda t, _kb=kb, _kc=kc, _s0=s0, _s1=s1: t.matmul(
                                pst[:, _s0:_s1], _kb[:, _kc * 128 : (_kc + 1) * 128],
                                QT[:, h * T + _s0 : h * T + _s1],
                                start=True, stop=True),
                                sem="stmm" + sfx if si == len(segs) - 1 else None)
                        cx.wait("scalar", "stmm" + sfx)
                        cx.op("scalar", lambda s, _a=a: s.activation(
                            PT[:, _a:T], pst[:, _a:T], AF.Exp, scale=INVSQ), "expk" + sfx)
                        cx.wait("vector", "expk" + sfx)
                        mw = min(256, T - a)
                        cx.op("vector", lambda v, _a=a, _kc=kc, _mw=mw: v.tensor_mul(
                            PT[:, _a : _a + _mw], PT[:, _a : _a + _mw],
                            dmask_sb[:, _kc * 256 : _kc * 256 + _mw]), "maskk" + sfx)
                        cx.wait("tensor", "maskk" + sfx)
                        for si, (s0, s1) in enumerate(segs):
                            stop_seg = (kc == 7) if s0 < 512 else (kc == NB - 1)
                            cx.op("tensor", lambda t, _vb=vb, _kc=kc, _s0=s0, _s1=s1, _st=stop_seg: t.matmul(
                                pzt[:, _s0:_s1], _vb[:, _kc * 128 : (_kc + 1) * 128],
                                PT[:, _s0:_s1],
                                start=(_kc == 0), stop=_st))
                            prs = prs0 if s0 < 512 else prs1
                            ro = 0 if s0 < 512 else 512
                            last = si == len(segs) - 1
                            cx.op("tensor", lambda t, _prs=prs, _ro=ro, _s0=s0, _s1=s1, _kc=kc, _st=stop_seg: t.matmul(
                                _prs[0:1, _s0 - _ro : _s1 - _ro], b["onesb"][:],
                                PT[:, _s0:_s1],
                                start=(_kc == 0), stop=_st),
                                sem="avk" + sfx if last else None, k=1)
                        nkc[0] += 1
                    cx.wait("vector", "avk" + sfx)
                    if h >= 1:
                        cx.wait("vector", "bcmm" + sfx, val=h)
                    cx.op("vector", lambda v: v.tensor_copy(rsv[0:1, 0:512], prs0[0:1, :]), "rnc" + sfx)
                    cx.op("vector", lambda v: v.tensor_copy(rsv[0:1, 512:1024], prs1[0:1, :]), "rnc" + sfx)
                    cx.wait("vector", "rnc" + sfx)
                    cx.op("vector", lambda v: v.reciprocal(rsv[0:1, T : 2 * T], rsv[0:1, 0:T]), "rn1" + sfx, k=2)
                    cx.wait("tensor", "rn1" + sfx)
                    if h >= 1:
                        cx.wait("tensor", "bcev" + sfx, val=h)
                    for th in range(2):
                        cx.op("tensor", lambda t, _th=th: t.matmul(
                            pbc[:, _th * 512 : (_th + 1) * 512], b["onesr"][:],
                            rsv[0:1, T + _th * 512 : T + (_th + 1) * 512], start=True, stop=True),
                            sem="bcmm" + sfx if th == 1 else None)
                    cx.wait("scalar", "bcmm" + sfx)
                    cx.op("scalar", lambda s: s.copy(rb[:], pbc[:]), "bcev" + sfx)
                    cx.wait("vector", "bcev" + sfx)
                    cx.op("vector", lambda v, _h=h: v.tensor_mul(
                        zsb[:, _h * T : (_h + 1) * T], pzt[:], rb[:]), "ztev" + sfx)
                cx.barrier("barB" + sfx)

            with (
                nc.sbuf_tensor(f"PT{sfx}", [128, T], BF16) as PT,
                nc.sbuf_tensor(f"rsv{sfx}", [1, 2 * T], F32) as rsv,
                nc.sbuf_tensor(f"rb{sfx}", [128, T], F32) as rb,
                nc.psum_tensor(f"pst{sfx}", [128, T], F32) as pst,
                nc.psum_tensor(f"pzt{sfx}", [128, T], F32) as pzt,
                nc.psum_tensor(f"prs0{sfx}", [1, 512], F32) as prs0,
                nc.psum_tensor(f"prs1{sfx}", [1, 512], F32) as prs1,
                nc.psum_tensor(f"pbc{sfx}", [128, T], F32) as pbc,
            ):
                pb = dict(PT=PT, pst=pst, pzt=pzt, prs0=prs0, prs1=prs1, pbc=pbc,
                          rsv=rsv, rb=rb)
                run_phase(emit_B, pb)

            def emit_C(cx, b, pb):
                wos = pb["wos"]
                po = pb["po"]
                for dtg in range(4):
                    for h in range(H):
                        slot = wos[h % 2]
                        idx = dtg * H + h
                        if idx >= 2:
                            cx.wait("gpsimd", "pomm" + sfx, val=idx - 1)
                        cx.dma("gpsimd", slot[:], wo_ap(h, dtg), f"wold{idx % 2}" + sfx)
                        cx.wait("tensor", f"wold{idx % 2}" + sfx, val=16 * (idx // 2 + 1))
                        if h == 0:
                            cx.wait("tensor", "poev" + sfx, val=8 * dtg)
                        for i in range(4):
                            for th in range(2):
                                last = i == 3 and th == 1
                                cx.op("tensor", lambda t, _po=po[i], _s=slot, _i=i, _th=th, _h=h: t.matmul(
                                    _po[:, _th * 512 : (_th + 1) * 512],
                                    _s[:, _i * 128 : (_i + 1) * 128],
                                    zsb[:, _h * T + _th * 512 : _h * T + (_th + 1) * 512],
                                    start=(_h == 0), stop=(_h == H - 1)),
                                    sem="pomm" + sfx if last else None)
                    cx.wait("vector", "pomm" + sfx)
                    for i in range(4):
                        dti = dtg * 4 + i
                        for th in range(2):
                            cx.op("vector", lambda v, _po=po[i], _dti=dti, _th=th: v.scalar_tensor_tensor(
                                b["xr"][:, _dti * T + _th * 512 : _dti * T + (_th + 1) * 512],
                                _po[:, _th * 512 : (_th + 1) * 512],
                                biaso[:, _dti : _dti + 1],
                                b["xr"][:, _dti * T + _th * 512 : _dti * T + (_th + 1) * 512],
                                op0=ALU.add, op1=ALU.add), "poev" + sfx)
                cx.barrier("barC" + sfx)

            with (
                nc.sbuf_tensor(f"wo{sfx}0", [128, 512], BF16) as wos0,
                nc.sbuf_tensor(f"wo{sfx}1", [128, 512], BF16) as wos1,
                nc.psum_tensor(f"po{sfx}0", [128, T], F32) as po0,
                nc.psum_tensor(f"po{sfx}1", [128, T], F32) as po1,
                nc.psum_tensor(f"po{sfx}2", [128, T], F32) as po2,
                nc.psum_tensor(f"po{sfx}3", [128, T], F32) as po3,
            ):
                run_phase(emit_C, dict(wos=[wos0, wos1], po=[po0, po1, po2, po3]))

    emit_postln_phase(nc, cx, block, b, run_phase, "ln" + sfx)



def emit_postln_phase(nc, cx, block, b, run_phase, tag):
    def emit_D(cx, b, pb):
        xq, tmpf, strow, sbc = pb["xq"], pb["tmpf"], pb["strow"], pb["sbc"]
        ps1a, ps1b, ps2a, ps2b = pb["ps"]
        pbc1, pbc2 = pb["pbc1"], pb["pbc2"]
        xr, xb = b["xr"], b["xb"]
        for dc in range(NC):
            cx.op("vector", lambda v, _dc=dc: v.tensor_copy(
                xb[:, _dc * T : (_dc + 1) * T], xr[:, _dc * T : (_dc + 1) * T]), tag + "cb")
        cx.wait("scalar", tag + "cb")
        for dc in range(NC):
            cx.op("scalar", lambda s, _dc=dc: s.activation(
                xq[:, _dc * T : (_dc + 1) * T], xb[:, _dc * T : (_dc + 1) * T],
                AF.Square), tag + "sq")
        cx.wait("tensor", tag + "cb")
        cx.wait("tensor", tag + "sq")
        for dc in range(NC):
            for th in range(2):
                for src, ps in ((xb, ps1a if th == 0 else ps1b),
                                (xq, ps2a if th == 0 else ps2b)):
                    cx.op("tensor", lambda t, _src=src, _ps=ps, _dc=dc, _th=th: t.matmul(
                        _ps[0:1, :], b["onesb"][:],
                        _src[:, _dc * T + _th * 512 : _dc * T + (_th + 1) * 512],
                        start=(_dc == 0), stop=(_dc == NC - 1)),
                        sem=tag + "st" if dc == NC - 1 else None)
        s1 = lambda: strow[0:1, 0:T]
        s2 = lambda: strow[0:1, T : 2 * T]
        mean = lambda: strow[0:1, 2 * T : 3 * T]
        var = lambda: strow[0:1, 3 * T : 4 * T]
        rstd = lambda: strow[0:1, 4 * T : 5 * T]
        nm = lambda: strow[0:1, 5 * T : 6 * T]
        cx.wait("vector", tag + "st")
        def vsd(fn):
            cx.op("vector", fn, tag + "vc")
            cx.wait("vector", tag + "vc")
        vsd(lambda v: v.tensor_copy(s1()[:, 0:512], ps1a[0:1, :]))
        vsd(lambda v: v.tensor_copy(s1()[:, 512:1024], ps1b[0:1, :]))
        vsd(lambda v: v.tensor_copy(s2()[:, 0:512], ps2a[0:1, :]))
        vsd(lambda v: v.tensor_copy(s2()[:, 512:1024], ps2b[0:1, :]))
        vsd(lambda v: v.tensor_scalar_mul(mean(), s1(), 1.0 / D))
        vsd(lambda v: v.tensor_scalar_mul(s2(), s2(), 1.0 / D))
        vsd(lambda v: v.tensor_mul(var(), mean(), mean()))
        cx.op("vector", lambda v: v.tensor_sub(var(), s2(), var()), tag + "v1")
        cx.wait("scalar", tag + "v1")
        cx.op("scalar", lambda s: s.activation(var(), var(), AF.Ln, bias=b["epsc"][0:1, 0:1]), tag + "sc1")
        cx.wait("scalar", tag + "sc1")
        cx.op("scalar", lambda s: s.activation(rstd(), var(), AF.Exp, scale=-0.5), tag + "v2")
        cx.wait("vector", tag + "v2")
        cx.op("vector", lambda v: v.scalar_tensor_tensor(
            nm(), mean(), -1.0, rstd(), op0=ALU.mult, op1=ALU.mult), tag + "v3")
        cx.wait("tensor", tag + "v3")
        for th in range(2):
            cx.op("tensor", lambda t, _th=th: t.matmul(
                pbc1[:, _th * 512 : (_th + 1) * 512], b["onesr"][:],
                rstd()[:, _th * 512 : (_th + 1) * 512], start=True, stop=True))
            cx.op("tensor", lambda t, _th=th: t.matmul(
                pbc2[:, _th * 512 : (_th + 1) * 512], b["onesr"][:],
                nm()[:, _th * 512 : (_th + 1) * 512], start=True, stop=True),
                sem=tag + "bc" if th == 1 else None)
        cx.wait("scalar", tag + "bc")
        cx.op("scalar", lambda s: s.copy(sbc[:, 0:T], pbc1[:]))
        cx.op("scalar", lambda s: s.copy(sbc[:, T : 2 * T], pbc2[:]), tag + "bce")
        cx.wait("vector", tag + "bce")
        for dc in range(NC):
            cx.op("vector", lambda v, _dc=dc: v.tensor_mul(
                tmpf[:], xr[:, _dc * T : (_dc + 1) * T], sbc[:, 0:T]), tag + "vc")
            cx.wait("vector", tag + "vc")
            cx.op("vector", lambda v, _dc=dc: v.tensor_add(
                xr[:, _dc * T : (_dc + 1) * T], tmpf[:], sbc[:, T : 2 * T]), tag + "ap")
        for dc in range(NC):
            cx.wait("scalar", tag + "ap", val=dc + 1)
            cx.op("scalar", lambda s, _dc=dc: s.copy(
                xb[:, _dc * T : (_dc + 1) * T], xr[:, _dc * T : (_dc + 1) * T]), tag + "cb2")
        cx.barrier("bar" + tag)

    with (
        nc.sbuf_tensor(tag + "xq", [128, NC * T], BF16) as xq,
        nc.sbuf_tensor(tag + "tmpf", [128, T], F32) as tmpf,
        nc.sbuf_tensor(tag + "strow", [1, 8 * T], F32) as strow,
        nc.sbuf_tensor(tag + "sbc", [128, 2 * T], F32) as sbc,
        nc.psum_tensor(tag + "ps1a", [1, 512], F32) as ps1a,
        nc.psum_tensor(tag + "ps1b", [1, 512], F32) as ps1b,
        nc.psum_tensor(tag + "ps2a", [1, 512], F32) as ps2a,
        nc.psum_tensor(tag + "ps2b", [1, 512], F32) as ps2b,
        nc.psum_tensor(tag + "pbc1", [128, T], F32) as pbc1,
        nc.psum_tensor(tag + "pbc2", [128, T], F32) as pbc2,
    ):
        run_phase(emit_D, dict(xq=xq, tmpf=tmpf, strow=strow, sbc=sbc,
                               ps=[ps1a, ps1b, ps2a, ps2b], pbc1=pbc1, pbc2=pbc2))



def emit_quant_phase(nc, cx, block, b, run_phase):
    dt = b["dt"]
    QW = 4096

    def emit_Q(cx, b, pb):
        wbf, wh = pb["wbf"], pb["wh"]
        gsf, scb, rsb = pb["gsf"], pb["scb"], pb["rsb"]
        ybf, t1, t2 = pb["ybf"], pb["t1"], pb["t2"]
        jobs = []
        for si in range(64):
            src = dt["qgu_my"].ap()[si * 128 : (si + 1) * 128, :]
            o0 = b["qgu_hat"][si * 128 : (si + 1) * 128, :]
            o1 = b["qgu_hat"][HG + si * 128 : HG + (si + 1) * 128, :]
            jobs.append((src, D, o0, o1))
        for si in range(8):
            for hw in range(2):
                c0, c1 = hw * QW, (hw + 1) * QW
                src = dt["qpr_my"].ap()[si * 128 : (si + 1) * 128, c0:c1]
                o0 = b["qpr_hat"][si * 128 : (si + 1) * 128, c0:c1]
                o1 = b["qpr_hat"][1024 + si * 128 : 1024 + (si + 1) * 128, c0:c1]
                jobs.append((src, QW, o0, o1))
        for ji, (src, W, o0, o1) in enumerate(jobs):
            sl = ji % 2
            G = W // 64
            if ji >= 2:
                cx.wait("gpsimd", "qv1", val=ji - 1)
            cx.dma("gpsimd", wbf[sl][:, 0:W], src, "qld")
            cx.wait("vector", "qld")
            if ji >= 2:
                cx.wait("vector", f"qwr{ji % 2}", val=16 * (ji // 2))
            wv = wbf[sl].ap()[:, 0:W].rearrange("p (g k) -> p g k", k=64)
            scbc = scb.ap()[:, 0:G].unsqueeze(2).broadcast_to([128, G, 64])
            rsbc = rsb.ap()[:, 0:G].unsqueeze(2).broadcast_to([128, G, 64])
            yv = ybf.ap()[:, 0:W].rearrange("p (g k) -> p g k", k=64)
            cx.op("vector", lambda v, _wv=wv, _G=G: v.reduce_sum(
                gsf[:, 0:_G], _wv, axis=AX.X, apply_absolute_value=True), "qvc")
            cx.wait("vector", "qvc")
            cx.op("vector", lambda v, _G=G: v.tensor_scalar(
                scb[:, 0:_G], gsf[:, 0:_G], 1.0 / 64.0, 1e-8,
                op0=ALU.mult, op1=ALU.max), "qvc")
            cx.wait("vector", "qvc")
            cx.op("vector", lambda v, _G=G: v.reciprocal(rsb[:, 0:_G], scb[:, 0:_G]), "qvc")
            cx.wait("vector", "qvc")
            cx.op("vector", lambda v, _wv=wv, _y=yv, _r=rsbc: v.tensor_tensor(
                _y, _wv, _r, op=ALU.mult), "qv1")
            cx.wait("vector", "qv1")
            cx.op("vector", lambda v, _W=W: v.tensor_single_scalar(
                t1[:, 0:_W], ybf[:, 0:_W], 0.5, op=ALU.is_gt), "qvc")
            cx.op("vector", lambda v, _W=W: v.tensor_single_scalar(
                t2[:, 0:_W], ybf[:, 0:_W], -0.5, op=ALU.is_lt), "qvc")
            cx.wait("vector", "qvc")
            cx.op("vector", lambda v, _W=W: v.tensor_sub(
                t1[:, 0:_W], t1[:, 0:_W], t2[:, 0:_W]), "qvc")
            cx.wait("vector", "qvc")
            cx.op("vector", lambda v, _sl=sl, _W=W, _s=scbc: v.tensor_tensor(
                wh[_sl].ap()[:, 0:_W].rearrange("p (g k) -> p g k", k=64),
                t1.ap()[:, 0:_W].rearrange("p (g k) -> p g k", k=64),
                _s, op=ALU.mult), "qv2")
            cx.wait("vector", "qv2")
            cx.wait("sync", "qv2")
            cx.ifpar_dma("sync", o0, o1, wh[sl][:, 0:W], f"qwr{ji % 2}", b["pid_sb"])
        cx.wait("gpsimd", "qwr0")
        cx.wait("gpsimd", "qwr1")
        cx.write_flag(2 * EV_QNT, 2 * EV_QNT + 1, b["fvals"], b["sh_flags"], b["pid_sb"])
        cx.poll_flag_pair(2 * EV_QNT, b["fvals"], b["spoll"], b["sh_flags"], "qrdy", b["pid_sb"])
        cx.barrier("barQ")

    with (
        nc.sbuf_tensor("qwbf0", [128, QW], BF16) as wbf0,
        nc.sbuf_tensor("qwbf1", [128, QW], BF16) as wbf1,
        nc.sbuf_tensor("qwh0", [128, QW], BF16) as wh0,
        nc.sbuf_tensor("qwh1", [128, QW], BF16) as wh1,
        nc.sbuf_tensor("qgsf", [128, QW // 64], F32) as gsf,
        nc.sbuf_tensor("qscb", [128, QW // 64], BF16) as scb,
        nc.sbuf_tensor("qrsb", [128, QW // 64], F32) as rsb,
        nc.sbuf_tensor("qybf", [128, QW], BF16) as ybf,
        nc.sbuf_tensor("qt1", [128, QW], BF16) as t1,
        nc.sbuf_tensor("qt2", [128, QW], BF16) as t2,
    ):
        run_phase(emit_Q, dict(wbf=[wbf0, wbf1], wh=[wh0, wh1], gsf=gsf,
                               scb=scb, rsb=rsb, ybf=ybf, t1=t1, t2=t2))



def emit_ffn0_phase(nc, cx, block, b, run_phase):
    BR = 512
    NGB = HG // BR
    GPB = BR // 128

    def emit_F(cx, b, pb):
        gsl, usl, psl = pb["gsl"], pb["usl"], pb["psl"]
        hbf, acc, sgf = pb["hbf"], pb["acc"], pb["sgf"]
        fsq = pb["fsq"]
        strow, sbc = pb["strow"], pb["sbc"]
        pg, pu, pss, pw = pb["pg"], pb["pu"], pb["pss"], pb["pw"]
        pbc1 = pb["pbc1"]
        xr, xb = b["xr"], b["xb"]
        cx.wait("sync", "qrdy")
        ngt = [0]
        nacc = [0]
        for th in range(2):
            to = th * 512
            for gb in range(NGB):
                if th > 0 or gb > 0:
                    cx.wait("sync", "fgu_mm", val=ngt[0])
                for dc in range(NC):
                    cx.dma("sync", gsl[:, dc * BR : (dc + 1) * BR],
                           b["qgu_hat"][gb * BR : (gb + 1) * BR, dc * 128 : (dc + 1) * 128],
                           "fld", transpose=True)
                    cx.dma("sync", usl[:, dc * BR : (dc + 1) * BR],
                           b["qgu_hat"][HG + gb * BR : HG + (gb + 1) * BR, dc * 128 : (dc + 1) * 128],
                           "fld", transpose=True)
                cx.wait("tensor", "fld")
                for gt in range(GPB):
                    if ngt[0] >= 1:
                        cx.wait("tensor", "fgu_ev", val=ngt[0])
                    for dc in range(NC):
                        cx.op("tensor", lambda t, _gt=gt, _dc=dc, _to=to: t.matmul(
                            pg[:], gsl[:, _dc * BR + _gt * 128 : _dc * BR + (_gt + 1) * 128],
                            xb[:, _dc * T + _to : _dc * T + _to + 512],
                            start=(_dc == 0), stop=(_dc == NC - 1)))
                        cx.op("tensor", lambda t, _gt=gt, _dc=dc, _to=to: t.matmul(
                            pu[:], usl[:, _dc * BR + _gt * 128 : _dc * BR + (_gt + 1) * 128],
                            xb[:, _dc * T + _to : _dc * T + _to + 512],
                            start=(_dc == 0), stop=(_dc == NC - 1)),
                            sem="fgu_mm" if dc == NC - 1 else None)
                    cx.wait("scalar", "fgu_mm")
                    cx.op("scalar", lambda s: s.activation(sgf[:, 0:512], pg[:], AF.Silu), "fsg")
                    cx.wait("vector", "fsg")
                    cx.op("vector", lambda v, _gt=gt: v.tensor_mul(
                        hbf[:, _gt * 512 : (_gt + 1) * 512], sgf[:, 0:512], pu[:]), "fgu_ev")
                    ngt[0] += 1
                    cx.wait("scalar", "fgu_ev", val=ngt[0])
                    if ngt[0] >= 2:
                        cx.wait("scalar", "fssm", val=ngt[0] - 1)
                    cx.op("scalar", lambda s, _gt=gt: s.activation(
                        fsq[:], hbf[:, _gt * 512 : (_gt + 1) * 512], AF.Square), "fsqd")
                    cx.wait("tensor", "fsqd")
                    cx.op("tensor", lambda t, _gt=gt, _gb=gb, _th=th: t.matmul(
                        pss[0:1, :], b["onesb"][:], fsq[:],
                        start=(_gb == 0 and _gt == 0), stop=(_gb == NGB - 1 and _gt == GPB - 1)),
                        sem="fssm")
                if th > 0 or gb > 0:
                    cx.wait("sync", "fpj_mm", val=(th * NGB + gb) * NC)
                for hc in range(GPB):
                    cx.dma("sync", psl[:, hc * D : (hc + 1) * D],
                           b["qpr_hat"][:, (gb * GPB + hc) * 128 : (gb * GPB + hc + 1) * 128],
                           "fpld", transpose=True)
                cx.wait("tensor", "fpld")
                for dt_ in range(NC):
                    pwc = pw[dt_ % 2]
                    if nacc[0] >= 2:
                        cx.wait("tensor", "facc", val=nacc[0] - 1)
                    for hc in range(GPB):
                        cx.op("tensor", lambda t, _p=pwc, _dt=dt_, _hc=hc: t.matmul(
                            _p[:], psl[:, _hc * D + _dt * 128 : _hc * D + (_dt + 1) * 128],
                            hbf[:, _hc * 512 : (_hc + 1) * 512],
                            start=(_hc == 0), stop=(_hc == GPB - 1)),
                            sem="fpj_mm" if hc == GPB - 1 else None)
                    cx.wait("vector", "fpj_mm")
                    if gb == 0:
                        cx.op("vector", lambda v, _pw=pwc, _dt=dt_: v.tensor_copy(
                            acc[:, _dt * 512 : (_dt + 1) * 512], _pw[:]), "facc")
                    else:
                        cx.op("vector", lambda v, _pw=pwc, _dt=dt_: v.tensor_add(
                            acc[:, _dt * 512 : (_dt + 1) * 512],
                            acc[:, _dt * 512 : (_dt + 1) * 512], _pw[:]), "facc")
                    nacc[0] += 1
            cx.wait("vector", "facc")
            mrow = lambda: strow[0:1, 0:512]
            rrow = lambda: strow[0:1, 512:1024]
            cx.op("vector", lambda v: v.tensor_copy(mrow(), pss[0:1, :]), "frmc")
            cx.wait("vector", "frmc")
            cx.op("vector", lambda v: v.tensor_scalar_mul(mrow(), mrow(), 1.0 / HG), "frm1")
            cx.wait("scalar", "frm1")
            cx.op("scalar", lambda s: s.activation(mrow(), mrow(), AF.Ln, bias=b["epsc"][0:1, 1:2]), "frmd")
            cx.wait("scalar", "frmd")
            cx.op("scalar", lambda s: s.activation(rrow(), mrow(), AF.Exp, scale=-0.5), "frm2")
            cx.wait("tensor", "frm2")
            cx.op("tensor", lambda t: t.matmul(
                pbc1[:], b["onesr"][:], rrow(), start=True, stop=True), "frbc")
            cx.wait("scalar", "frbc")
            cx.op("scalar", lambda s: s.copy(sbc[:, 0:512], pbc1[:]), "frbe")
            cx.wait("vector", "frbe")
            for dt_ in range(NC):
                cx.op("vector", lambda v, _dt=dt_: v.tensor_mul(
                    sgf[:, 0:512], acc[:, _dt * 512 : (_dt + 1) * 512], sbc[:, 0:512]), "frmc")
                cx.wait("vector", "frmc")
                cx.op("vector", lambda v, _dt=dt_, _to=to: v.tensor_add(
                    xr[:, _dt * T + _to : _dt * T + _to + 512],
                    xr[:, _dt * T + _to : _dt * T + _to + 512], sgf[:, 0:512]), "fres")
            cx.wait("tensor", "fres")
        cx.barrier("barF")

    with contextlib.ExitStack() as st_:
        A = st_.enter_context
        gsl = A(nc.sbuf_tensor("fgsl", [128, NC * BR], BF16))
        usl = A(nc.sbuf_tensor("fusl", [128, NC * BR], BF16))
        psl = A(nc.sbuf_tensor("fpsl", [128, GPB * D], BF16))
        hbf = A(nc.sbuf_tensor("fhbf", [128, GPB * 512], BF16))
        acc = A(nc.sbuf_tensor("faccb", [128, NC * 512], F32))
        sgf = A(nc.sbuf_tensor("fsgf", [128, 512], F32))
        fsq = A(nc.sbuf_tensor("ffsq", [128, 512], BF16))
        strow = A(nc.sbuf_tensor("fstrow", [1, 2 * T], F32))
        sbc = A(nc.sbuf_tensor("fsbc", [128, 512], F32))
        pg = A(nc.psum_tensor("fpg", [128, 512], F32))
        pu = A(nc.psum_tensor("fpu", [128, 512], F32))
        pss = A(nc.psum_tensor("fpss", [1, 512], F32))
        pw0 = A(nc.psum_tensor("fpw0", [128, 512], F32))
        pw1 = A(nc.psum_tensor("fpw1", [128, 512], F32))
        pbc1 = A(nc.psum_tensor("fpbc1", [128, 512], F32))
        run_phase(emit_F, dict(gsl=gsl, usl=usl, psl=psl, hbf=hbf, acc=acc,
                               sgf=sgf, fsq=fsq, strow=strow, sbc=sbc, pg=pg, pu=pu,
                               pss=pss, pw=[pw0, pw1], pbc1=pbc1))
    emit_postln_phase(nc, cx, block, b, run_phase, "lnF")



def emit_moe_phase(nc, cx, block, b, run_phase, skip_ln=False):
    dt = b["dt"]

    def emit_R(cx, b, pb):
        gwt, scT, sc, pT = pb["gwt"], pb["scT"], pb["sc"], pb["pT"]
        gbc, sel = pb["gbc"], pb["sel"]
        work, tmp16, mx, msk, wsr = pb["work"], pb["tmp16"], pb["mx"], pb["msk"], pb["wsr"]
        pr, ptp = pb["pr"], pb["ptp"]
        cx.dma("sync", gwt.ap().rearrange("p (c e) -> p c e", e=NE),
               dt["gate_wT"].ap().rearrange("(c p) e -> p c e", p=128), "rld")
        cx.dma("sync", gbc[:], dt["gate_b_col"].ap(), "rld")
        cx.dma("sync", sel[:], dt["sel16"].ap(), "rld")
        cx.wait("tensor", "rld")
        for dc in range(NC):
            for th in range(2):
                cx.op("tensor", lambda t, _dc=dc, _th=th: t.matmul(
                    pr[:, _th * 512 : (_th + 1) * 512],
                    gwt[:, _dc * NE : (_dc + 1) * NE],
                    b["xr"][:, _dc * T + _th * 512 : _dc * T + (_th + 1) * 512],
                    start=(_dc == 0), stop=(_dc == NC - 1)),
                    sem="rmm" if (dc == NC - 1 and th == 1) else None)
        cx.wait("scalar", "rmm")
        cx.op("scalar", lambda s: s.activation(scT[:], pr[0:NE, :], AF.Sigmoid), "rsg")
        cx.wait("vector", "rsg")
        cx.op("vector", lambda v: v.tensor_scalar_add(scT[:], scT[:], gbc[:]), "rsb")
        cx.wait("tensor", "rsb")
        for j in range(8):
            cx.op("tensor", lambda t, _j=j: t.transpose(
                ptp[:, 0:NE], scT[:, _j * 128 : (_j + 1) * 128], b["idf"][0:NE, 0:NE]),
                "rtp")
            cx.wait("vector", "rtp")
            cx.op("vector", lambda v, _j=j: v.tensor_copy(
                sc[:, _j * NE : (_j + 1) * NE], ptp[:, 0:NE]), "rtc")
            cx.wait("tensor", "rtc")
        def vs(fn):
            cx.op("vector", fn, "rvc")
            cx.wait("vector", "rvc")
        for j in range(8):
            scj = sc[:, j * NE : (j + 1) * NE]
            vs(lambda v, _s=scj: v.tensor_copy(work[:], _s))
            for r in range(TOPK):
                vs(lambda v, _r=r: v.reduce_max(
                    mx[:, _r : _r + 1], work[:], axis=AX.X))
                if r < TOPK - 1:
                    vs(lambda v, _r=r: v.tensor_scalar(
                        tmp16[:], work[:], mx[:, _r : _r + 1], -1e30,
                        op0=ALU.is_equal, op1=ALU.mult))
                    vs(lambda v: v.tensor_add(work[:], work[:], tmp16[:]))
            vs(lambda v, _s=scj: v.scalar_tensor_tensor(
                msk[:], _s, mx[:, TOPK - 1 : TOPK], _s, op0=ALU.is_ge, op1=ALU.mult))
            vs(lambda v: v.reduce_sum(wsr[:, 0:1], msk[:], axis=AX.X))
            vs(lambda v: v.reciprocal(wsr[:, 1:2], wsr[:, 0:1]))
            cx.op("vector", lambda v, _j=j: v.tensor_scalar_mul(
                sc[:, _j * NE : (_j + 1) * NE], msk[:], wsr[:, 1:2]), "rpj")
            cx.wait("vector", "rpj")
        cx.wait("tensor", "rpj")
        for j in range(8):
            if j >= 1:
                cx.wait("tensor", "rpe", val=j)
            cx.op("tensor", lambda t, _j=j: t.transpose(
                ptp[0:NE, 0:128],
                sc[:, _j * NE : (_j + 1) * NE], b["idf"][:]), "rtb")
            cx.wait("scalar", "rtb")
            cx.op("scalar", lambda s, _j=j: s.copy(
                pT[:, _j * 128 : (_j + 1) * 128], ptp[0:NE, 0:128]), "rpe")
        cx.barrier("barR")

    with (
        nc.sbuf_tensor("mgwt", [128, NC * NE], F32) as gwt,
        nc.sbuf_tensor("mscT", [NE, T], F32) as scT,
        nc.sbuf_tensor("msc", [128, 8 * NE], F32) as sc,
        nc.sbuf_tensor("mpT", [NE, 8 * 128], F32) as pT_,
        nc.sbuf_tensor("mgbc", [NE, 1], F32) as gbc,
        nc.sbuf_tensor("msel", [NE, NE * 128], F32) as sel,
        nc.sbuf_tensor("mwork", [128, NE], F32) as work,
        nc.sbuf_tensor("mtmp16", [128, NE], F32) as tmp16,
        nc.sbuf_tensor("mmx", [128, TOPK], F32) as mx,
        nc.sbuf_tensor("mmsk", [128, NE], F32) as msk,
        nc.sbuf_tensor("mwsr", [128, 2], F32) as wsr,
    ):
        with contextlib.ExitStack() as rst_:
            pr = rst_.enter_context(nc.psum_tensor("mpr", [NE, T], F32))
            ptp = rst_.enter_context(nc.psum_tensor("mptp", [128, 128], F32))
            run_phase(emit_R, dict(gwt=gwt, scT=scT, sc=sc, pT=pT_, gbc=gbc, sel=sel,
                                   work=work, tmp16=tmp16, mx=mx, msk=msk, wsr=wsr,
                                   pr=pr, ptp=ptp))

        def emit_E(cx, b, pb):
            w1s, w3s, w2s = pb["w1s"], pb["w3s"], pb["w2s"]
            h3s, s1f, peb = pb["h3s"], pb["s1f"], pb["peb"]
            pbe, ph1, ph2, pw = pb["pbe"], pb["ph1"], pb["ph2"], pb["pw"]
            xr, xb = b["xr"], b["xb"]
            nht = [0]
            nw2 = [0]
            for e in range(NE + 1):
                routed = e < NE
                if routed:
                    w1t, w3t = dt["L1_ffn_w1_routed"].ap()[e], dt["L1_ffn_w3_routed"].ap()[e]
                    w2t = dt["L1_ffn_w2_routed"].ap()[e]
                else:
                    w1t, w3t = dt["L1_ffn_w1_shared"].ap(), dt["L1_ffn_w3_shared"].ap()
                    w2t = dt["L1_ffn_w2_shared"].ap()
                if routed:
                    if e >= 1:
                        cx.wait("tensor", "mbev", val=e)
                    for th in range(2):
                        cx.op("tensor", lambda t, _e=e, _th=th: t.matmul(
                            pbe[:, _th * 512 : (_th + 1) * 512],
                            sel[:, _e * 128 : (_e + 1) * 128],
                            pT_[:, _th * 512 : (_th + 1) * 512],
                            start=True, stop=True),
                            sem="mbe" if th == 1 else None)
                    cx.wait("scalar", "mbe")
                    if e >= 1:
                        cx.wait("scalar", "mh3", val=e * HMT)
                    cx.op("scalar", lambda s: s.copy(peb[:], pbe[:]), "mbev")
                if e >= 1:
                    cx.wait("gpsimd", "mw2mm", val=nw2[0])
                for hc in range(HMT):
                    cx.dma("gpsimd", w2s[:, hc * D : (hc + 1) * D],
                           w2t[hc * 128 : (hc + 1) * 128, :], "mw2ld")
                for ht in range(HMT):
                    sl = ht % 2
                    idx = e * HMT + ht
                    if idx >= 2:
                        cx.wait("gpsimd", "mhmm", val=2 * (idx - 1))
                    cx.dma("gpsimd", w1s[sl].ap().rearrange("p (c e) -> p c e", e=128),
                           w1t[:, ht * 128 : (ht + 1) * 128].rearrange("(c p) e -> p c e", p=128),
                           "mwld")
                    cx.dma("gpsimd", w3s[sl].ap().rearrange("p (c e) -> p c e", e=128),
                           w3t[:, ht * 128 : (ht + 1) * 128].rearrange("(c p) e -> p c e", p=128),
                           "mwld")
                    cx.wait("tensor", "mwld")
                    if idx >= 1:
                        cx.wait("tensor", "mh3", val=nht[0])
                    if routed and ht == 0:
                        cx.wait("vector", "mbev", val=e + 1)
                    for dc in range(NC):
                        for th in range(2):
                            cx.op("tensor", lambda t, _s=w1s[sl], _dc=dc, _th=th: t.matmul(
                                ph1[:, _th * 512 : (_th + 1) * 512],
                                _s[:, _dc * 128 : (_dc + 1) * 128],
                                xb[:, _dc * T + _th * 512 : _dc * T + (_th + 1) * 512],
                                start=(_dc == 0), stop=(_dc == NC - 1)))
                            cx.op("tensor", lambda t, _s=w3s[sl], _dc=dc, _th=th: t.matmul(
                                ph2[:, _th * 512 : (_th + 1) * 512],
                                _s[:, _dc * 128 : (_dc + 1) * 128],
                                xb[:, _dc * T + _th * 512 : _dc * T + (_th + 1) * 512],
                                start=(_dc == 0), stop=(_dc == NC - 1)),
                                sem="mhmm" if (dc == NC - 1 and th == 1) else None, k=2)
                    cx.wait("scalar", "mhmm")
                    if routed:
                        cx.op("scalar", lambda s: s.copy(s1f[:], ph1[:]), "mcv1")
                        cx.wait("vector", "mcv1")
                        cx.op("vector", lambda v: v.tensor_mul(s1f[:], s1f[:], ph2[:]), "mcv2")
                        cx.wait("scalar", "mcv2")
                        cx.op("scalar", lambda s: s.activation(s1f[:], s1f[:], AF.Silu), "mcv3")
                        cx.wait("vector", "mcv3")
                        cx.op("vector", lambda v, _ht=ht: v.tensor_mul(
                            h3s[:, _ht * T : (_ht + 1) * T], s1f[:], peb[:]), "mh3")
                    else:
                        cx.op("scalar", lambda s: s.activation(s1f[:], ph1[:], AF.Silu), "mcv1")
                        cx.mark("vector", "mcv2")
                        cx.mark("scalar", "mcv3")
                        cx.wait("vector", "mcv3")
                        cx.op("vector", lambda v, _ht=ht: v.tensor_mul(
                            h3s[:, _ht * T : (_ht + 1) * T], s1f[:], ph2[:]), "mh3")
                    nht[0] += 1
                cx.wait("tensor", "mw2ld")
                cx.wait("tensor", "mh3", val=nht[0])
                for dt_ in range(NC):
                    for th in range(2):
                        pwc = pw[(dt_ * 2 + th) % 2]
                        if nw2[0] >= 2:
                            cx.wait("tensor", "mxw", val=nw2[0] - 1)
                        for hc in range(HMT):
                            cx.op("tensor", lambda t, _p=pwc, _dt=dt_, _th=th, _hc=hc: t.matmul(
                                _p[:], w2s[:, _hc * D + _dt * 128 : _hc * D + (_dt + 1) * 128],
                                h3s[:, _hc * T + _th * 512 : _hc * T + (_th + 1) * 512],
                                start=(_hc == 0), stop=(_hc == HMT - 1)),
                                sem="mw2mm" if hc == HMT - 1 else None)
                        cx.wait("vector", "mw2mm")
                        cx.op("vector", lambda v, _p=pwc, _dt=dt_, _th=th: v.tensor_add(
                            xr[:, _dt * T + _th * 512 : _dt * T + (_th + 1) * 512],
                            xr[:, _dt * T + _th * 512 : _dt * T + (_th + 1) * 512],
                            _p[:]), "mxw")
                        nw2[0] += 1
            cx.barrier("barE")

        with contextlib.ExitStack() as st_:
            A = st_.enter_context
            w1s0 = A(nc.sbuf_tensor("mw1s0", [128, NC * 128], BF16))
            w1s1 = A(nc.sbuf_tensor("mw1s1", [128, NC * 128], BF16))
            w3s0 = A(nc.sbuf_tensor("mw3s0", [128, NC * 128], BF16))
            w3s1 = A(nc.sbuf_tensor("mw3s1", [128, NC * 128], BF16))
            w2s = A(nc.sbuf_tensor("mw2s", [128, HMT * D], BF16))
            h3s = A(nc.sbuf_tensor("mh3s", [128, HMT * T], BF16))
            s1f = A(nc.sbuf_tensor("ms1f", [128, T], F32))
            peb = A(nc.sbuf_tensor("mpeb", [128, T], F32))
            pbe = A(nc.psum_tensor("mpbe", [128, T], F32))
            ph1 = A(nc.psum_tensor("mph1", [128, T], F32))
            ph2 = A(nc.psum_tensor("mph2", [128, T], F32))
            pw0 = A(nc.psum_tensor("mpw0", [128, 512], F32))
            pw1 = A(nc.psum_tensor("mpw1", [128, 512], F32))
            run_phase(emit_E, dict(w1s=[w1s0, w1s1], w3s=[w3s0, w3s1], w2s=w2s,
                                   h3s=h3s, s1f=s1f, peb=peb,
                                   pbe=pbe, ph1=ph1, ph2=ph2, pw=[pw0, pw1]))
    if not skip_ln:
        emit_postln_phase(nc, cx, block, b, run_phase, "lnM")



def own_rows(p):
    return np.concatenate([np.arange(g * 128, (g + 1) * 128) for g in own_blocks(p)])


def _consts(p, nonce):
    d = {}
    d["fvals"] = (nonce + np.arange(NFLAG, dtype=np.int64) % (2**31 - 1)).astype(np.int32).reshape(1, NFLAG)
    d["ident_f32"] = np.eye(128, dtype=np.float32)
    d["ident_bf"] = np.eye(128, dtype=ml_dtypes.bfloat16)
    d["ones_bf"] = np.ones((128, 1), dtype=ml_dtypes.bfloat16)
    d["onesrow_f32"] = np.ones((1, 128), dtype=np.float32)
    d["dmask"] = make_dmask(p)
    return d


def make_in_maps(inputs, phases, extra=None):
    need = set(phases)
    f = lambda k: np.ascontiguousarray(np.asarray(inputs[k], dtype=np.float32))
    nonce = int(np.random.randint(1, 2**30))
    maps = []
    for c in range(8):
        bb, p = c >> 1, c & 1
        m = dict(_consts(p, nonce))
        if "load_x" in need:
            m["x_own"] = np.ascontiguousarray(f("L0_attn_x")[bb][own_rows(p)])
        if "loadT" in need:
            m["xT_in"] = extra[c]
        if "l0attn" in need:
            for k in ("L0_attn_W_Q", "L0_attn_W_K", "L0_attn_W_V", "L0_attn_W_O"):
                m[k] = f(k)
            m["bqT"] = np.ascontiguousarray(f("L0_attn_b_Q").T)
            m["bkT"] = np.ascontiguousarray(f("L0_attn_b_K").T)
            m["bvT"] = np.ascontiguousarray(f("L0_attn_b_V").T)
            m["boT"] = np.ascontiguousarray(f("L0_attn_b_O").reshape(NC, 128).T)
        if "l1attn" in need:
            m["wqkvT"] = np.ascontiguousarray(f("L1_attn_w_qkv").T)
            m["woutT"] = np.ascontiguousarray(f("L1_attn_w_out").T)
            m["bqkvT"] = np.ascontiguousarray(
                f("L1_attn_b_qkv").reshape(3, H, 128).transpose(2, 0, 1).reshape(128, 3 * H))
            m["boutT"] = np.ascontiguousarray(f("L1_attn_b_out").reshape(NC, 128).T)
        if "quant" in need:
            wgu = f("L0_ffn_w_gate_up")
            m["qgu_my"] = np.ascontiguousarray(wgu[0:HG] if p == 0 else wgu[HG:])
            wpr = f("L0_ffn_w_proj")
            m["qpr_my"] = np.ascontiguousarray(wpr[p * 1024 : (p + 1) * 1024])
        if "moe" in need or "moe_noln" in need:
            m["gate_wT"] = np.ascontiguousarray(f("L1_ffn_gate_w").T)
            m["gate_b_col"] = np.ascontiguousarray(f("L1_ffn_gate_b").reshape(NE, 1))
            m["sel16"] = np.ascontiguousarray(np.repeat(np.eye(NE, dtype=np.float32), 128, axis=1))
            for k in ("L1_ffn_w1_shared", "L1_ffn_w2_shared", "L1_ffn_w3_shared",
                      "L1_ffn_w1_routed", "L1_ffn_w2_routed", "L1_ffn_w3_routed"):
                m[k] = f(k)
        maps.append(m)
    return maps


FUSED_PHASES = ("load_x", "l0attn", "quant", "ffn0", "l1attn", "moe", "store")


def kernel(**inputs):
    nc = build_program(list(FUSED_PHASES))
    in_maps = make_in_maps(inputs, FUSED_PHASES)
    res = run_bass_kernel_spmd(nc, in_maps, core_ids=list(range(8)))
    y = np.zeros((B, S, D), np.float32)
    for c in range(8):
        bb, p = c >> 1, c & 1
        y[bb][own_rows(p)] = res.results[c]["y_own"]
    return y


# version 04
# speedup vs baseline: 1.0874x; 1.0874x over previous
import contextlib
import re
import numpy as np
import ml_dtypes
import concourse.bass as bass
import concourse.mybir as mybir
from concourse.bass_utils import run_bass_kernel_spmd

F32 = mybir.dt.float32
BF16 = mybir.dt.bfloat16
I32 = mybir.dt.int32
AF = mybir.ActivationFunctionType
ALU = mybir.AluOpType
AX = mybir.AxisListType

B, S, D = 4, 2048, 2048
H, DH = 16, 128
NB = 16
T = 1024
NC = 16
HG = 8192
NE = 16
HM = 1408
HMT = 11
TOPK = 4
INVSQ = float(1.0 / np.sqrt(128.0))
LN_EPS = 1e-5
RMS_EPS = float(np.finfo(np.float32).eps)

ENGINES = ("tensor", "vector", "scalar", "sync", "gpsimd")

EV_KV0, EV_QNT, EV_KV1 = 0, 1, 2
NFLAG = 16


def _ceil_div(a, b):
    return (a + b - 1) // b


def js_of(kc):
    return max(0, _ceil_div(kc - 1, 2))


class Ctx:

    @staticmethod
    def canon(name):
        if name.startswith("bar"):
            return "bar"
        name = name.replace("_at1", "_at0")
        name = re.sub(r"^(ln_at0|lnF|lnM)", "LNx", name)
        return name

    def __init__(self, nc):
        self.nc = nc
        self.sems = {}
        self.counts = {}
        self.pbase = {}
        self.cur = None
        self.eng = None
        self._rn = 0

    def rn(self, p):
        self._rn += 1
        return f"{p}{self._rn}"

    def h(self, name):
        name = self.canon(name)
        if name not in self.sems:
            self.sems[name] = self.nc.alloc_semaphore(name=name)
        self.counts.setdefault(name, 0)
        return self.sems[name]

    def snapshot(self):
        return dict(self.counts)

    def start_pass(self, name, eng, base):
        self.cur = name
        self.eng = eng
        self.pbase = dict(base)
        self.counts = dict(base)

    def on(self, name):
        return self.cur == name

    def bump(self, name, k):
        name = self.canon(name)
        self.h(name)
        self.counts[name] += k
        return self.counts[name]

    def val(self, name):
        name = self.canon(name)
        self.h(name)
        return self.counts[name]

    def base(self, name):
        return self.pbase.get(self.canon(name), 0)

    def dma(self, q, out, in_, sem, **kw):
        self.h(sem)
        if self.on(q):
            self.eng.dma_start(out=out, in_=in_, **kw).then_inc(self.sems[self.canon(sem)], 16)
        self.bump(sem, 16)

    def wait(self, engname, sem, val=None):
        self.h(sem)
        if self.on(engname):
            tgt = self.val(sem) if val is None else self.base(sem) + val
            self.eng.wait_ge(self.sems[self.canon(sem)], tgt)

    def op(self, engname, fn, sem=None, k=1):
        if sem is not None:
            self.h(sem)
        if self.on(engname):
            ins = fn(self.eng)
            if sem is not None:
                ins.then_inc(self.sems[self.canon(sem)], k)
        if sem is not None:
            self.bump(sem, k)

    def mark(self, engname, sem, k=1):
        self.h(sem)
        if self.on(engname):
            self.eng.sem_inc(self.sems[self.canon(sem)], k)
        self.bump(sem, k)

    def barrier(self, name):
        s = self.h(name)
        for _ in ENGINES:
            self.bump(name, 1)
        if self.cur is not None:
            self.eng.sem_inc(s, 1)
            self.eng.wait_ge(s, self.val(name))

    def poll_flag(self, slot, fvals_sb, spoll_sb, sh_flags, donesem):
        self.h(donesem)
        if self.on("gpsimd"):
            g = self.eng
            with g.register() as loopv, g.register() as tmp, g.register() as want:
                g.reg_load(want, fvals_sb[0:1, slot : slot + 1])
                g.reg_mov(loopv, 1)
                with g.While(loopv):
                    g.dma_start(
                        out=spoll_sb[0:1, 0:1], in_=sh_flags[0:1, slot : slot + 1]
                    ).then_inc(self.sems["pollsem"], 16, skip_validation=True)
                    g.reg_load(tmp, spoll_sb[0:1, 0:1])
                    g.reg_alu(tmp, tmp, want, op=ALU.is_equal)
                    g.reg_alu(loopv, tmp, 1, op=ALU.not_equal)
            g.sem_inc(self.sems[self.canon(donesem)], 1)
        self.bump(donesem, 1)

    def write_flag(self, slot_p0, slot_p1, fvals_sb, sh_flags, pid_sb=None):
        if self.on("gpsimd"):
            g = self.eng
            with g.register(self.rn("wfpr")) as pr:
                g.reg_load(pr, pid_sb[0:1, 0:1])
                g.reg_alu(pr, pr, 1, op=ALU.bitwise_and)
                with g.If_eq(pr, 0):
                    g.dma_start(
                        out=sh_flags[0:1, slot_p0 : slot_p0 + 1],
                        in_=fvals_sb[0:1, slot_p0 : slot_p0 + 1],
                    ).then_inc(self.sems["pollsem"], 16, skip_validation=True)
                with g.If_eq(pr, 1):
                    g.dma_start(
                        out=sh_flags[0:1, slot_p1 : slot_p1 + 1],
                        in_=fvals_sb[0:1, slot_p1 : slot_p1 + 1],
                    ).then_inc(self.sems["pollsem"], 16, skip_validation=True)

    def poll_flag_pair(self, ev2, fvals_sb, spoll_sb, sh_flags, donesem, pid_sb=None):
        self.h(donesem)
        if self.on("gpsimd"):
            g = self.eng
            with g.register(self.rn("pfp")) as pr:
                g.reg_load(pr, pid_sb[0:1, 0:1])
                g.reg_alu(pr, pr, 1, op=ALU.bitwise_and)
                for mypar, slot in ((0, ev2 + 1), (1, ev2)):
                    with g.If_eq(pr, mypar):
                        with (
                            g.register(self.rn("pfl")) as loopv,
                            g.register(self.rn("pft")) as tmp,
                            g.register(self.rn("pfw")) as want,
                        ):
                            g.reg_load(want, fvals_sb[0:1, slot : slot + 1])
                            g.reg_mov(loopv, 1)
                            with g.While(loopv):
                                if self.poll_nop:
                                    g.nop(cycle_cnt=4800)
                                g.dma_start(
                                    out=spoll_sb[0:1, 0:1],
                                    in_=sh_flags[0:1, slot : slot + 1],
                                ).then_inc(self.sems["pollsem"], 16, skip_validation=True)
                                g.reg_load(tmp, spoll_sb[0:1, 0:1])
                                g.reg_alu(tmp, tmp, want, op=ALU.is_equal)
                                g.reg_alu(loopv, tmp, 1, op=ALU.not_equal)
            g.sem_inc(self.sems[self.canon(donesem)], 1)
        self.bump(donesem, 1)

    def ifpar_dma(self, q, out_p0, out_p1, in_, sem, pid_sb=None):
        self.h(sem)
        if self.on(q):
            e = self.eng
            with e.register(self.rn("ipr")) as pr:
                e.reg_load(pr, pid_sb[0:1, 0:1])
                e.reg_alu(pr, pr, 1, op=ALU.bitwise_and)
                with e.If_eq(pr, 0):
                    e.dma_start(out=out_p0, in_=in_).then_inc(self.sems[self.canon(sem)], 16)
                with e.If_eq(pr, 1):
                    e.dma_start(out=out_p1, in_=in_).then_inc(self.sems[self.canon(sem)], 16)
        self.bump(sem, 16)



def own_blocks(p):
    return [g for g in range(NB) if g % 2 == p]


def make_dmask(p):
    m = np.zeros((NB, 128, 256), np.float32)
    blocks = own_blocks(p)
    for kc in range(NB):
        js = js_of(kc)
        for jj in range(2):
            j = js + jj
            col = jj * 128
            if j >= 8:
                continue
            g = blocks[j]
            if g < kc:
                pass
            elif g == kc:
                rk = np.arange(128)[:, None]
                rq = np.arange(128)[None, :]
                m[kc, :, col : col + 128] = (rk <= rq).astype(np.float32)
            else:
                m[kc, :, col : col + 128] = 1.0
    return m.astype(ml_dtypes.bfloat16)



def build_program(phases, detect_races=True, poll_nop=True):
    nc = bass.Bass("TRN2", target_bir_lowering=False, debug=False, num_devices=8,
                   detect_race_conditions=detect_races)
    dt = {}

    def din(name, shape, dtype=F32):
        dt[name] = nc.dram_tensor(name, list(shape), dtype, kind="ExternalInput")
        return dt[name]

    def dout(name, shape, dtype=F32):
        dt[name] = nc.dram_tensor(name, list(shape), dtype, kind="ExternalOutput")
        return dt[name]

    need = set(phases)

    din("fvals", (1, NFLAG), I32)
    din("ident_f32", (128, 128), F32)
    din("ident_bf", (128, 128), BF16)
    din("ones_bf", (128, 1), BF16)
    din("onesrow_f32", (1, 128), F32)

    if "load_x" in need:
        din("x_own", (T, D))
    if "loadT" in need:
        din("xT_in", (D, T))
    if "storeT" in need:
        dout("xT_out", (D, T))
    if "qdump" in need:
        dt["qgu_dump"] = nc.dram_tensor("qgu_dump", [2 * HG, D], BF16, kind="ExternalOutput")
        dt["qpr_dump"] = nc.dram_tensor("qpr_dump", [D, HG], BF16, kind="ExternalOutput")
    if "store" in need:
        dout("y_own", (T, D))

    if "l0attn" in need or "l1attn" in need:
        din("dmask", (NB, 128, 256), BF16)
    if "l0attn" in need:
        din("L0_attn_W_Q", (H, D, DH))
        din("L0_attn_W_K", (H, D, DH))
        din("L0_attn_W_V", (H, D, DH))
        din("L0_attn_W_O", (H, DH, D))
        din("bqT", (128, H))
        din("bkT", (128, H))
        din("bvT", (128, H))
        din("boT", (128, NC))
    if "l1attn" in need:
        din("wqkvT", (D, 3 * D))
        din("woutT", (D, D))
        din("bqkvT", (128, 3 * H))
        din("boutT", (128, NC))
    if "quant" in need:
        din("qgu_my", (HG, D))
        din("qpr_my", (D // 2, HG))
    if "moe" in need or "moe_noln" in need:
        din("gate_wT", (D, NE))
        din("gate_b_col", (NE, 1))
        din("sel16", (NE, NE * 128))
        din("L1_ffn_w1_shared", (D, HM))
        din("L1_ffn_w2_shared", (HM, D))
        din("L1_ffn_w3_shared", (D, HM))
        din("L1_ffn_w1_routed", (NE, D, HM))
        din("L1_ffn_w2_routed", (NE, HM, D))
        din("L1_ffn_w3_routed", (NE, D, HM))

    sh_flags = nc.dram_tensor("sh_flags", [1, NFLAG + 8], I32, addr_space="Shared")
    kv = {}
    for layer in (0, 1):
        if ("l0attn" if layer == 0 else "l1attn") in need:
            kv[layer] = (
                nc.dram_tensor(f"kvK{layer}", [2, H, 128, T], BF16, addr_space="Shared"),
                nc.dram_tensor(f"kvV{layer}", [2, H, 8, 128, 128], BF16, addr_space="Shared"),
            )
    if "quant" in need or "ffn0" in need:
        qgu_hat = nc.dram_tensor("qgu_hat", [2 * HG, D], BF16, addr_space="Shared")
        qpr_hat = nc.dram_tensor("qpr_hat", [D, HG], BF16, addr_space="Shared")
    else:
        qgu_hat = qpr_hat = None

    cx = Ctx(nc)
    cx.poll_nop = poll_nop
    cx.h("pollsem")

    with (
        nc.sbuf_tensor("xr", [128, NC * T], F32) as xr,
        nc.sbuf_tensor("xb", [128, NC * T], BF16) as xb,
        nc.sbuf_tensor("fvals_sb", [1, NFLAG], I32) as fvals_sb,
        nc.sbuf_tensor("spoll", [1, 1], I32) as spoll,
        nc.sbuf_tensor("pid_sb", [1, 1], mybir.dt.uint32) as pid_sb,
        nc.sbuf_tensor("epsc", [1, 2], F32) as epsc,
        nc.sbuf_tensor("ident_f32_sb", [128, 128], F32) as idf,
        nc.sbuf_tensor("ident_bf_sb", [128, 128], BF16) as idb,
        nc.sbuf_tensor("ones_bf_sb", [128, 1], BF16) as onesb,
        nc.sbuf_tensor("onesrow_sb", [1, 128], F32) as onesr,
        nc.Block() as block,
    ):
        bufs = dict(
            xr=xr, xb=xb, fvals=fvals_sb, spoll=spoll, idf=idf, idb=idb,
            onesb=onesb, onesr=onesr, dt=dt, kv=kv, sh_flags=sh_flags,
            qgu_hat=qgu_hat, qpr_hat=qpr_hat, pid_sb=pid_sb, epsc=epsc,
        )

        def run_phase(emitfn, pbufs):
            base = cx.snapshot()
            for engname in ENGINES:
                eng = getattr(nc, engname)

                def seg(e, _emitfn=emitfn, _n=engname, _pb=pbufs):
                    cx.start_pass(_n, e, base)
                    _emitfn(cx, bufs, _pb)

                getattr(block, engname)(seg)

        def emit_prologue(cx, b, pb):
            cx.op("gpsimd", lambda g: g.memset(b["spoll"][:], 0), "clds0")
            cx.op("gpsimd", lambda g: g.memset(b["epsc"][0:1, 0:1], LN_EPS), "clds0")
            cx.op("gpsimd", lambda g: g.memset(b["epsc"][0:1, 1:2], RMS_EPS), "clds0")
            cx.dma("sync", b["pid_sb"][:], nc.partition_id_tensor.ap(), "clds")
            cx.dma("sync", b["fvals"][:], dt["fvals"].ap(), "clds")
            cx.dma("sync", b["idf"][:], dt["ident_f32"].ap(), "clds")
            cx.dma("sync", b["idb"][:], dt["ident_bf"].ap(), "clds")
            cx.dma("sync", b["onesb"][:], dt["ones_bf"].ap(), "clds")
            cx.dma("sync", b["onesr"][:], dt["onesrow_f32"].ap(), "clds")
            for e in ENGINES:
                cx.wait(e, "clds")
            cx.barrier("bar_prologue")

        run_phase(emit_prologue, {})

        for ph in phases:
            if ph == "load_x":
                with (
                    nc.sbuf_tensor("sx0", [128, D], F32) as sx0,
                    nc.sbuf_tensor("sx1", [128, D], F32) as sx1,
                    nc.psum_tensor("ptr0", [128, 128], F32) as ptr0,
                    nc.psum_tensor("ptr1", [128, 128], F32) as ptr1,
                    nc.psum_tensor("ptr2", [128, 128], F32) as ptr2,
                    nc.psum_tensor("ptr3", [128, 128], F32) as ptr3,
                ):
                    pb = dict(sx=[sx0, sx1], ptr=[ptr0, ptr1, ptr2, ptr3])
                    run_phase(emit_load_x, pb)
            elif ph == "loadT":
                run_phase(emit_loadT, {})
            elif ph == "storeT":
                run_phase(emit_storeT, {})
            elif ph == "store":
                with (
                    nc.sbuf_tensor("sy0", [128, D], F32) as sy0,
                    nc.sbuf_tensor("sy1", [128, D], F32) as sy1,
                    nc.psum_tensor("pso0", [128, 128], F32) as pso0,
                    nc.psum_tensor("pso1", [128, 128], F32) as pso1,
                    nc.psum_tensor("pso2", [128, 128], F32) as pso2,
                    nc.psum_tensor("pso3", [128, 128], F32) as pso3,
                ):
                    pb = dict(sy=[sy0, sy1], pso=[pso0, pso1, pso2, pso3])
                    run_phase(emit_store, pb)
            elif ph in ("l0attn", "l1attn"):
                layer = 0 if ph == "l0attn" else 1
                emit_attention_phase(nc, cx, block, bufs, run_phase, layer)
            elif ph == "quant":
                emit_quant_phase(nc, cx, block, bufs, run_phase)
            elif ph == "ffn0":
                emit_ffn0_phase(nc, cx, block, bufs, run_phase)
            elif ph == "moe":
                emit_moe_phase(nc, cx, block, bufs, run_phase)
            elif ph == "moe_noln":
                emit_moe_phase(nc, cx, block, bufs, run_phase, skip_ln=True)
            elif ph == "qdump":
                def emit_qd(cx, b, pb):
                    cx.wait("sync", "qrdy")
                    cx.dma("sync", dt["qgu_dump"].ap(), b["qgu_hat"].ap(), "final_store")
                    cx.dma("sync", dt["qpr_dump"].ap(), b["qpr_hat"].ap(), "final_store")
                    cx.wait("sync", "final_store")
                run_phase(emit_qd, {})
            else:
                raise ValueError(ph)

        def emit_epilogue(cx, b, pb):
            cx.barrier("bar_epilogue")

        run_phase(emit_epilogue, {})

    return nc



def emit_load_x(cx, b, pb):
    x_own = b["dt"]["x_own"]
    for j in range(8):
        sx = pb["sx"][j % 2]
        if j >= 2:
            cx.wait("sync", "p0done", val=16 * (j - 1))
        cx.dma("sync", sx[:], x_own[j * 128 : (j + 1) * 128, :], "p0ld")
        cx.wait("tensor", "p0ld")
        for dc in range(NC):
            pt = pb["ptr"][dc % 4]
            n = j * NC + dc
            if n >= 4:
                cx.wait("tensor", "p0ev", val=2 * (n - 3))
            cx.op("tensor", lambda t, _sx=sx, _pt=pt, _dc=dc: t.transpose(
                _pt[:], _sx[:, _dc * 128 : (_dc + 1) * 128], b["idf"][:]), "p0tr")
            cx.wait("vector", "p0tr")
            cx.op("vector", lambda v, _pt=pt, _dc=dc, _j=j: v.tensor_copy(
                b["xr"][:, _dc * T + _j * 128 : _dc * T + (_j + 1) * 128], _pt[:]), "p0ev")
            cx.op("vector", lambda v, _pt=pt, _dc=dc, _j=j: v.tensor_copy(
                b["xb"][:, _dc * T + _j * 128 : _dc * T + (_j + 1) * 128], _pt[:]), "p0ev")
        cx.wait("sync", "p0ev")
        cx.mark("sync", "p0done", 16)
    cx.barrier("bar_load_x")


def emit_loadT(cx, b, pb):
    xT = b["dt"]["xT_in"]
    cx.dma("sync", b["xr"].ap().rearrange("p (c t) -> p c t", t=T),
           xT.ap().rearrange("(c p) t -> p c t", p=128), "pTld")
    cx.wait("vector", "pTld")
    for dc in range(NC):
        cx.op("vector", lambda v, _dc=dc: v.tensor_copy(
            b["xb"][:, _dc * T : (_dc + 1) * T], b["xr"][:, _dc * T : (_dc + 1) * T]), "pTev")
    cx.barrier("bar_loadT")


def emit_storeT(cx, b, pb):
    cx.barrier("bar_storeT0")
    xT = b["dt"]["xT_out"]
    cx.dma("sync", xT.ap().rearrange("(c p) t -> p c t", p=128),
           b["xr"].ap().rearrange("p (c t) -> p c t", t=T), "final_store")
    cx.wait("sync", "final_store")


def emit_store(cx, b, pb):
    cx.barrier("bar_store0")
    y = b["dt"]["y_own"]
    for j in range(8):
        sy = pb["sy"][j % 2]
        if j >= 2:
            cx.wait("vector", "final_store", val=16 * j)
        for dc in range(NC):
            ps = pb["pso"][dc % 4]
            n = j * NC + dc
            if n >= 4:
                cx.wait("tensor", "p6ev", val=n - 3)
            cx.op("tensor", lambda t, _ps=ps, _dc=dc, _j=j: t.transpose(
                _ps[:], b["xr"][:, _dc * T + _j * 128 : _dc * T + (_j + 1) * 128],
                b["idf"][:]), "p6tr")
            cx.wait("vector", "p6tr")
            cx.op("vector", lambda v, _ps=ps, _sy=sy, _dc=dc: v.tensor_copy(
                _sy[:, _dc * 128 : (_dc + 1) * 128], _ps[:]), "p6ev")
        cx.wait("sync", "p6ev")
        cx.dma("sync", y[j * 128 : (j + 1) * 128, :], sy[:], "final_store")
    cx.wait("sync", "final_store")



def emit_attention_phase(nc, cx, block, b, run_phase, layer):
    dt = b["dt"]
    if layer == 0:
        def wap(part, h):
            t = dt[{"q": "L0_attn_W_Q", "k": "L0_attn_W_K", "v": "L0_attn_W_V"}[part]]
            return t.ap()[h].rearrange("(c p) e -> p c e", p=128)

        def wo_ap(h, dtg):
            return dt["L0_attn_W_O"].ap()[h][:, dtg * 512 : (dtg + 1) * 512]

        bias_col = {"q": ("bqT",), "k": ("bkT",), "v": ("bvT",)}
        bo_name = "boT"
        ev = EV_KV0
    else:
        def wap(part, h):
            sec = {"q": 0, "k": 1, "v": 2}[part]
            col = sec * D + h * 128
            return dt["wqkvT"].ap()[:, col : col + 128].rearrange("(c p) e -> p c e", p=128)

        def wo_ap(h, dtg):
            return dt["woutT"].ap()[h * 128 : (h + 1) * 128, dtg * 512 : (dtg + 1) * 512]

        bias_col = {"q": ("bqkvT", 0), "k": ("bqkvT", 1), "v": ("bqkvT", 2)}
        bo_name = "boutT"
        ev = EV_KV0 if layer == 0 else EV_KV1
    kvK, kvV = b["kv"][layer]
    sfx = f"_at{layer}"

    def bias_ap(part, h):
        if layer == 0:
            return dt_sb[{"q": "bq", "k": "bk", "v": "bv"}[part]][:, h : h + 1]
        sec = {"q": 0, "k": 1, "v": 2}[part]
        return dt_sb["bqkv"][:, sec * H + h : sec * H + h + 1]

    with (
        nc.sbuf_tensor(f"QT{sfx}", [128, H * T], BF16) as QT,
        nc.sbuf_tensor(f"biasq{sfx}", [128, 3 * H], F32) as biasq,
        nc.sbuf_tensor(f"biaso{sfx}", [128, NC], F32) as biaso,
        nc.sbuf_tensor(f"dmask{sfx}", [128, NB * 256], BF16) as dmask_sb,
    ):
        dt_sb = dict(bq=biasq[:, 0:H], bk=biasq[:, H : 2 * H], bv=biasq[:, 2 * H : 3 * H],
                     bqkv=biasq)

        def emit_A(cx, b, pb):
            wsl = pb["wsl"]
            ppj = pb["ppj"]
            pvt = pb["pvt"]
            KT, VT, Vtm = pb["KT"], pb["VT"], pb["Vtm"]
            if layer == 0:
                cx.dma("sync", biasq[:, 0:H], dt["bqT"].ap(), "bld" + sfx)
                cx.dma("sync", biasq[:, H : 2 * H], dt["bkT"].ap(), "bld" + sfx)
                cx.dma("sync", biasq[:, 2 * H : 3 * H], dt["bvT"].ap(), "bld" + sfx)
            else:
                cx.dma("sync", biasq[:], dt["bqkvT"].ap(), "bld" + sfx)
            cx.dma("sync", biaso[:], dt[bo_name].ap(), "bld" + sfx)
            cx.dma("sync", dmask_sb.ap().rearrange("p (k q) -> p k q", q=256),
                   dt["dmask"].ap().rearrange("k p q -> p k q"), "bld" + sfx)
            cx.wait("scalar", "bld" + sfx)

            npj = [0]
            for h in range(H):
                slot = wsl[h % 2]
                if h >= 2:
                    cx.wait("gpsimd", "pjmm" + sfx, val=3 * (h - 1))
                for part in ("q", "k", "v"):
                    cx.dma("gpsimd", slot[part].ap().rearrange("p (c e) -> p c e", e=128),
                           wap(part, h), f"wld{h % 2}" + sfx)
                for pi, part in enumerate(("q", "k", "v")):
                    pp = ppj[pi % 2]
                    idx = h * 3 + pi
                    if idx >= 2:
                        cx.wait("tensor", "pjev" + sfx, val=idx - 1)
                    cx.wait("tensor", f"wld{h % 2}" + sfx, val=48 * (h // 2 + 1))
                    for dc in range(NC):
                        for th in range(2):
                            last = dc == NC - 1 and th == 1
                            cx.op("tensor", lambda t, _pp=pp, _s=slot[part], _dc=dc, _th=th: t.matmul(
                                _pp[:, _th * 512 : (_th + 1) * 512],
                                _s[:, _dc * 128 : (_dc + 1) * 128],
                                b["xb"][:, _dc * T + _th * 512 : _dc * T + (_th + 1) * 512],
                                start=(_dc == 0), stop=(_dc == NC - 1)),
                                sem="pjmm" + sfx if last else None)
                    dest = {"q": QT[:, h * T : (h + 1) * T], "k": KT[:], "v": VT[:]}[part]
                    cx.wait("scalar", "pjmm" + sfx)
                    if part in ("k", "v") and h >= 1:
                        cx.wait("scalar", "kxwr" + sfx, val=32 * h)
                    cx.op("scalar", lambda s, _d=dest, _pp=pp, _pt=part, _h=h: s.activation(
                        _d, _pp[:], AF.Identity, bias=bias_ap(_pt, _h)), "pjev" + sfx)
                for j in range(8):
                    pv = pvt[j % 2]
                    if j >= 2:
                        cx.wait("tensor", "vtev" + sfx, val=8 * h + j - 1)
                    cx.wait("tensor", "pjev" + sfx)
                    cx.op("tensor", lambda t, _pv=pv, _j=j: t.transpose(
                        _pv[:], VT[:, _j * 128 : (_j + 1) * 128], b["idb"][:]), "vttr" + sfx)
                    cx.wait("scalar", "vttr" + sfx)
                    cx.op("scalar", lambda s, _pv=pv, _j=j: s.copy(
                        Vtm[:, _j * 128 : (_j + 1) * 128], _pv[:]), "vtev" + sfx)
                cx.wait("sync", "pjev" + sfx)
                cx.wait("sync", "vtev" + sfx)
                cx.ifpar_dma("sync", kvK[0, h], kvK[1, h], KT[:], "kxwr" + sfx, b["pid_sb"])
                cx.ifpar_dma("sync",
                             kvV[0, h].rearrange("li p r -> p li r"),
                             kvV[1, h].rearrange("li p r -> p li r"),
                             Vtm.ap().rearrange("p (j e) -> p j e", e=128), "kxwr" + sfx,
                             b["pid_sb"])
            cx.wait("gpsimd", "kxwr" + sfx)
            cx.write_flag(2 * ev, 2 * ev + 1, b["fvals"], b["sh_flags"], b["pid_sb"])
            cx.poll_flag_pair(2 * ev, b["fvals"], b["spoll"], b["sh_flags"], "kvrdy" + sfx, b["pid_sb"])
            cx.barrier("barA" + sfx)

        with (
            nc.sbuf_tensor(f"wq{sfx}0", [128, NC * 128], BF16) as w00,
            nc.sbuf_tensor(f"wk{sfx}0", [128, NC * 128], BF16) as w01,
            nc.sbuf_tensor(f"wv{sfx}0", [128, NC * 128], BF16) as w02,
            nc.sbuf_tensor(f"wq{sfx}1", [128, NC * 128], BF16) as w10,
            nc.sbuf_tensor(f"wk{sfx}1", [128, NC * 128], BF16) as w11,
            nc.sbuf_tensor(f"wv{sfx}1", [128, NC * 128], BF16) as w12,
            nc.sbuf_tensor(f"KT{sfx}", [128, T], BF16) as KT,
            nc.sbuf_tensor(f"VT{sfx}", [128, T], BF16) as VT,
            nc.sbuf_tensor(f"Vtm{sfx}", [128, T], BF16) as Vtm,
            nc.psum_tensor(f"ppj{sfx}0", [128, T], F32) as ppj0,
            nc.psum_tensor(f"ppj{sfx}1", [128, T], F32) as ppj1,
            nc.psum_tensor(f"pvt{sfx}0", [128, 128], BF16) as pvt0,
            nc.psum_tensor(f"pvt{sfx}1", [128, 128], BF16) as pvt1,
        ):
            pb = dict(
                wsl=[dict(q=w00, k=w01, v=w02), dict(q=w10, k=w11, v=w12)],
                ppj=[ppj0, ppj1], pvt=[pvt0, pvt1], KT=KT, VT=VT, Vtm=Vtm,
            )
            run_phase(emit_A, pb)

        with (
            nc.sbuf_tensor(f"zsb{sfx}", [128, H * T], BF16) as zsb,
            nc.sbuf_tensor(f"ktf{sfx}0", [128, S], BF16) as ktf0,
            nc.sbuf_tensor(f"ktf{sfx}1", [128, S], BF16) as ktf1,
            nc.sbuf_tensor(f"vtf{sfx}0", [128, S], BF16) as vtf0,
            nc.sbuf_tensor(f"vtf{sfx}1", [128, S], BF16) as vtf1,
        ):

            def emit_B(cx, b, pb):
                PT = pb["PT"]
                pst = pb["pst"]
                pzt = pb["pzt"]
                prs0, prs1 = pb["prs0"], pb["prs1"]
                rsv, rb = pb["rsv"], pb["rb"]
                ktf, vtf = [ktf0, ktf1], [vtf0, vtf1]
                nst = [0]
                cx.wait("sync", "kvrdy" + sfx)

                def emit_av(h, kc, g):
                    vb = vtf[h % 2]
                    a = js_of(kc) * 128
                    segs = [(a, 512), (512, 1024)] if a < 512 else [(a, 1024)]
                    cx.wait("tensor", "maskk" + sfx, val=g + 1)
                    for si, (s0, s1) in enumerate(segs):
                        stop_seg = (kc == 7) if s0 < 512 else (kc == NB - 1)
                        cx.op("tensor", lambda t, _vb=vb, _kc=kc, _s0=s0, _s1=s1, _st=stop_seg, _g=g: t.matmul(
                            pzt[:, _s0:_s1], _vb[:, _kc * 128 : (_kc + 1) * 128],
                            PT[_g % 2][:, _s0:_s1],
                            start=(_kc == 0), stop=_st))
                        prs = prs0 if s0 < 512 else prs1
                        ro = 0 if s0 < 512 else 512
                        last = si == len(segs) - 1
                        cx.op("tensor", lambda t, _prs=prs, _ro=ro, _s0=s0, _s1=s1, _kc=kc, _st=stop_seg, _g=g: t.matmul(
                            _prs[0:1, _s0 - _ro : _s1 - _ro], b["onesb"][:],
                            PT[_g % 2][:, _s0:_s1],
                            start=(_kc == 0), stop=_st),
                            sem="avk" + sfx if last else None, k=1)

                for h in range(H):
                    kb, vb = ktf[h % 2], vtf[h % 2]
                    if h >= 2:
                        cx.wait("sync", "bcmm" + sfx, val=h - 1)
                    for p_ in range(2):
                        cx.dma("sync",
                               kb.ap().rearrange("p (li half) -> p li half", half=256)[:, :, p_ * 128 : (p_ + 1) * 128],
                               kvK[p_, h].rearrange("p (li r) -> p li r", r=128),
                               "kvrd" + sfx)
                        cx.dma("sync",
                               vb.ap().rearrange("p (li half) -> p li half", half=256)[:, :, p_ * 128 : (p_ + 1) * 128],
                               kvV[p_, h].rearrange("li p r -> p li r"),
                               "kvrd" + sfx)
                    cx.wait("tensor", "kvrd" + sfx)
                    if h >= 1:
                        cx.wait("tensor", "ztev" + sfx, val=h)
                        cx.wait("tensor", "rn1" + sfx, val=2 * h)
                        cx.wait("tensor", "bcev" + sfx, val=h)
                    for kc in range(NB):
                        g = nst[0]
                        a = js_of(kc) * 128
                        segs = [(a, 512), (512, 1024)] if a < 512 else [(a, 1024)]
                        if g >= 2:
                            cx.wait("tensor", "expk" + sfx, val=g - 1)
                        for si, (s0, s1) in enumerate(segs):
                            cx.op("tensor", lambda t, _kb=kb, _kc=kc, _s0=s0, _s1=s1, _g=g: t.matmul(
                                pst[_g % 2][:, _s0:_s1], _kb[:, _kc * 128 : (_kc + 1) * 128],
                                QT[:, h * T + _s0 : h * T + _s1],
                                start=True, stop=True),
                                sem="stmm" + sfx if si == len(segs) - 1 else None)
                        cx.wait("scalar", "stmm" + sfx, val=g + 1)
                        if g >= 2:
                            cx.wait("scalar", "avk" + sfx, val=g - 1)
                        cx.op("scalar", lambda s, _a=a, _g=g: s.activation(
                            PT[_g % 2][:, _a:T], pst[_g % 2][:, _a:T], AF.Exp,
                            scale=INVSQ), "expk" + sfx)
                        cx.wait("vector", "expk" + sfx, val=g + 1)
                        mw = min(256, T - a)
                        cx.op("vector", lambda v, _a=a, _kc=kc, _mw=mw, _g=g: v.tensor_mul(
                            PT[_g % 2][:, _a : _a + _mw], PT[_g % 2][:, _a : _a + _mw],
                            dmask_sb[:, _kc * 256 : _kc * 256 + _mw]), "maskk" + sfx)
                        nst[0] += 1
                        if kc >= 1:
                            emit_av(h, kc - 1, g - 1)
                    emit_av(h, NB - 1, nst[0] - 1)
                    cx.wait("vector", "avk" + sfx, val=nst[0])
                    if h >= 1:
                        cx.wait("vector", "bcmm" + sfx, val=h)
                    cx.op("vector", lambda v: v.tensor_copy(rsv[0:1, 0:512], prs0[0:1, :]), "rnc" + sfx)
                    cx.op("vector", lambda v: v.tensor_copy(rsv[0:1, 512:1024], prs1[0:1, :]), "rnc" + sfx)
                    cx.wait("vector", "rnc" + sfx)
                    cx.op("vector", lambda v: v.reciprocal(rsv[0:1, T : 2 * T], rsv[0:1, 0:T]), "rn1" + sfx, k=2)
                    cx.wait("tensor", "rn1" + sfx)
                    cx.wait("tensor", "expk" + sfx, val=nst[0])
                    for th in range(2):
                        cx.op("tensor", lambda t, _th=th: t.matmul(
                            pst[0][:, _th * 512 : (_th + 1) * 512], b["onesr"][:],
                            rsv[0:1, T + _th * 512 : T + (_th + 1) * 512], start=True, stop=True),
                            sem="bcmm" + sfx if th == 1 else None)
                    cx.wait("scalar", "bcmm" + sfx)
                    cx.op("scalar", lambda s: s.copy(rb[:], pst[0][:]), "bcev" + sfx)
                    cx.wait("vector", "bcev" + sfx)
                    cx.op("vector", lambda v, _h=h: v.tensor_mul(
                        zsb[:, _h * T : (_h + 1) * T], pzt[:], rb[:]), "ztev" + sfx)
                cx.barrier("barB" + sfx)

            with (
                nc.sbuf_tensor(f"PT{sfx}0", [128, T], BF16) as PT0,
                nc.sbuf_tensor(f"PT{sfx}1", [128, T], BF16) as PT1,
                nc.sbuf_tensor(f"rsv{sfx}", [1, 2 * T], F32) as rsv,
                nc.sbuf_tensor(f"rb{sfx}", [128, T], F32) as rb,
                nc.psum_tensor(f"pst{sfx}0", [128, T], F32) as pst0,
                nc.psum_tensor(f"pst{sfx}1", [128, T], F32) as pst1,
                nc.psum_tensor(f"pzt{sfx}", [128, T], F32) as pzt,
                nc.psum_tensor(f"prs0{sfx}", [1, 512], F32) as prs0,
                nc.psum_tensor(f"prs1{sfx}", [1, 512], F32) as prs1,
            ):
                pb = dict(PT=[PT0, PT1], pst=[pst0, pst1], pzt=pzt, prs0=prs0,
                          prs1=prs1, rsv=rsv, rb=rb)
                run_phase(emit_B, pb)

            def emit_C(cx, b, pb):
                wos = pb["wos"]
                po = pb["po"]
                for dtg in range(4):
                    for h in range(H):
                        slot = wos[h % 2]
                        idx = dtg * H + h
                        if idx >= 2:
                            cx.wait("gpsimd", "pomm" + sfx, val=idx - 1)
                        cx.dma("gpsimd", slot[:], wo_ap(h, dtg), f"wold{idx % 2}" + sfx)
                        cx.wait("tensor", f"wold{idx % 2}" + sfx, val=16 * (idx // 2 + 1))
                        if h == 0:
                            cx.wait("tensor", "poev" + sfx, val=8 * dtg)
                        for i in range(4):
                            for th in range(2):
                                last = i == 3 and th == 1
                                cx.op("tensor", lambda t, _po=po[i], _s=slot, _i=i, _th=th, _h=h: t.matmul(
                                    _po[:, _th * 512 : (_th + 1) * 512],
                                    _s[:, _i * 128 : (_i + 1) * 128],
                                    zsb[:, _h * T + _th * 512 : _h * T + (_th + 1) * 512],
                                    start=(_h == 0), stop=(_h == H - 1)),
                                    sem="pomm" + sfx if last else None)
                    cx.wait("vector", "pomm" + sfx)
                    for i in range(4):
                        dti = dtg * 4 + i
                        for th in range(2):
                            cx.op("vector", lambda v, _po=po[i], _dti=dti, _th=th: v.scalar_tensor_tensor(
                                b["xr"][:, _dti * T + _th * 512 : _dti * T + (_th + 1) * 512],
                                _po[:, _th * 512 : (_th + 1) * 512],
                                biaso[:, _dti : _dti + 1],
                                b["xr"][:, _dti * T + _th * 512 : _dti * T + (_th + 1) * 512],
                                op0=ALU.add, op1=ALU.add), "poev" + sfx)
                cx.barrier("barC" + sfx)

            with (
                nc.sbuf_tensor(f"wo{sfx}0", [128, 512], BF16) as wos0,
                nc.sbuf_tensor(f"wo{sfx}1", [128, 512], BF16) as wos1,
                nc.psum_tensor(f"po{sfx}0", [128, T], F32) as po0,
                nc.psum_tensor(f"po{sfx}1", [128, T], F32) as po1,
                nc.psum_tensor(f"po{sfx}2", [128, T], F32) as po2,
                nc.psum_tensor(f"po{sfx}3", [128, T], F32) as po3,
            ):
                run_phase(emit_C, dict(wos=[wos0, wos1], po=[po0, po1, po2, po3]))

    emit_postln_phase(nc, cx, block, b, run_phase, "ln" + sfx)



def emit_postln_phase(nc, cx, block, b, run_phase, tag):
    def emit_D(cx, b, pb):
        xq, tmpf, strow, sbc = pb["xq"], pb["tmpf"], pb["strow"], pb["sbc"]
        ps1a, ps1b, ps2a, ps2b = pb["ps"]
        pbc1, pbc2 = pb["pbc1"], pb["pbc2"]
        xr, xb = b["xr"], b["xb"]
        for dc in range(NC):
            cx.op("vector", lambda v, _dc=dc: v.tensor_copy(
                xb[:, _dc * T : (_dc + 1) * T], xr[:, _dc * T : (_dc + 1) * T]), tag + "cb")
        cx.wait("scalar", tag + "cb")
        for dc in range(NC):
            cx.op("scalar", lambda s, _dc=dc: s.activation(
                xq[:, _dc * T : (_dc + 1) * T], xb[:, _dc * T : (_dc + 1) * T],
                AF.Square), tag + "sq")
        cx.wait("tensor", tag + "cb")
        cx.wait("tensor", tag + "sq")
        for dc in range(NC):
            for th in range(2):
                for src, ps in ((xb, ps1a if th == 0 else ps1b),
                                (xq, ps2a if th == 0 else ps2b)):
                    cx.op("tensor", lambda t, _src=src, _ps=ps, _dc=dc, _th=th: t.matmul(
                        _ps[0:1, :], b["onesb"][:],
                        _src[:, _dc * T + _th * 512 : _dc * T + (_th + 1) * 512],
                        start=(_dc == 0), stop=(_dc == NC - 1)),
                        sem=tag + "st" if dc == NC - 1 else None)
        s1 = lambda: strow[0:1, 0:T]
        s2 = lambda: strow[0:1, T : 2 * T]
        mean = lambda: strow[0:1, 2 * T : 3 * T]
        var = lambda: strow[0:1, 3 * T : 4 * T]
        rstd = lambda: strow[0:1, 4 * T : 5 * T]
        nm = lambda: strow[0:1, 5 * T : 6 * T]
        cx.wait("vector", tag + "st")
        def vsd(fn):
            cx.op("vector", fn, tag + "vc")
            cx.wait("vector", tag + "vc")
        vsd(lambda v: v.tensor_copy(s1()[:, 0:512], ps1a[0:1, :]))
        vsd(lambda v: v.tensor_copy(s1()[:, 512:1024], ps1b[0:1, :]))
        vsd(lambda v: v.tensor_copy(s2()[:, 0:512], ps2a[0:1, :]))
        vsd(lambda v: v.tensor_copy(s2()[:, 512:1024], ps2b[0:1, :]))
        vsd(lambda v: v.tensor_scalar_mul(mean(), s1(), 1.0 / D))
        vsd(lambda v: v.tensor_scalar_mul(s2(), s2(), 1.0 / D))
        vsd(lambda v: v.tensor_mul(var(), mean(), mean()))
        cx.op("vector", lambda v: v.tensor_sub(var(), s2(), var()), tag + "v1")
        cx.wait("scalar", tag + "v1")
        cx.op("scalar", lambda s: s.activation(var(), var(), AF.Ln, bias=b["epsc"][0:1, 0:1]), tag + "sc1")
        cx.wait("scalar", tag + "sc1")
        cx.op("scalar", lambda s: s.activation(rstd(), var(), AF.Exp, scale=-0.5), tag + "v2")
        cx.wait("vector", tag + "v2")
        cx.op("vector", lambda v: v.scalar_tensor_tensor(
            nm(), mean(), -1.0, rstd(), op0=ALU.mult, op1=ALU.mult), tag + "v3")
        cx.wait("tensor", tag + "v3")
        for th in range(2):
            cx.op("tensor", lambda t, _th=th: t.matmul(
                pbc1[:, _th * 512 : (_th + 1) * 512], b["onesr"][:],
                rstd()[:, _th * 512 : (_th + 1) * 512], start=True, stop=True))
            cx.op("tensor", lambda t, _th=th: t.matmul(
                pbc2[:, _th * 512 : (_th + 1) * 512], b["onesr"][:],
                nm()[:, _th * 512 : (_th + 1) * 512], start=True, stop=True),
                sem=tag + "bc" if th == 1 else None)
        cx.wait("scalar", tag + "bc")
        cx.op("scalar", lambda s: s.copy(sbc[:, 0:T], pbc1[:]))
        cx.op("scalar", lambda s: s.copy(sbc[:, T : 2 * T], pbc2[:]), tag + "bce")
        cx.wait("vector", tag + "bce")
        for dc in range(NC):
            cx.op("vector", lambda v, _dc=dc: v.tensor_mul(
                tmpf[:], xr[:, _dc * T : (_dc + 1) * T], sbc[:, 0:T]), tag + "vc")
            cx.wait("vector", tag + "vc")
            cx.op("vector", lambda v, _dc=dc: v.tensor_add(
                xr[:, _dc * T : (_dc + 1) * T], tmpf[:], sbc[:, T : 2 * T]), tag + "ap")
        for dc in range(NC):
            cx.wait("scalar", tag + "ap", val=dc + 1)
            cx.op("scalar", lambda s, _dc=dc: s.copy(
                xb[:, _dc * T : (_dc + 1) * T], xr[:, _dc * T : (_dc + 1) * T]), tag + "cb2")
        cx.barrier("bar" + tag)

    with (
        nc.sbuf_tensor(tag + "xq", [128, NC * T], BF16) as xq,
        nc.sbuf_tensor(tag + "tmpf", [128, T], F32) as tmpf,
        nc.sbuf_tensor(tag + "strow", [1, 8 * T], F32) as strow,
        nc.sbuf_tensor(tag + "sbc", [128, 2 * T], F32) as sbc,
        nc.psum_tensor(tag + "ps1a", [1, 512], F32) as ps1a,
        nc.psum_tensor(tag + "ps1b", [1, 512], F32) as ps1b,
        nc.psum_tensor(tag + "ps2a", [1, 512], F32) as ps2a,
        nc.psum_tensor(tag + "ps2b", [1, 512], F32) as ps2b,
        nc.psum_tensor(tag + "pbc1", [128, T], F32) as pbc1,
        nc.psum_tensor(tag + "pbc2", [128, T], F32) as pbc2,
    ):
        run_phase(emit_D, dict(xq=xq, tmpf=tmpf, strow=strow, sbc=sbc,
                               ps=[ps1a, ps1b, ps2a, ps2b], pbc1=pbc1, pbc2=pbc2))



def emit_quant_phase(nc, cx, block, b, run_phase):
    dt = b["dt"]
    QW = 4096

    def emit_Q(cx, b, pb):
        wbf, wh = pb["wbf"], pb["wh"]
        gsf, scb, rsb = pb["gsf"], pb["scb"], pb["rsb"]
        ybf, t1, t2 = pb["ybf"], pb["t1"], pb["t2"]
        jobs = []
        for si in range(64):
            src = dt["qgu_my"].ap()[si * 128 : (si + 1) * 128, :]
            o0 = b["qgu_hat"][si * 128 : (si + 1) * 128, :]
            o1 = b["qgu_hat"][HG + si * 128 : HG + (si + 1) * 128, :]
            jobs.append((src, D, o0, o1))
        for si in range(8):
            for hw in range(2):
                c0, c1 = hw * QW, (hw + 1) * QW
                src = dt["qpr_my"].ap()[si * 128 : (si + 1) * 128, c0:c1]
                o0 = b["qpr_hat"][si * 128 : (si + 1) * 128, c0:c1]
                o1 = b["qpr_hat"][1024 + si * 128 : 1024 + (si + 1) * 128, c0:c1]
                jobs.append((src, QW, o0, o1))
        for ji, (src, W, o0, o1) in enumerate(jobs):
            sl = ji % 2
            G = W // 64
            if ji >= 2:
                cx.wait("gpsimd", "qv1", val=ji - 1)
            cx.dma("gpsimd", wbf[sl][:, 0:W], src, "qld")
            cx.wait("vector", "qld")
            if ji >= 2:
                cx.wait("vector", f"qwr{ji % 2}", val=16 * (ji // 2))
            wv = wbf[sl].ap()[:, 0:W].rearrange("p (g k) -> p g k", k=64)
            scbc = scb.ap()[:, 0:G].unsqueeze(2).broadcast_to([128, G, 64])
            rsbc = rsb.ap()[:, 0:G].unsqueeze(2).broadcast_to([128, G, 64])
            yv = ybf.ap()[:, 0:W].rearrange("p (g k) -> p g k", k=64)
            cx.op("vector", lambda v, _wv=wv, _G=G: v.reduce_sum(
                gsf[:, 0:_G], _wv, axis=AX.X, apply_absolute_value=True), "qvc")
            cx.wait("vector", "qvc")
            cx.op("vector", lambda v, _G=G: v.tensor_scalar(
                scb[:, 0:_G], gsf[:, 0:_G], 1.0 / 64.0, 1e-8,
                op0=ALU.mult, op1=ALU.max), "qvc")
            cx.wait("vector", "qvc")
            cx.op("vector", lambda v, _G=G: v.reciprocal(rsb[:, 0:_G], scb[:, 0:_G]), "qvc")
            cx.wait("vector", "qvc")
            cx.op("vector", lambda v, _wv=wv, _y=yv, _r=rsbc: v.tensor_tensor(
                _y, _wv, _r, op=ALU.mult), "qv1")
            cx.wait("vector", "qv1")
            cx.op("vector", lambda v, _W=W: v.tensor_single_scalar(
                t1[:, 0:_W], ybf[:, 0:_W], 0.5, op=ALU.is_gt), "qvc")
            cx.op("vector", lambda v, _W=W: v.tensor_single_scalar(
                t2[:, 0:_W], ybf[:, 0:_W], -0.5, op=ALU.is_lt), "qvc")
            cx.wait("vector", "qvc")
            cx.op("vector", lambda v, _W=W: v.tensor_sub(
                t1[:, 0:_W], t1[:, 0:_W], t2[:, 0:_W]), "qvc")
            cx.wait("vector", "qvc")
            cx.op("vector", lambda v, _sl=sl, _W=W, _s=scbc: v.tensor_tensor(
                wh[_sl].ap()[:, 0:_W].rearrange("p (g k) -> p g k", k=64),
                t1.ap()[:, 0:_W].rearrange("p (g k) -> p g k", k=64),
                _s, op=ALU.mult), "qv2")
            cx.wait("vector", "qv2")
            cx.wait("sync", "qv2")
            cx.ifpar_dma("sync", o0, o1, wh[sl][:, 0:W], f"qwr{ji % 2}", b["pid_sb"])
        cx.wait("gpsimd", "qwr0")
        cx.wait("gpsimd", "qwr1")
        cx.write_flag(2 * EV_QNT, 2 * EV_QNT + 1, b["fvals"], b["sh_flags"], b["pid_sb"])
        cx.poll_flag_pair(2 * EV_QNT, b["fvals"], b["spoll"], b["sh_flags"], "qrdy", b["pid_sb"])
        cx.barrier("barQ")

    with (
        nc.sbuf_tensor("qwbf0", [128, QW], BF16) as wbf0,
        nc.sbuf_tensor("qwbf1", [128, QW], BF16) as wbf1,
        nc.sbuf_tensor("qwh0", [128, QW], BF16) as wh0,
        nc.sbuf_tensor("qwh1", [128, QW], BF16) as wh1,
        nc.sbuf_tensor("qgsf", [128, QW // 64], F32) as gsf,
        nc.sbuf_tensor("qscb", [128, QW // 64], BF16) as scb,
        nc.sbuf_tensor("qrsb", [128, QW // 64], F32) as rsb,
        nc.sbuf_tensor("qybf", [128, QW], BF16) as ybf,
        nc.sbuf_tensor("qt1", [128, QW], BF16) as t1,
        nc.sbuf_tensor("qt2", [128, QW], BF16) as t2,
    ):
        run_phase(emit_Q, dict(wbf=[wbf0, wbf1], wh=[wh0, wh1], gsf=gsf,
                               scb=scb, rsb=rsb, ybf=ybf, t1=t1, t2=t2))



def emit_ffn0_phase(nc, cx, block, b, run_phase):
    BR = 512
    NGB = HG // BR
    GPB = BR // 128

    def emit_F(cx, b, pb):
        gsl, usl, psl = pb["gsl"], pb["usl"], pb["psl"]
        hbf, acc, sgf = pb["hbf"], pb["acc"], pb["sgf"]
        fsq = pb["fsq"]
        strow, sbc = pb["strow"], pb["sbc"]
        pg, pu, pss, pw = pb["pg"], pb["pu"], pb["pss"], pb["pw"]
        pbc1 = pb["pbc1"]
        xr, xb = b["xr"], b["xb"]
        cx.wait("sync", "qrdy")
        ngt = [0]
        nacc = [0]
        for th in range(2):
            to = th * 512
            for gb in range(NGB):
                if th > 0 or gb > 0:
                    cx.wait("sync", "fgu_mm", val=ngt[0])
                for dc in range(NC):
                    cx.dma("sync", gsl[:, dc * BR : (dc + 1) * BR],
                           b["qgu_hat"][gb * BR : (gb + 1) * BR, dc * 128 : (dc + 1) * 128],
                           "fld", transpose=True)
                    cx.dma("sync", usl[:, dc * BR : (dc + 1) * BR],
                           b["qgu_hat"][HG + gb * BR : HG + (gb + 1) * BR, dc * 128 : (dc + 1) * 128],
                           "fld", transpose=True)
                cx.wait("tensor", "fld")
                for gt in range(GPB):
                    if ngt[0] >= 1:
                        cx.wait("tensor", "fgu_ev", val=ngt[0])
                    for dc in range(NC):
                        cx.op("tensor", lambda t, _gt=gt, _dc=dc, _to=to: t.matmul(
                            pg[:], gsl[:, _dc * BR + _gt * 128 : _dc * BR + (_gt + 1) * 128],
                            xb[:, _dc * T + _to : _dc * T + _to + 512],
                            start=(_dc == 0), stop=(_dc == NC - 1)))
                        cx.op("tensor", lambda t, _gt=gt, _dc=dc, _to=to: t.matmul(
                            pu[:], usl[:, _dc * BR + _gt * 128 : _dc * BR + (_gt + 1) * 128],
                            xb[:, _dc * T + _to : _dc * T + _to + 512],
                            start=(_dc == 0), stop=(_dc == NC - 1)),
                            sem="fgu_mm" if dc == NC - 1 else None)
                    cx.wait("scalar", "fgu_mm")
                    cx.op("scalar", lambda s: s.activation(sgf[:, 0:512], pg[:], AF.Silu), "fsg")
                    cx.wait("vector", "fsg")
                    cx.op("vector", lambda v, _gt=gt: v.tensor_mul(
                        hbf[:, _gt * 512 : (_gt + 1) * 512], sgf[:, 0:512], pu[:]), "fgu_ev")
                    ngt[0] += 1
                    cx.wait("scalar", "fgu_ev", val=ngt[0])
                    if ngt[0] >= 2:
                        cx.wait("scalar", "fssm", val=ngt[0] - 1)
                    cx.op("scalar", lambda s, _gt=gt: s.activation(
                        fsq[:], hbf[:, _gt * 512 : (_gt + 1) * 512], AF.Square), "fsqd")
                    cx.wait("tensor", "fsqd")
                    cx.op("tensor", lambda t, _gt=gt, _gb=gb, _th=th: t.matmul(
                        pss[0:1, :], b["onesb"][:], fsq[:],
                        start=(_gb == 0 and _gt == 0), stop=(_gb == NGB - 1 and _gt == GPB - 1)),
                        sem="fssm")
                if th > 0 or gb > 0:
                    cx.wait("sync", "fpj_mm", val=(th * NGB + gb) * NC)
                for hc in range(GPB):
                    cx.dma("sync", psl[:, hc * D : (hc + 1) * D],
                           b["qpr_hat"][:, (gb * GPB + hc) * 128 : (gb * GPB + hc + 1) * 128],
                           "fpld", transpose=True)
                cx.wait("tensor", "fpld")
                for dt_ in range(NC):
                    pwc = pw[dt_ % 2]
                    if nacc[0] >= 2:
                        cx.wait("tensor", "facc", val=nacc[0] - 1)
                    for hc in range(GPB):
                        cx.op("tensor", lambda t, _p=pwc, _dt=dt_, _hc=hc: t.matmul(
                            _p[:], psl[:, _hc * D + _dt * 128 : _hc * D + (_dt + 1) * 128],
                            hbf[:, _hc * 512 : (_hc + 1) * 512],
                            start=(_hc == 0), stop=(_hc == GPB - 1)),
                            sem="fpj_mm" if hc == GPB - 1 else None)
                    cx.wait("vector", "fpj_mm")
                    if gb == 0:
                        cx.op("vector", lambda v, _pw=pwc, _dt=dt_: v.tensor_copy(
                            acc[:, _dt * 512 : (_dt + 1) * 512], _pw[:]), "facc")
                    else:
                        cx.op("vector", lambda v, _pw=pwc, _dt=dt_: v.tensor_add(
                            acc[:, _dt * 512 : (_dt + 1) * 512],
                            acc[:, _dt * 512 : (_dt + 1) * 512], _pw[:]), "facc")
                    nacc[0] += 1
            cx.wait("vector", "facc")
            mrow = lambda: strow[0:1, 0:512]
            rrow = lambda: strow[0:1, 512:1024]
            cx.op("vector", lambda v: v.tensor_copy(mrow(), pss[0:1, :]), "frmc")
            cx.wait("vector", "frmc")
            cx.op("vector", lambda v: v.tensor_scalar_mul(mrow(), mrow(), 1.0 / HG), "frm1")
            cx.wait("scalar", "frm1")
            cx.op("scalar", lambda s: s.activation(mrow(), mrow(), AF.Ln, bias=b["epsc"][0:1, 1:2]), "frmd")
            cx.wait("scalar", "frmd")
            cx.op("scalar", lambda s: s.activation(rrow(), mrow(), AF.Exp, scale=-0.5), "frm2")
            cx.wait("tensor", "frm2")
            cx.op("tensor", lambda t: t.matmul(
                pbc1[:], b["onesr"][:], rrow(), start=True, stop=True), "frbc")
            cx.wait("scalar", "frbc")
            cx.op("scalar", lambda s: s.copy(sbc[:, 0:512], pbc1[:]), "frbe")
            cx.wait("vector", "frbe")
            for dt_ in range(NC):
                cx.op("vector", lambda v, _dt=dt_: v.tensor_mul(
                    sgf[:, 0:512], acc[:, _dt * 512 : (_dt + 1) * 512], sbc[:, 0:512]), "frmc")
                cx.wait("vector", "frmc")
                cx.op("vector", lambda v, _dt=dt_, _to=to: v.tensor_add(
                    xr[:, _dt * T + _to : _dt * T + _to + 512],
                    xr[:, _dt * T + _to : _dt * T + _to + 512], sgf[:, 0:512]), "fres")
            cx.wait("tensor", "fres")
        cx.barrier("barF")

    with contextlib.ExitStack() as st_:
        A = st_.enter_context
        gsl = A(nc.sbuf_tensor("fgsl", [128, NC * BR], BF16))
        usl = A(nc.sbuf_tensor("fusl", [128, NC * BR], BF16))
        psl = A(nc.sbuf_tensor("fpsl", [128, GPB * D], BF16))
        hbf = A(nc.sbuf_tensor("fhbf", [128, GPB * 512], BF16))
        acc = A(nc.sbuf_tensor("faccb", [128, NC * 512], F32))
        sgf = A(nc.sbuf_tensor("fsgf", [128, 512], F32))
        fsq = A(nc.sbuf_tensor("ffsq", [128, 512], BF16))
        strow = A(nc.sbuf_tensor("fstrow", [1, 2 * T], F32))
        sbc = A(nc.sbuf_tensor("fsbc", [128, 512], F32))
        pg = A(nc.psum_tensor("fpg", [128, 512], F32))
        pu = A(nc.psum_tensor("fpu", [128, 512], F32))
        pss = A(nc.psum_tensor("fpss", [1, 512], F32))
        pw0 = A(nc.psum_tensor("fpw0", [128, 512], F32))
        pw1 = A(nc.psum_tensor("fpw1", [128, 512], F32))
        pbc1 = A(nc.psum_tensor("fpbc1", [128, 512], F32))
        run_phase(emit_F, dict(gsl=gsl, usl=usl, psl=psl, hbf=hbf, acc=acc,
                               sgf=sgf, fsq=fsq, strow=strow, sbc=sbc, pg=pg, pu=pu,
                               pss=pss, pw=[pw0, pw1], pbc1=pbc1))
    emit_postln_phase(nc, cx, block, b, run_phase, "lnF")



def emit_moe_phase(nc, cx, block, b, run_phase, skip_ln=False):
    dt = b["dt"]

    def emit_R(cx, b, pb):
        gwt, scT, sc, pT = pb["gwt"], pb["scT"], pb["sc"], pb["pT"]
        gbc, sel = pb["gbc"], pb["sel"]
        work, tmp16, mx, msk, wsr = pb["work"], pb["tmp16"], pb["mx"], pb["msk"], pb["wsr"]
        pr, ptp = pb["pr"], pb["ptp"]
        cx.dma("sync", gwt.ap().rearrange("p (c e) -> p c e", e=NE),
               dt["gate_wT"].ap().rearrange("(c p) e -> p c e", p=128), "rld")
        cx.dma("sync", gbc[:], dt["gate_b_col"].ap(), "rld")
        cx.dma("sync", sel[:], dt["sel16"].ap(), "rld")
        cx.wait("tensor", "rld")
        for dc in range(NC):
            for th in range(2):
                cx.op("tensor", lambda t, _dc=dc, _th=th: t.matmul(
                    pr[:, _th * 512 : (_th + 1) * 512],
                    gwt[:, _dc * NE : (_dc + 1) * NE],
                    b["xr"][:, _dc * T + _th * 512 : _dc * T + (_th + 1) * 512],
                    start=(_dc == 0), stop=(_dc == NC - 1)),
                    sem="rmm" if (dc == NC - 1 and th == 1) else None)
        cx.wait("scalar", "rmm")
        cx.op("scalar", lambda s: s.activation(scT[:], pr[0:NE, :], AF.Sigmoid), "rsg")
        cx.wait("vector", "rsg")
        cx.op("vector", lambda v: v.tensor_scalar_add(scT[:], scT[:], gbc[:]), "rsb")
        cx.wait("tensor", "rsb")
        for j in range(8):
            cx.op("tensor", lambda t, _j=j: t.transpose(
                ptp[:, 0:NE], scT[:, _j * 128 : (_j + 1) * 128], b["idf"][0:NE, 0:NE]),
                "rtp")
            cx.wait("vector", "rtp")
            cx.op("vector", lambda v, _j=j: v.tensor_copy(
                sc[:, _j * NE : (_j + 1) * NE], ptp[:, 0:NE]), "rtc")
            cx.wait("tensor", "rtc")
        def vs(fn):
            cx.op("vector", fn, "rvc")
            cx.wait("vector", "rvc")
        for j in range(8):
            scj = sc[:, j * NE : (j + 1) * NE]
            vs(lambda v, _s=scj: v.tensor_copy(work[:], _s))
            for r in range(TOPK):
                vs(lambda v, _r=r: v.reduce_max(
                    mx[:, _r : _r + 1], work[:], axis=AX.X))
                if r < TOPK - 1:
                    vs(lambda v, _r=r: v.tensor_scalar(
                        tmp16[:], work[:], mx[:, _r : _r + 1], -1e30,
                        op0=ALU.is_equal, op1=ALU.mult))
                    vs(lambda v: v.tensor_add(work[:], work[:], tmp16[:]))
            vs(lambda v, _s=scj: v.scalar_tensor_tensor(
                msk[:], _s, mx[:, TOPK - 1 : TOPK], _s, op0=ALU.is_ge, op1=ALU.mult))
            vs(lambda v: v.reduce_sum(wsr[:, 0:1], msk[:], axis=AX.X))
            vs(lambda v: v.reciprocal(wsr[:, 1:2], wsr[:, 0:1]))
            cx.op("vector", lambda v, _j=j: v.tensor_scalar_mul(
                sc[:, _j * NE : (_j + 1) * NE], msk[:], wsr[:, 1:2]), "rpj")
            cx.wait("vector", "rpj")
        cx.wait("tensor", "rpj")
        for j in range(8):
            if j >= 1:
                cx.wait("tensor", "rpe", val=j)
            cx.op("tensor", lambda t, _j=j: t.transpose(
                ptp[0:NE, 0:128],
                sc[:, _j * NE : (_j + 1) * NE], b["idf"][:]), "rtb")
            cx.wait("scalar", "rtb")
            cx.op("scalar", lambda s, _j=j: s.copy(
                pT[:, _j * 128 : (_j + 1) * 128], ptp[0:NE, 0:128]), "rpe")
        cx.barrier("barR")

    with (
        nc.sbuf_tensor("mgwt", [128, NC * NE], F32) as gwt,
        nc.sbuf_tensor("mscT", [NE, T], F32) as scT,
        nc.sbuf_tensor("msc", [128, 8 * NE], F32) as sc,
        nc.sbuf_tensor("mpT", [NE, 8 * 128], F32) as pT_,
        nc.sbuf_tensor("mgbc", [NE, 1], F32) as gbc,
        nc.sbuf_tensor("msel", [NE, NE * 128], F32) as sel,
        nc.sbuf_tensor("mwork", [128, NE], F32) as work,
        nc.sbuf_tensor("mtmp16", [128, NE], F32) as tmp16,
        nc.sbuf_tensor("mmx", [128, TOPK], F32) as mx,
        nc.sbuf_tensor("mmsk", [128, NE], F32) as msk,
        nc.sbuf_tensor("mwsr", [128, 2], F32) as wsr,
    ):
        with contextlib.ExitStack() as rst_:
            pr = rst_.enter_context(nc.psum_tensor("mpr", [NE, T], F32))
            ptp = rst_.enter_context(nc.psum_tensor("mptp", [128, 128], F32))
            run_phase(emit_R, dict(gwt=gwt, scT=scT, sc=sc, pT=pT_, gbc=gbc, sel=sel,
                                   work=work, tmp16=tmp16, mx=mx, msk=msk, wsr=wsr,
                                   pr=pr, ptp=ptp))

        def emit_E(cx, b, pb):
            w1s, w3s, w2s = pb["w1s"], pb["w3s"], pb["w2s"]
            h3s, s1f, peb = pb["h3s"], pb["s1f"], pb["peb"]
            pbe, ph1, ph2, pw = pb["pbe"], pb["ph1"], pb["ph2"], pb["pw"]
            xr, xb = b["xr"], b["xb"]
            nht = [0]
            nw2 = [0]
            for e in range(NE + 1):
                routed = e < NE
                if routed:
                    w1t, w3t = dt["L1_ffn_w1_routed"].ap()[e], dt["L1_ffn_w3_routed"].ap()[e]
                    w2t = dt["L1_ffn_w2_routed"].ap()[e]
                else:
                    w1t, w3t = dt["L1_ffn_w1_shared"].ap(), dt["L1_ffn_w3_shared"].ap()
                    w2t = dt["L1_ffn_w2_shared"].ap()
                if routed:
                    if e >= 1:
                        cx.wait("tensor", "mbev", val=e)
                    for th in range(2):
                        cx.op("tensor", lambda t, _e=e, _th=th: t.matmul(
                            pbe[:, _th * 512 : (_th + 1) * 512],
                            sel[:, _e * 128 : (_e + 1) * 128],
                            pT_[:, _th * 512 : (_th + 1) * 512],
                            start=True, stop=True),
                            sem="mbe" if th == 1 else None)
                    cx.wait("scalar", "mbe")
                    if e >= 1:
                        cx.wait("scalar", "mh3", val=e * HMT)
                    cx.op("scalar", lambda s: s.copy(peb[:], pbe[:]), "mbev")
                if e >= 1:
                    cx.wait("gpsimd", "mw2mm", val=nw2[0])
                for hc in range(HMT):
                    cx.dma("gpsimd", w2s[:, hc * D : (hc + 1) * D],
                           w2t[hc * 128 : (hc + 1) * 128, :], "mw2ld")
                for ht in range(HMT):
                    sl = ht % 2
                    idx = e * HMT + ht
                    if idx >= 2:
                        cx.wait("gpsimd", "mhmm", val=2 * (idx - 1))
                    cx.dma("gpsimd", w1s[sl].ap().rearrange("p (c e) -> p c e", e=128),
                           w1t[:, ht * 128 : (ht + 1) * 128].rearrange("(c p) e -> p c e", p=128),
                           "mwld")
                    cx.dma("gpsimd", w3s[sl].ap().rearrange("p (c e) -> p c e", e=128),
                           w3t[:, ht * 128 : (ht + 1) * 128].rearrange("(c p) e -> p c e", p=128),
                           "mwld")
                    cx.wait("tensor", "mwld")
                    if idx >= 1:
                        cx.wait("tensor", "mh3", val=nht[0])
                    if routed and ht == 0:
                        cx.wait("vector", "mbev", val=e + 1)
                    for dc in range(NC):
                        for th in range(2):
                            cx.op("tensor", lambda t, _s=w1s[sl], _dc=dc, _th=th: t.matmul(
                                ph1[:, _th * 512 : (_th + 1) * 512],
                                _s[:, _dc * 128 : (_dc + 1) * 128],
                                xb[:, _dc * T + _th * 512 : _dc * T + (_th + 1) * 512],
                                start=(_dc == 0), stop=(_dc == NC - 1)))
                            cx.op("tensor", lambda t, _s=w3s[sl], _dc=dc, _th=th: t.matmul(
                                ph2[:, _th * 512 : (_th + 1) * 512],
                                _s[:, _dc * 128 : (_dc + 1) * 128],
                                xb[:, _dc * T + _th * 512 : _dc * T + (_th + 1) * 512],
                                start=(_dc == 0), stop=(_dc == NC - 1)),
                                sem="mhmm" if (dc == NC - 1 and th == 1) else None, k=2)
                    cx.wait("scalar", "mhmm")
                    if routed:
                        cx.op("scalar", lambda s: s.copy(s1f[:], ph1[:]), "mcv1")
                        cx.wait("vector", "mcv1")
                        cx.op("vector", lambda v: v.tensor_mul(s1f[:], s1f[:], ph2[:]), "mcv2")
                        cx.wait("scalar", "mcv2")
                        cx.op("scalar", lambda s: s.activation(s1f[:], s1f[:], AF.Silu), "mcv3")
                        cx.wait("vector", "mcv3")
                        cx.op("vector", lambda v, _ht=ht: v.tensor_mul(
                            h3s[:, _ht * T : (_ht + 1) * T], s1f[:], peb[:]), "mh3")
                    else:
                        cx.op("scalar", lambda s: s.activation(s1f[:], ph1[:], AF.Silu), "mcv1")
                        cx.mark("vector", "mcv2")
                        cx.mark("scalar", "mcv3")
                        cx.wait("vector", "mcv3")
                        cx.op("vector", lambda v, _ht=ht: v.tensor_mul(
                            h3s[:, _ht * T : (_ht + 1) * T], s1f[:], ph2[:]), "mh3")
                    nht[0] += 1
                cx.wait("tensor", "mw2ld")
                cx.wait("tensor", "mh3", val=nht[0])
                for dt_ in range(NC):
                    for th in range(2):
                        pwc = pw[(dt_ * 2 + th) % 2]
                        if nw2[0] >= 2:
                            cx.wait("tensor", "mxw", val=nw2[0] - 1)
                        for hc in range(HMT):
                            cx.op("tensor", lambda t, _p=pwc, _dt=dt_, _th=th, _hc=hc: t.matmul(
                                _p[:], w2s[:, _hc * D + _dt * 128 : _hc * D + (_dt + 1) * 128],
                                h3s[:, _hc * T + _th * 512 : _hc * T + (_th + 1) * 512],
                                start=(_hc == 0), stop=(_hc == HMT - 1)),
                                sem="mw2mm" if hc == HMT - 1 else None)
                        cx.wait("vector", "mw2mm")
                        cx.op("vector", lambda v, _p=pwc, _dt=dt_, _th=th: v.tensor_add(
                            xr[:, _dt * T + _th * 512 : _dt * T + (_th + 1) * 512],
                            xr[:, _dt * T + _th * 512 : _dt * T + (_th + 1) * 512],
                            _p[:]), "mxw")
                        nw2[0] += 1
            cx.barrier("barE")

        with contextlib.ExitStack() as st_:
            A = st_.enter_context
            w1s0 = A(nc.sbuf_tensor("mw1s0", [128, NC * 128], BF16))
            w1s1 = A(nc.sbuf_tensor("mw1s1", [128, NC * 128], BF16))
            w3s0 = A(nc.sbuf_tensor("mw3s0", [128, NC * 128], BF16))
            w3s1 = A(nc.sbuf_tensor("mw3s1", [128, NC * 128], BF16))
            w2s = A(nc.sbuf_tensor("mw2s", [128, HMT * D], BF16))
            h3s = A(nc.sbuf_tensor("mh3s", [128, HMT * T], BF16))
            s1f = A(nc.sbuf_tensor("ms1f", [128, T], F32))
            peb = A(nc.sbuf_tensor("mpeb", [128, T], F32))
            pbe = A(nc.psum_tensor("mpbe", [128, T], F32))
            ph1 = A(nc.psum_tensor("mph1", [128, T], F32))
            ph2 = A(nc.psum_tensor("mph2", [128, T], F32))
            pw0 = A(nc.psum_tensor("mpw0", [128, 512], F32))
            pw1 = A(nc.psum_tensor("mpw1", [128, 512], F32))
            run_phase(emit_E, dict(w1s=[w1s0, w1s1], w3s=[w3s0, w3s1], w2s=w2s,
                                   h3s=h3s, s1f=s1f, peb=peb,
                                   pbe=pbe, ph1=ph1, ph2=ph2, pw=[pw0, pw1]))
    if not skip_ln:
        emit_postln_phase(nc, cx, block, b, run_phase, "lnM")



def own_rows(p):
    return np.concatenate([np.arange(g * 128, (g + 1) * 128) for g in own_blocks(p)])


def _consts(p, nonce):
    d = {}
    d["fvals"] = (nonce + np.arange(NFLAG, dtype=np.int64) % (2**31 - 1)).astype(np.int32).reshape(1, NFLAG)
    d["ident_f32"] = np.eye(128, dtype=np.float32)
    d["ident_bf"] = np.eye(128, dtype=ml_dtypes.bfloat16)
    d["ones_bf"] = np.ones((128, 1), dtype=ml_dtypes.bfloat16)
    d["onesrow_f32"] = np.ones((1, 128), dtype=np.float32)
    d["dmask"] = make_dmask(p)
    return d


def make_in_maps(inputs, phases, extra=None):
    need = set(phases)
    f = lambda k: np.ascontiguousarray(np.asarray(inputs[k], dtype=np.float32))
    nonce = int(np.random.randint(1, 2**30))
    maps = []
    for c in range(8):
        bb, p = c >> 1, c & 1
        m = dict(_consts(p, nonce))
        if "load_x" in need:
            m["x_own"] = np.ascontiguousarray(f("L0_attn_x")[bb][own_rows(p)])
        if "loadT" in need:
            m["xT_in"] = extra[c]
        if "l0attn" in need:
            for k in ("L0_attn_W_Q", "L0_attn_W_K", "L0_attn_W_V", "L0_attn_W_O"):
                m[k] = f(k)
            m["bqT"] = np.ascontiguousarray(f("L0_attn_b_Q").T)
            m["bkT"] = np.ascontiguousarray(f("L0_attn_b_K").T)
            m["bvT"] = np.ascontiguousarray(f("L0_attn_b_V").T)
            m["boT"] = np.ascontiguousarray(f("L0_attn_b_O").reshape(NC, 128).T)
        if "l1attn" in need:
            m["wqkvT"] = np.ascontiguousarray(f("L1_attn_w_qkv").T)
            m["woutT"] = np.ascontiguousarray(f("L1_attn_w_out").T)
            m["bqkvT"] = np.ascontiguousarray(
                f("L1_attn_b_qkv").reshape(3, H, 128).transpose(2, 0, 1).reshape(128, 3 * H))
            m["boutT"] = np.ascontiguousarray(f("L1_attn_b_out").reshape(NC, 128).T)
        if "quant" in need:
            wgu = f("L0_ffn_w_gate_up")
            m["qgu_my"] = np.ascontiguousarray(wgu[0:HG] if p == 0 else wgu[HG:])
            wpr = f("L0_ffn_w_proj")
            m["qpr_my"] = np.ascontiguousarray(wpr[p * 1024 : (p + 1) * 1024])
        if "moe" in need or "moe_noln" in need:
            m["gate_wT"] = np.ascontiguousarray(f("L1_ffn_gate_w").T)
            m["gate_b_col"] = np.ascontiguousarray(f("L1_ffn_gate_b").reshape(NE, 1))
            m["sel16"] = np.ascontiguousarray(np.repeat(np.eye(NE, dtype=np.float32), 128, axis=1))
            for k in ("L1_ffn_w1_shared", "L1_ffn_w2_shared", "L1_ffn_w3_shared",
                      "L1_ffn_w1_routed", "L1_ffn_w2_routed", "L1_ffn_w3_routed"):
                m[k] = f(k)
        maps.append(m)
    return maps


FUSED_PHASES = ("load_x", "l0attn", "quant", "ffn0", "l1attn", "moe", "store")


def kernel(**inputs):
    nc = build_program(list(FUSED_PHASES))
    in_maps = make_in_maps(inputs, FUSED_PHASES)
    res = run_bass_kernel_spmd(nc, in_maps, core_ids=list(range(8)))
    y = np.zeros((B, S, D), np.float32)
    for c in range(8):
        bb, p = c >> 1, c & 1
        y[bb][own_rows(p)] = res.results[c]["y_own"]
    return y


# version 08 (final)
# speedup vs baseline: 1.1150x; 1.0254x over previous
import contextlib
import re
import numpy as np
import ml_dtypes
import concourse.bass as bass
import concourse.mybir as mybir
from concourse.bass_utils import run_bass_kernel_spmd

F32 = mybir.dt.float32
BF16 = mybir.dt.bfloat16
I32 = mybir.dt.int32
AF = mybir.ActivationFunctionType
ALU = mybir.AluOpType
AX = mybir.AxisListType

B, S, D = 4, 2048, 2048
H, DH = 16, 128
NB = 16
T = 1024
NC = 16
HG = 8192
NE = 16
HM = 1408
HMT = 11
TOPK = 4
INVSQ = float(1.0 / np.sqrt(128.0))
LN_EPS = 1e-5
RMS_EPS = float(np.finfo(np.float32).eps)

ENGINES = ("tensor", "vector", "scalar", "sync", "gpsimd")

EV_KV0, EV_QNT, EV_KV1 = 0, 1, 2
NFLAG = 16


def _ceil_div(a, b):
    return (a + b - 1) // b


def js_of(kc):
    return max(0, _ceil_div(kc - 1, 2))


class Ctx:

    @staticmethod
    def canon(name):
        if name.startswith("bar"):
            return "bar"
        name = name.replace("_at1", "_at0")
        name = re.sub(r"^(ln_at0|lnF|lnM)", "LNx", name)
        return name

    def __init__(self, nc):
        self.nc = nc
        self.sems = {}
        self.counts = {}
        self.pbase = {}
        self.cur = None
        self.eng = None
        self._rn = 0

    def rn(self, p):
        self._rn += 1
        return f"{p}{self._rn}"

    def h(self, name):
        name = self.canon(name)
        if name not in self.sems:
            self.sems[name] = self.nc.alloc_semaphore(name=name)
        self.counts.setdefault(name, 0)
        return self.sems[name]

    def snapshot(self):
        return dict(self.counts)

    def start_pass(self, name, eng, base):
        self.cur = name
        self.eng = eng
        self.pbase = dict(base)
        self.counts = dict(base)

    def on(self, name):
        return self.cur == name

    def bump(self, name, k):
        name = self.canon(name)
        self.h(name)
        self.counts[name] += k
        return self.counts[name]

    def val(self, name):
        name = self.canon(name)
        self.h(name)
        return self.counts[name]

    def base(self, name):
        return self.pbase.get(self.canon(name), 0)

    def dma(self, q, out, in_, sem, **kw):
        self.h(sem)
        if self.on(q):
            self.eng.dma_start(out=out, in_=in_, **kw).then_inc(self.sems[self.canon(sem)], 16)
        self.bump(sem, 16)

    def wait(self, engname, sem, val=None):
        self.h(sem)
        if self.on(engname):
            tgt = self.val(sem) if val is None else self.base(sem) + val
            self.eng.wait_ge(self.sems[self.canon(sem)], tgt)

    def op(self, engname, fn, sem=None, k=1):
        if sem is not None:
            self.h(sem)
        if self.on(engname):
            ins = fn(self.eng)
            if sem is not None:
                ins.then_inc(self.sems[self.canon(sem)], k)
        if sem is not None:
            self.bump(sem, k)

    def mark(self, engname, sem, k=1):
        self.h(sem)
        if self.on(engname):
            self.eng.sem_inc(self.sems[self.canon(sem)], k)
        self.bump(sem, k)

    def barrier(self, name):
        s = self.h(name)
        for _ in ENGINES:
            self.bump(name, 1)
        if self.cur is not None:
            self.eng.sem_inc(s, 1)
            self.eng.wait_ge(s, self.val(name))

    def poll_flag(self, slot, fvals_sb, spoll_sb, sh_flags, donesem):
        self.h(donesem)
        if self.on("gpsimd"):
            g = self.eng
            with g.register() as loopv, g.register() as tmp, g.register() as want:
                g.reg_load(want, fvals_sb[0:1, slot : slot + 1])
                g.reg_mov(loopv, 1)
                with g.While(loopv):
                    g.dma_start(
                        out=spoll_sb[0:1, 0:1], in_=sh_flags[0:1, slot : slot + 1]
                    ).then_inc(self.sems["pollsem"], 16, skip_validation=True)
                    g.reg_load(tmp, spoll_sb[0:1, 0:1])
                    g.reg_alu(tmp, tmp, want, op=ALU.is_equal)
                    g.reg_alu(loopv, tmp, 1, op=ALU.not_equal)
            g.sem_inc(self.sems[self.canon(donesem)], 1)
        self.bump(donesem, 1)

    def write_flag(self, slot_p0, slot_p1, fvals_sb, sh_flags, pid_sb=None):
        if self.on("gpsimd"):
            g = self.eng
            with g.register(self.rn("wfpr")) as pr:
                g.reg_load(pr, pid_sb[0:1, 0:1])
                g.reg_alu(pr, pr, 1, op=ALU.bitwise_and)
                with g.If_eq(pr, 0):
                    g.dma_start(
                        out=sh_flags[0:1, slot_p0 : slot_p0 + 1],
                        in_=fvals_sb[0:1, slot_p0 : slot_p0 + 1],
                    ).then_inc(self.sems["pollsem"], 16, skip_validation=True)
                with g.If_eq(pr, 1):
                    g.dma_start(
                        out=sh_flags[0:1, slot_p1 : slot_p1 + 1],
                        in_=fvals_sb[0:1, slot_p1 : slot_p1 + 1],
                    ).then_inc(self.sems["pollsem"], 16, skip_validation=True)

    def poll_flag_pair(self, ev2, fvals_sb, spoll_sb, sh_flags, donesem, pid_sb=None):
        self.h(donesem)
        if self.on("gpsimd"):
            g = self.eng
            with g.register(self.rn("pfp")) as pr:
                g.reg_load(pr, pid_sb[0:1, 0:1])
                g.reg_alu(pr, pr, 1, op=ALU.bitwise_and)
                for mypar, slot in ((0, ev2 + 1), (1, ev2)):
                    with g.If_eq(pr, mypar):
                        with (
                            g.register(self.rn("pfl")) as loopv,
                            g.register(self.rn("pft")) as tmp,
                            g.register(self.rn("pfw")) as want,
                        ):
                            g.reg_load(want, fvals_sb[0:1, slot : slot + 1])
                            g.reg_mov(loopv, 1)
                            with g.While(loopv):
                                for _ in range(4):
                                    g.dma_start(
                                        out=spoll_sb[0:1, 0:1],
                                        in_=sh_flags[0:1, slot : slot + 1],
                                    ).then_inc(self.sems["pollsem"], 16, skip_validation=True)
                                g.reg_load(tmp, spoll_sb[0:1, 0:1])
                                g.reg_alu(tmp, tmp, want, op=ALU.is_equal)
                                g.reg_alu(loopv, tmp, 1, op=ALU.not_equal)
            g.sem_inc(self.sems[self.canon(donesem)], 1)
        self.bump(donesem, 1)

    def ifpar_dma(self, q, out_p0, out_p1, in_, sem, pid_sb=None):
        self.h(sem)
        if self.on(q):
            e = self.eng
            with e.register(self.rn("ipr")) as pr:
                e.reg_load(pr, pid_sb[0:1, 0:1])
                e.reg_alu(pr, pr, 1, op=ALU.bitwise_and)
                with e.If_eq(pr, 0):
                    e.dma_start(out=out_p0, in_=in_).then_inc(self.sems[self.canon(sem)], 16)
                with e.If_eq(pr, 1):
                    e.dma_start(out=out_p1, in_=in_).then_inc(self.sems[self.canon(sem)], 16)
        self.bump(sem, 16)



def own_blocks(p):
    return [g for g in range(NB) if g % 2 == p]


def make_dmask(p):
    m = np.zeros((NB, 128, 256), np.float32)
    blocks = own_blocks(p)
    for kc in range(NB):
        js = js_of(kc)
        for jj in range(2):
            j = js + jj
            col = jj * 128
            if j >= 8:
                continue
            g = blocks[j]
            if g < kc:
                pass
            elif g == kc:
                rk = np.arange(128)[:, None]
                rq = np.arange(128)[None, :]
                m[kc, :, col : col + 128] = (rk <= rq).astype(np.float32)
            else:
                m[kc, :, col : col + 128] = 1.0
    return m.astype(ml_dtypes.bfloat16)



def build_program(phases, detect_races=True, poll_nop=True):
    nc = bass.Bass("TRN2", target_bir_lowering=False, debug=False, num_devices=8,
                   detect_race_conditions=detect_races)
    dt = {}

    def din(name, shape, dtype=F32):
        dt[name] = nc.dram_tensor(name, list(shape), dtype, kind="ExternalInput")
        return dt[name]

    def dout(name, shape, dtype=F32):
        dt[name] = nc.dram_tensor(name, list(shape), dtype, kind="ExternalOutput")
        return dt[name]

    need = set(phases)

    din("fvals", (1, NFLAG), I32)
    din("ident_f32", (128, 128), F32)
    din("ident_bf", (128, 128), BF16)
    din("ones_bf", (128, 1), BF16)
    din("onesrow_f32", (1, 128), F32)

    if "load_x" in need:
        din("x_own", (T, D))
    if "loadT" in need:
        din("xT_in", (D, T))
    if "storeT" in need:
        dout("xT_out", (D, T))
    if "qdump" in need:
        dt["qgu_dump"] = nc.dram_tensor("qgu_dump", [2 * HG, D], BF16, kind="ExternalOutput")
        dt["qpr_dump"] = nc.dram_tensor("qpr_dump", [D, HG], BF16, kind="ExternalOutput")
    if "store" in need:
        dout("y_own", (T, D))

    if "l0attn" in need or "l1attn" in need:
        din("dmask", (NB, 128, 256), BF16)
    if "l0attn" in need:
        din("L0_attn_W_Q", (H, D, DH))
        din("L0_attn_W_K", (H, D, DH))
        din("L0_attn_W_V", (H, D, DH))
        din("L0_attn_W_O", (H, DH, D))
        din("bqT", (128, H))
        din("bkT", (128, H))
        din("bvT", (128, H))
        din("boT", (128, NC))
    if "l1attn" in need:
        din("wqkvT", (D, 3 * D))
        din("woutT", (D, D))
        din("bqkvT", (128, 3 * H))
        din("boutT", (128, NC))
    if "quant" in need:
        din("qgu_my", (HG, D))
        din("qpr_my", (D // 2, HG))
    if "moe" in need or "moe_noln" in need:
        din("gate_wT", (D, NE))
        din("gate_b_col", (NE, 1))
        din("sel16", (NE, NE * 128))
        din("L1_ffn_w1_shared", (D, HM))
        din("L1_ffn_w2_shared", (HM, D))
        din("L1_ffn_w3_shared", (D, HM))
        din("L1_ffn_w1_routed", (NE, D, HM))
        din("L1_ffn_w2_routed", (NE, HM, D))
        din("L1_ffn_w3_routed", (NE, D, HM))

    sh_flags = nc.dram_tensor("sh_flags", [1, NFLAG + 8], I32, addr_space="Shared")
    kv = {}
    for layer in (0, 1):
        if ("l0attn" if layer == 0 else "l1attn") in need:
            kv[layer] = (
                nc.dram_tensor(f"kvK{layer}", [2, H, 128, T], BF16, addr_space="Shared"),
                nc.dram_tensor(f"kvV{layer}", [2, H, 8, 128, 128], BF16, addr_space="Shared"),
            )
    if "quant" in need or "ffn0" in need:
        qgu_hat = nc.dram_tensor("qgu_hat", [2 * HG, D], BF16, addr_space="Shared")
        qpr_hat = nc.dram_tensor("qpr_hat", [D, HG], BF16, addr_space="Shared")
    else:
        qgu_hat = qpr_hat = None

    cx = Ctx(nc)
    cx.poll_nop = poll_nop
    cx.h("pollsem")

    with (
        nc.sbuf_tensor("xr", [128, NC * T], F32) as xr,
        nc.sbuf_tensor("xb", [128, NC * T], BF16) as xb,
        nc.sbuf_tensor("fvals_sb", [1, NFLAG], I32) as fvals_sb,
        nc.sbuf_tensor("spoll", [1, 1], I32) as spoll,
        nc.sbuf_tensor("pid_sb", [1, 1], mybir.dt.uint32) as pid_sb,
        nc.sbuf_tensor("epsc", [1, 2], F32) as epsc,
        nc.sbuf_tensor("ident_f32_sb", [128, 128], F32) as idf,
        nc.sbuf_tensor("ident_bf_sb", [128, 128], BF16) as idb,
        nc.sbuf_tensor("ones_bf_sb", [128, 1], BF16) as onesb,
        nc.sbuf_tensor("onesrow_sb", [1, 128], F32) as onesr,
        nc.Block() as block,
    ):
        bufs = dict(
            xr=xr, xb=xb, fvals=fvals_sb, spoll=spoll, idf=idf, idb=idb,
            onesb=onesb, onesr=onesr, dt=dt, kv=kv, sh_flags=sh_flags,
            qgu_hat=qgu_hat, qpr_hat=qpr_hat, pid_sb=pid_sb, epsc=epsc,
        )

        def run_phase(emitfn, pbufs):
            base = cx.snapshot()
            for engname in ENGINES:
                eng = getattr(nc, engname)

                def seg(e, _emitfn=emitfn, _n=engname, _pb=pbufs):
                    cx.start_pass(_n, e, base)
                    _emitfn(cx, bufs, _pb)

                getattr(block, engname)(seg)

        def emit_prologue(cx, b, pb):
            cx.op("gpsimd", lambda g: g.memset(b["spoll"][:], 0), "clds0")
            cx.op("gpsimd", lambda g: g.memset(b["epsc"][0:1, 0:1], LN_EPS), "clds0")
            cx.op("gpsimd", lambda g: g.memset(b["epsc"][0:1, 1:2], RMS_EPS), "clds0")
            cx.dma("sync", b["pid_sb"][:], nc.partition_id_tensor.ap(), "clds")
            cx.dma("sync", b["fvals"][:], dt["fvals"].ap(), "clds")
            cx.dma("sync", b["idf"][:], dt["ident_f32"].ap(), "clds")
            cx.dma("sync", b["idb"][:], dt["ident_bf"].ap(), "clds")
            cx.dma("sync", b["onesb"][:], dt["ones_bf"].ap(), "clds")
            cx.dma("sync", b["onesr"][:], dt["onesrow_f32"].ap(), "clds")
            for e in ENGINES:
                cx.wait(e, "clds")
            cx.barrier("bar_prologue")

        run_phase(emit_prologue, {})

        for ph in phases:
            if ph == "load_x":
                with (
                    nc.sbuf_tensor("sx0", [128, D], F32) as sx0,
                    nc.sbuf_tensor("sx1", [128, D], F32) as sx1,
                    nc.psum_tensor("ptr0", [128, 128], F32) as ptr0,
                    nc.psum_tensor("ptr1", [128, 128], F32) as ptr1,
                    nc.psum_tensor("ptr2", [128, 128], F32) as ptr2,
                    nc.psum_tensor("ptr3", [128, 128], F32) as ptr3,
                ):
                    pb = dict(sx=[sx0, sx1], ptr=[ptr0, ptr1, ptr2, ptr3])
                    run_phase(emit_load_x, pb)
            elif ph == "loadT":
                run_phase(emit_loadT, {})
            elif ph == "storeT":
                run_phase(emit_storeT, {})
            elif ph == "store":
                with (
                    nc.sbuf_tensor("sy0", [128, D], F32) as sy0,
                    nc.sbuf_tensor("sy1", [128, D], F32) as sy1,
                    nc.psum_tensor("pso0", [128, 128], F32) as pso0,
                    nc.psum_tensor("pso1", [128, 128], F32) as pso1,
                    nc.psum_tensor("pso2", [128, 128], F32) as pso2,
                    nc.psum_tensor("pso3", [128, 128], F32) as pso3,
                ):
                    pb = dict(sy=[sy0, sy1], pso=[pso0, pso1, pso2, pso3])
                    run_phase(emit_store, pb)
            elif ph in ("l0attn", "l1attn"):
                layer = 0 if ph == "l0attn" else 1
                emit_attention_phase(nc, cx, block, bufs, run_phase, layer)
            elif ph == "quant":
                emit_quant_phase(nc, cx, block, bufs, run_phase)
            elif ph == "ffn0":
                emit_ffn0_phase(nc, cx, block, bufs, run_phase)
            elif ph == "moe":
                emit_moe_phase(nc, cx, block, bufs, run_phase)
            elif ph == "moe_noln":
                emit_moe_phase(nc, cx, block, bufs, run_phase, skip_ln=True)
            elif ph == "qdump":
                def emit_qd(cx, b, pb):
                    cx.wait("sync", "qrdy")
                    cx.dma("sync", dt["qgu_dump"].ap(), b["qgu_hat"].ap(), "final_store")
                    cx.dma("sync", dt["qpr_dump"].ap(), b["qpr_hat"].ap(), "final_store")
                    cx.wait("sync", "final_store")
                run_phase(emit_qd, {})
            else:
                raise ValueError(ph)

        def emit_epilogue(cx, b, pb):
            cx.barrier("bar_epilogue")

        run_phase(emit_epilogue, {})

    return nc



def emit_load_x(cx, b, pb):
    x_own = b["dt"]["x_own"]
    for j in range(8):
        sx = pb["sx"][j % 2]
        if j >= 2:
            cx.wait("sync", "p0done", val=16 * (j - 1))
        cx.dma("sync", sx[:], x_own[j * 128 : (j + 1) * 128, :], "p0ld")
        cx.wait("tensor", "p0ld")
        for dc in range(NC):
            pt = pb["ptr"][dc % 4]
            n = j * NC + dc
            if n >= 4:
                cx.wait("tensor", "p0ev", val=2 * (n - 3))
            cx.op("tensor", lambda t, _sx=sx, _pt=pt, _dc=dc: t.transpose(
                _pt[:], _sx[:, _dc * 128 : (_dc + 1) * 128], b["idf"][:]), "p0tr")
            cx.wait("vector", "p0tr")
            cx.op("vector", lambda v, _pt=pt, _dc=dc, _j=j: v.tensor_copy(
                b["xr"][:, _dc * T + _j * 128 : _dc * T + (_j + 1) * 128], _pt[:]), "p0ev")
            cx.op("vector", lambda v, _pt=pt, _dc=dc, _j=j: v.tensor_copy(
                b["xb"][:, _dc * T + _j * 128 : _dc * T + (_j + 1) * 128], _pt[:]), "p0ev")
        cx.wait("sync", "p0ev")
        cx.mark("sync", "p0done", 16)
    cx.barrier("bar_load_x")


def emit_loadT(cx, b, pb):
    xT = b["dt"]["xT_in"]
    cx.dma("sync", b["xr"].ap().rearrange("p (c t) -> p c t", t=T),
           xT.ap().rearrange("(c p) t -> p c t", p=128), "pTld")
    cx.wait("vector", "pTld")
    for dc in range(NC):
        cx.op("vector", lambda v, _dc=dc: v.tensor_copy(
            b["xb"][:, _dc * T : (_dc + 1) * T], b["xr"][:, _dc * T : (_dc + 1) * T]), "pTev")
    cx.barrier("bar_loadT")


def emit_storeT(cx, b, pb):
    cx.barrier("bar_storeT0")
    xT = b["dt"]["xT_out"]
    cx.dma("sync", xT.ap().rearrange("(c p) t -> p c t", p=128),
           b["xr"].ap().rearrange("p (c t) -> p c t", t=T), "final_store")
    cx.wait("sync", "final_store")


def emit_store(cx, b, pb):
    cx.barrier("bar_store0")
    y = b["dt"]["y_own"]
    for j in range(8):
        sy = pb["sy"][j % 2]
        if j >= 2:
            cx.wait("vector", "final_store", val=16 * j)
        for dc in range(NC):
            ps = pb["pso"][dc % 4]
            n = j * NC + dc
            if n >= 4:
                cx.wait("tensor", "p6ev", val=n - 3)
            cx.op("tensor", lambda t, _ps=ps, _dc=dc, _j=j: t.transpose(
                _ps[:], b["xr"][:, _dc * T + _j * 128 : _dc * T + (_j + 1) * 128],
                b["idf"][:]), "p6tr")
            cx.wait("vector", "p6tr")
            cx.op("vector", lambda v, _ps=ps, _sy=sy, _dc=dc: v.tensor_copy(
                _sy[:, _dc * 128 : (_dc + 1) * 128], _ps[:]), "p6ev")
        cx.wait("sync", "p6ev")
        cx.dma("sync", y[j * 128 : (j + 1) * 128, :], sy[:], "final_store")
    cx.wait("sync", "final_store")



def emit_attention_phase(nc, cx, block, b, run_phase, layer):
    dt = b["dt"]
    if layer == 0:
        def wap(part, h):
            t = dt[{"q": "L0_attn_W_Q", "k": "L0_attn_W_K", "v": "L0_attn_W_V"}[part]]
            return t.ap()[h].rearrange("(c p) e -> p c e", p=128)

        def wo_ap(h, dtg):
            return dt["L0_attn_W_O"].ap()[h][:, dtg * 512 : (dtg + 1) * 512]

        bias_col = {"q": ("bqT",), "k": ("bkT",), "v": ("bvT",)}
        bo_name = "boT"
        ev = EV_KV0
    else:
        def wap(part, h):
            sec = {"q": 0, "k": 1, "v": 2}[part]
            col = sec * D + h * 128
            return dt["wqkvT"].ap()[:, col : col + 128].rearrange("(c p) e -> p c e", p=128)

        def wo_ap(h, dtg):
            return dt["woutT"].ap()[h * 128 : (h + 1) * 128, dtg * 512 : (dtg + 1) * 512]

        bias_col = {"q": ("bqkvT", 0), "k": ("bqkvT", 1), "v": ("bqkvT", 2)}
        bo_name = "boutT"
        ev = EV_KV0 if layer == 0 else EV_KV1
    kvK, kvV = b["kv"][layer]
    sfx = f"_at{layer}"

    def bias_ap(part, h):
        if layer == 0:
            return dt_sb[{"q": "bq", "k": "bk", "v": "bv"}[part]][:, h : h + 1]
        sec = {"q": 0, "k": 1, "v": 2}[part]
        return dt_sb["bqkv"][:, sec * H + h : sec * H + h + 1]

    with (
        nc.sbuf_tensor(f"QT{sfx}", [128, H * T], BF16) as QT,
        nc.sbuf_tensor(f"biasq{sfx}", [128, 3 * H], F32) as biasq,
        nc.sbuf_tensor(f"biaso{sfx}", [128, NC], F32) as biaso,
        nc.sbuf_tensor(f"dmask{sfx}", [128, NB * 256], BF16) as dmask_sb,
    ):
        dt_sb = dict(bq=biasq[:, 0:H], bk=biasq[:, H : 2 * H], bv=biasq[:, 2 * H : 3 * H],
                     bqkv=biasq)

        def emit_A(cx, b, pb):
            wsl = pb["wsl"]
            ppj = pb["ppj"]
            pvt = pb["pvt"]
            KT, VT, Vtm = pb["KT"], pb["VT"], pb["Vtm"]
            if layer == 0:
                cx.dma("sync", biasq[:, 0:H], dt["bqT"].ap(), "bld" + sfx)
                cx.dma("sync", biasq[:, H : 2 * H], dt["bkT"].ap(), "bld" + sfx)
                cx.dma("sync", biasq[:, 2 * H : 3 * H], dt["bvT"].ap(), "bld" + sfx)
            else:
                cx.dma("sync", biasq[:], dt["bqkvT"].ap(), "bld" + sfx)
            cx.dma("sync", biaso[:], dt[bo_name].ap(), "bld" + sfx)
            cx.dma("sync", dmask_sb.ap().rearrange("p (k q) -> p k q", q=256),
                   dt["dmask"].ap().rearrange("k p q -> p k q"), "bld" + sfx)
            cx.wait("scalar", "bld" + sfx)

            npj = [0]
            for h in range(H):
                slot = wsl[h % 2]
                if h >= 2:
                    cx.wait("gpsimd", "pjmm" + sfx, val=3 * (h - 1))
                for part in ("q", "k", "v"):
                    cx.dma("gpsimd", slot[part].ap().rearrange("p (c e) -> p c e", e=128),
                           wap(part, h), f"wld{h % 2}" + sfx)
                for pi, part in enumerate(("q", "k", "v")):
                    pp = ppj[pi % 2]
                    idx = h * 3 + pi
                    if idx >= 2:
                        cx.wait("tensor", "pjev" + sfx, val=idx - 1)
                    cx.wait("tensor", f"wld{h % 2}" + sfx, val=48 * (h // 2 + 1))
                    for dc in range(NC):
                        for th in range(2):
                            last = dc == NC - 1 and th == 1
                            cx.op("tensor", lambda t, _pp=pp, _s=slot[part], _dc=dc, _th=th: t.matmul(
                                _pp[:, _th * 512 : (_th + 1) * 512],
                                _s[:, _dc * 128 : (_dc + 1) * 128],
                                b["xb"][:, _dc * T + _th * 512 : _dc * T + (_th + 1) * 512],
                                start=(_dc == 0), stop=(_dc == NC - 1)),
                                sem="pjmm" + sfx if last else None)
                    dest = {"q": QT[:, h * T : (h + 1) * T], "k": KT[:], "v": VT[:]}[part]
                    cx.wait("scalar", "pjmm" + sfx)
                    if part in ("k", "v") and h >= 1:
                        cx.wait("scalar", "kxwr" + sfx, val=32 * h)
                    cx.op("scalar", lambda s, _d=dest, _pp=pp, _pt=part, _h=h: s.activation(
                        _d, _pp[:], AF.Identity, bias=bias_ap(_pt, _h)), "pjev" + sfx)
                for j in range(8):
                    pv = pvt[j % 2]
                    if j >= 2:
                        cx.wait("tensor", "vtev" + sfx, val=8 * h + j - 1)
                    cx.wait("tensor", "pjev" + sfx)
                    cx.op("tensor", lambda t, _pv=pv, _j=j: t.transpose(
                        _pv[:], VT[:, _j * 128 : (_j + 1) * 128], b["idb"][:]), "vttr" + sfx)
                    cx.wait("scalar", "vttr" + sfx)
                    cx.op("scalar", lambda s, _pv=pv, _j=j: s.copy(
                        Vtm[:, _j * 128 : (_j + 1) * 128], _pv[:]), "vtev" + sfx)
                cx.wait("sync", "pjev" + sfx)
                cx.wait("sync", "vtev" + sfx)
                cx.ifpar_dma("sync", kvK[0, h], kvK[1, h], KT[:], "kxwr" + sfx, b["pid_sb"])
                cx.ifpar_dma("sync",
                             kvV[0, h].rearrange("li p r -> p li r"),
                             kvV[1, h].rearrange("li p r -> p li r"),
                             Vtm.ap().rearrange("p (j e) -> p j e", e=128), "kxwr" + sfx,
                             b["pid_sb"])
            cx.wait("gpsimd", "kxwr" + sfx)
            cx.write_flag(2 * ev, 2 * ev + 1, b["fvals"], b["sh_flags"], b["pid_sb"])
            cx.poll_flag_pair(2 * ev, b["fvals"], b["spoll"], b["sh_flags"], "kvrdy" + sfx, b["pid_sb"])
            cx.barrier("barA" + sfx)

        with (
            nc.sbuf_tensor(f"wq{sfx}0", [128, NC * 128], BF16) as w00,
            nc.sbuf_tensor(f"wk{sfx}0", [128, NC * 128], BF16) as w01,
            nc.sbuf_tensor(f"wv{sfx}0", [128, NC * 128], BF16) as w02,
            nc.sbuf_tensor(f"wq{sfx}1", [128, NC * 128], BF16) as w10,
            nc.sbuf_tensor(f"wk{sfx}1", [128, NC * 128], BF16) as w11,
            nc.sbuf_tensor(f"wv{sfx}1", [128, NC * 128], BF16) as w12,
            nc.sbuf_tensor(f"KT{sfx}", [128, T], BF16) as KT,
            nc.sbuf_tensor(f"VT{sfx}", [128, T], BF16) as VT,
            nc.sbuf_tensor(f"Vtm{sfx}", [128, T], BF16) as Vtm,
            nc.psum_tensor(f"ppj{sfx}0", [128, T], F32) as ppj0,
            nc.psum_tensor(f"ppj{sfx}1", [128, T], F32) as ppj1,
            nc.psum_tensor(f"pvt{sfx}0", [128, 128], BF16) as pvt0,
            nc.psum_tensor(f"pvt{sfx}1", [128, 128], BF16) as pvt1,
        ):
            pb = dict(
                wsl=[dict(q=w00, k=w01, v=w02), dict(q=w10, k=w11, v=w12)],
                ppj=[ppj0, ppj1], pvt=[pvt0, pvt1], KT=KT, VT=VT, Vtm=Vtm,
            )
            run_phase(emit_A, pb)

        with (
            nc.sbuf_tensor(f"zsb{sfx}", [128, H * T], BF16) as zsb,
            nc.sbuf_tensor(f"ktf{sfx}0", [128, S], BF16) as ktf0,
            nc.sbuf_tensor(f"ktf{sfx}1", [128, S], BF16) as ktf1,
            nc.sbuf_tensor(f"vtf{sfx}0", [128, S], BF16) as vtf0,
            nc.sbuf_tensor(f"vtf{sfx}1", [128, S], BF16) as vtf1,
        ):

            def emit_B(cx, b, pb):
                PT = pb["PT"]
                pst = pb["pst"]
                pzt = pb["pzt"]
                prs0, prs1 = pb["prs0"], pb["prs1"]
                rsv, rb = pb["rsv"], pb["rb"]
                ktf, vtf = [ktf0, ktf1], [vtf0, vtf1]
                nst = [0]
                cx.wait("sync", "kvrdy" + sfx)

                def emit_av(h, kc, g):
                    vb = vtf[h % 2]
                    a = js_of(kc) * 128
                    segs = [(a, 512), (512, 1024)] if a < 512 else [(a, 1024)]
                    cx.wait("tensor", "maskk" + sfx, val=g + 1)
                    for si, (s0, s1) in enumerate(segs):
                        stop_seg = (kc == 7) if s0 < 512 else (kc == NB - 1)
                        cx.op("tensor", lambda t, _vb=vb, _kc=kc, _s0=s0, _s1=s1, _st=stop_seg, _g=g: t.matmul(
                            pzt[:, _s0:_s1], _vb[:, _kc * 128 : (_kc + 1) * 128],
                            PT[_g % 2][:, _s0:_s1],
                            start=(_kc == 0), stop=_st))
                        prs = prs0 if s0 < 512 else prs1
                        ro = 0 if s0 < 512 else 512
                        last = si == len(segs) - 1
                        cx.op("tensor", lambda t, _prs=prs, _ro=ro, _s0=s0, _s1=s1, _kc=kc, _st=stop_seg, _g=g: t.matmul(
                            _prs[0:1, _s0 - _ro : _s1 - _ro], b["onesb"][:],
                            PT[_g % 2][:, _s0:_s1],
                            start=(_kc == 0), stop=_st),
                            sem="avk" + sfx if last else None, k=1)

                for h in range(H):
                    kb, vb = ktf[h % 2], vtf[h % 2]
                    if h >= 2:
                        cx.wait("sync", "bcmm" + sfx, val=h - 1)
                    for p_ in range(2):
                        cx.dma("sync",
                               kb.ap().rearrange("p (li half) -> p li half", half=256)[:, :, p_ * 128 : (p_ + 1) * 128],
                               kvK[p_, h].rearrange("p (li r) -> p li r", r=128),
                               "kvrd" + sfx)
                        cx.dma("sync",
                               vb.ap().rearrange("p (li half) -> p li half", half=256)[:, :, p_ * 128 : (p_ + 1) * 128],
                               kvV[p_, h].rearrange("li p r -> p li r"),
                               "kvrd" + sfx)
                    cx.wait("tensor", "kvrd" + sfx)
                    if h >= 1:
                        cx.wait("tensor", "ztev" + sfx, val=h)
                        cx.wait("tensor", "rn1" + sfx, val=2 * h)
                        cx.wait("tensor", "bcev" + sfx, val=h)
                    for kc in range(NB):
                        g = nst[0]
                        a = js_of(kc) * 128
                        segs = [(a, 512), (512, 1024)] if a < 512 else [(a, 1024)]
                        if g >= 2:
                            cx.wait("tensor", "expk" + sfx, val=g - 1)
                        for si, (s0, s1) in enumerate(segs):
                            cx.op("tensor", lambda t, _kb=kb, _kc=kc, _s0=s0, _s1=s1, _g=g: t.matmul(
                                pst[_g % 2][:, _s0:_s1], _kb[:, _kc * 128 : (_kc + 1) * 128],
                                QT[:, h * T + _s0 : h * T + _s1],
                                start=True, stop=True),
                                sem="stmm" + sfx if si == len(segs) - 1 else None)
                        cx.wait("scalar", "stmm" + sfx, val=g + 1)
                        if g >= 2:
                            cx.wait("scalar", "avk" + sfx, val=g - 1)
                        cx.op("scalar", lambda s, _a=a, _g=g: s.activation(
                            PT[_g % 2][:, _a:T], pst[_g % 2][:, _a:T], AF.Exp,
                            scale=INVSQ), "expk" + sfx)
                        cx.wait("vector", "expk" + sfx, val=g + 1)
                        mw = min(256, T - a)
                        cx.op("vector", lambda v, _a=a, _kc=kc, _mw=mw, _g=g: v.tensor_mul(
                            PT[_g % 2][:, _a : _a + _mw], PT[_g % 2][:, _a : _a + _mw],
                            dmask_sb[:, _kc * 256 : _kc * 256 + _mw]), "maskk" + sfx)
                        nst[0] += 1
                        if kc >= 1:
                            emit_av(h, kc - 1, g - 1)
                    emit_av(h, NB - 1, nst[0] - 1)
                    cx.wait("vector", "avk" + sfx, val=nst[0])
                    if h >= 1:
                        cx.wait("vector", "bcmm" + sfx, val=h)
                    cx.op("vector", lambda v: v.tensor_copy(rsv[0:1, 0:512], prs0[0:1, :]), "rnc" + sfx)
                    cx.op("vector", lambda v: v.tensor_copy(rsv[0:1, 512:1024], prs1[0:1, :]), "rnc" + sfx)
                    cx.wait("vector", "rnc" + sfx)
                    cx.op("vector", lambda v: v.reciprocal(rsv[0:1, T : 2 * T], rsv[0:1, 0:T]), "rn1" + sfx, k=2)
                    cx.wait("tensor", "rn1" + sfx)
                    cx.wait("tensor", "expk" + sfx, val=nst[0])
                    for th in range(2):
                        cx.op("tensor", lambda t, _th=th: t.matmul(
                            pst[0][:, _th * 512 : (_th + 1) * 512], b["onesr"][:],
                            rsv[0:1, T + _th * 512 : T + (_th + 1) * 512], start=True, stop=True),
                            sem="bcmm" + sfx if th == 1 else None)
                    cx.wait("scalar", "bcmm" + sfx)
                    cx.op("scalar", lambda s: s.copy(rb[:], pst[0][:]), "bcev" + sfx)
                    cx.wait("vector", "bcev" + sfx)
                    cx.op("vector", lambda v, _h=h: v.tensor_mul(
                        zsb[:, _h * T : (_h + 1) * T], pzt[:], rb[:]), "ztev" + sfx)
                cx.barrier("barB" + sfx)

            with (
                nc.sbuf_tensor(f"PT{sfx}0", [128, T], BF16) as PT0,
                nc.sbuf_tensor(f"PT{sfx}1", [128, T], BF16) as PT1,
                nc.sbuf_tensor(f"rsv{sfx}", [1, 2 * T], F32) as rsv,
                nc.sbuf_tensor(f"rb{sfx}", [128, T], F32) as rb,
                nc.psum_tensor(f"pst{sfx}0", [128, T], F32) as pst0,
                nc.psum_tensor(f"pst{sfx}1", [128, T], F32) as pst1,
                nc.psum_tensor(f"pzt{sfx}", [128, T], F32) as pzt,
                nc.psum_tensor(f"prs0{sfx}", [1, 512], F32) as prs0,
                nc.psum_tensor(f"prs1{sfx}", [1, 512], F32) as prs1,
            ):
                pb = dict(PT=[PT0, PT1], pst=[pst0, pst1], pzt=pzt, prs0=prs0,
                          prs1=prs1, rsv=rsv, rb=rb)
                run_phase(emit_B, pb)

            def emit_C(cx, b, pb):
                wos = pb["wos"]
                po = pb["po"]
                for dtg in range(4):
                    for h in range(H):
                        slot = wos[h % 2]
                        idx = dtg * H + h
                        if idx >= 2:
                            cx.wait("gpsimd", "pomm" + sfx, val=idx - 1)
                        cx.dma("gpsimd", slot[:], wo_ap(h, dtg), f"wold{idx % 2}" + sfx)
                        cx.wait("tensor", f"wold{idx % 2}" + sfx, val=16 * (idx // 2 + 1))
                        if h == 0:
                            cx.wait("tensor", "poev" + sfx, val=8 * dtg)
                        for i in range(4):
                            for th in range(2):
                                last = i == 3 and th == 1
                                cx.op("tensor", lambda t, _po=po[i], _s=slot, _i=i, _th=th, _h=h: t.matmul(
                                    _po[:, _th * 512 : (_th + 1) * 512],
                                    _s[:, _i * 128 : (_i + 1) * 128],
                                    zsb[:, _h * T + _th * 512 : _h * T + (_th + 1) * 512],
                                    start=(_h == 0), stop=(_h == H - 1)),
                                    sem="pomm" + sfx if last else None)
                    cx.wait("vector", "pomm" + sfx)
                    for i in range(4):
                        dti = dtg * 4 + i
                        for th in range(2):
                            cx.op("vector", lambda v, _po=po[i], _dti=dti, _th=th: v.scalar_tensor_tensor(
                                b["xr"][:, _dti * T + _th * 512 : _dti * T + (_th + 1) * 512],
                                _po[:, _th * 512 : (_th + 1) * 512],
                                biaso[:, _dti : _dti + 1],
                                b["xr"][:, _dti * T + _th * 512 : _dti * T + (_th + 1) * 512],
                                op0=ALU.add, op1=ALU.add), "poev" + sfx)
                cx.barrier("barC" + sfx)

            with (
                nc.sbuf_tensor(f"wo{sfx}0", [128, 512], BF16) as wos0,
                nc.sbuf_tensor(f"wo{sfx}1", [128, 512], BF16) as wos1,
                nc.psum_tensor(f"po{sfx}0", [128, T], F32) as po0,
                nc.psum_tensor(f"po{sfx}1", [128, T], F32) as po1,
                nc.psum_tensor(f"po{sfx}2", [128, T], F32) as po2,
                nc.psum_tensor(f"po{sfx}3", [128, T], F32) as po3,
            ):
                run_phase(emit_C, dict(wos=[wos0, wos1], po=[po0, po1, po2, po3]))

    emit_postln_phase(nc, cx, block, b, run_phase, "ln" + sfx)



def emit_postln_phase(nc, cx, block, b, run_phase, tag):
    def emit_D(cx, b, pb):
        xq, tmpf, strow, sbc = pb["xq"], pb["tmpf"], pb["strow"], pb["sbc"]
        ps1a, ps1b, ps2a, ps2b = pb["ps"]
        pbc1, pbc2 = pb["pbc1"], pb["pbc2"]
        xr, xb = b["xr"], b["xb"]
        for dc in range(NC):
            cx.op("vector", lambda v, _dc=dc: v.tensor_copy(
                xb[:, _dc * T : (_dc + 1) * T], xr[:, _dc * T : (_dc + 1) * T]), tag + "cb")
        cx.wait("scalar", tag + "cb")
        for dc in range(NC):
            cx.op("scalar", lambda s, _dc=dc: s.activation(
                xq[:, _dc * T : (_dc + 1) * T], xb[:, _dc * T : (_dc + 1) * T],
                AF.Square), tag + "sq")
        cx.wait("tensor", tag + "cb")
        cx.wait("tensor", tag + "sq")
        for dc in range(NC):
            for th in range(2):
                for src, ps in ((xb, ps1a if th == 0 else ps1b),
                                (xq, ps2a if th == 0 else ps2b)):
                    cx.op("tensor", lambda t, _src=src, _ps=ps, _dc=dc, _th=th: t.matmul(
                        _ps[0:1, :], b["onesb"][:],
                        _src[:, _dc * T + _th * 512 : _dc * T + (_th + 1) * 512],
                        start=(_dc == 0), stop=(_dc == NC - 1)),
                        sem=tag + "st" if dc == NC - 1 else None)
        s1 = lambda: strow[0:1, 0:T]
        s2 = lambda: strow[0:1, T : 2 * T]
        mean = lambda: strow[0:1, 2 * T : 3 * T]
        var = lambda: strow[0:1, 3 * T : 4 * T]
        rstd = lambda: strow[0:1, 4 * T : 5 * T]
        nm = lambda: strow[0:1, 5 * T : 6 * T]
        cx.wait("vector", tag + "st")
        def vsd(fn):
            cx.op("vector", fn, tag + "vc")
            cx.wait("vector", tag + "vc")
        vsd(lambda v: v.tensor_copy(s1()[:, 0:512], ps1a[0:1, :]))
        vsd(lambda v: v.tensor_copy(s1()[:, 512:1024], ps1b[0:1, :]))
        vsd(lambda v: v.tensor_copy(s2()[:, 0:512], ps2a[0:1, :]))
        vsd(lambda v: v.tensor_copy(s2()[:, 512:1024], ps2b[0:1, :]))
        vsd(lambda v: v.tensor_scalar_mul(mean(), s1(), 1.0 / D))
        vsd(lambda v: v.tensor_scalar_mul(s2(), s2(), 1.0 / D))
        vsd(lambda v: v.tensor_mul(var(), mean(), mean()))
        cx.op("vector", lambda v: v.tensor_sub(var(), s2(), var()), tag + "v1")
        cx.wait("scalar", tag + "v1")
        cx.op("scalar", lambda s: s.activation(var(), var(), AF.Ln, bias=b["epsc"][0:1, 0:1]), tag + "sc1")
        cx.wait("scalar", tag + "sc1")
        cx.op("scalar", lambda s: s.activation(rstd(), var(), AF.Exp, scale=-0.5), tag + "v2")
        cx.wait("vector", tag + "v2")
        cx.op("vector", lambda v: v.scalar_tensor_tensor(
            nm(), mean(), -1.0, rstd(), op0=ALU.mult, op1=ALU.mult), tag + "v3")
        cx.wait("tensor", tag + "v3")
        for th in range(2):
            cx.op("tensor", lambda t, _th=th: t.matmul(
                pbc1[:, _th * 512 : (_th + 1) * 512], b["onesr"][:],
                rstd()[:, _th * 512 : (_th + 1) * 512], start=True, stop=True))
            cx.op("tensor", lambda t, _th=th: t.matmul(
                pbc2[:, _th * 512 : (_th + 1) * 512], b["onesr"][:],
                nm()[:, _th * 512 : (_th + 1) * 512], start=True, stop=True),
                sem=tag + "bc" if th == 1 else None)
        cx.wait("scalar", tag + "bc")
        cx.op("scalar", lambda s: s.copy(sbc[:, 0:T], pbc1[:]))
        cx.op("scalar", lambda s: s.copy(sbc[:, T : 2 * T], pbc2[:]), tag + "bce")
        cx.wait("vector", tag + "bce")
        for dc in range(NC):
            cx.op("vector", lambda v, _dc=dc: v.tensor_mul(
                tmpf[:], xr[:, _dc * T : (_dc + 1) * T], sbc[:, 0:T]), tag + "vc")
            cx.wait("vector", tag + "vc")
            cx.op("vector", lambda v, _dc=dc: v.tensor_add(
                xr[:, _dc * T : (_dc + 1) * T], tmpf[:], sbc[:, T : 2 * T]), tag + "ap")
        for dc in range(NC):
            cx.wait("scalar", tag + "ap", val=dc + 1)
            cx.op("scalar", lambda s, _dc=dc: s.copy(
                xb[:, _dc * T : (_dc + 1) * T], xr[:, _dc * T : (_dc + 1) * T]), tag + "cb2")
        cx.barrier("bar" + tag)

    with (
        nc.sbuf_tensor(tag + "xq", [128, NC * T], BF16) as xq,
        nc.sbuf_tensor(tag + "tmpf", [128, T], F32) as tmpf,
        nc.sbuf_tensor(tag + "strow", [1, 8 * T], F32) as strow,
        nc.sbuf_tensor(tag + "sbc", [128, 2 * T], F32) as sbc,
        nc.psum_tensor(tag + "ps1a", [1, 512], F32) as ps1a,
        nc.psum_tensor(tag + "ps1b", [1, 512], F32) as ps1b,
        nc.psum_tensor(tag + "ps2a", [1, 512], F32) as ps2a,
        nc.psum_tensor(tag + "ps2b", [1, 512], F32) as ps2b,
        nc.psum_tensor(tag + "pbc1", [128, T], F32) as pbc1,
        nc.psum_tensor(tag + "pbc2", [128, T], F32) as pbc2,
    ):
        run_phase(emit_D, dict(xq=xq, tmpf=tmpf, strow=strow, sbc=sbc,
                               ps=[ps1a, ps1b, ps2a, ps2b], pbc1=pbc1, pbc2=pbc2))



def emit_quant_phase(nc, cx, block, b, run_phase):
    dt = b["dt"]
    QW = 4096

    def emit_Q(cx, b, pb):
        wbf, wh = pb["wbf"], pb["wh"]
        gsf, scb, rsb = pb["gsf"], pb["scb"], pb["rsb"]
        ybf, t1, t2 = pb["ybf"], pb["t1"], pb["t2"]
        jobs = []
        for si in range(64):
            src = dt["qgu_my"].ap()[si * 128 : (si + 1) * 128, :]
            o0 = b["qgu_hat"][si * 128 : (si + 1) * 128, :]
            o1 = b["qgu_hat"][HG + si * 128 : HG + (si + 1) * 128, :]
            jobs.append((src, D, o0, o1))
        for si in range(8):
            for hw in range(2):
                c0, c1 = hw * QW, (hw + 1) * QW
                src = dt["qpr_my"].ap()[si * 128 : (si + 1) * 128, c0:c1]
                o0 = b["qpr_hat"][si * 128 : (si + 1) * 128, c0:c1]
                o1 = b["qpr_hat"][1024 + si * 128 : 1024 + (si + 1) * 128, c0:c1]
                jobs.append((src, QW, o0, o1))
        for ji, (src, W, o0, o1) in enumerate(jobs):
            sl = ji % 2
            G = W // 64
            if ji >= 2:
                cx.wait("gpsimd", "qv1", val=ji - 1)
            cx.dma("gpsimd", wbf[sl][:, 0:W], src, "qld")
            cx.wait("vector", "qld")
            if ji >= 2:
                cx.wait("vector", f"qwr{ji % 2}", val=16 * (ji // 2))
            wv = wbf[sl].ap()[:, 0:W].rearrange("p (g k) -> p g k", k=64)
            scbc = scb.ap()[:, 0:G].unsqueeze(2).broadcast_to([128, G, 64])
            rsbc = rsb.ap()[:, 0:G].unsqueeze(2).broadcast_to([128, G, 64])
            yv = ybf.ap()[:, 0:W].rearrange("p (g k) -> p g k", k=64)
            cx.op("vector", lambda v, _wv=wv, _G=G: v.reduce_sum(
                gsf[:, 0:_G], _wv, axis=AX.X, apply_absolute_value=True), "qvc")
            cx.wait("vector", "qvc")
            cx.op("vector", lambda v, _G=G: v.tensor_scalar(
                scb[:, 0:_G], gsf[:, 0:_G], 1.0 / 64.0, 1e-8,
                op0=ALU.mult, op1=ALU.max), "qvc")
            cx.wait("vector", "qvc")
            cx.op("vector", lambda v, _G=G: v.reciprocal(rsb[:, 0:_G], scb[:, 0:_G]), "qvc")
            cx.wait("vector", "qvc")
            cx.op("vector", lambda v, _wv=wv, _y=yv, _r=rsbc: v.tensor_tensor(
                _y, _wv, _r, op=ALU.mult), "qv1")
            cx.wait("vector", "qv1")
            cx.op("vector", lambda v, _W=W: v.tensor_single_scalar(
                t1[:, 0:_W], ybf[:, 0:_W], 0.5, op=ALU.is_gt), "qvc")
            cx.op("vector", lambda v, _W=W: v.tensor_single_scalar(
                t2[:, 0:_W], ybf[:, 0:_W], -0.5, op=ALU.is_lt), "qvc")
            cx.wait("vector", "qvc")
            cx.op("vector", lambda v, _W=W: v.tensor_sub(
                t1[:, 0:_W], t1[:, 0:_W], t2[:, 0:_W]), "qvc")
            cx.wait("vector", "qvc")
            cx.op("vector", lambda v, _sl=sl, _W=W, _s=scbc: v.tensor_tensor(
                wh[_sl].ap()[:, 0:_W].rearrange("p (g k) -> p g k", k=64),
                t1.ap()[:, 0:_W].rearrange("p (g k) -> p g k", k=64),
                _s, op=ALU.mult), "qv2")
            cx.wait("vector", "qv2")
            cx.wait("sync", "qv2")
            cx.ifpar_dma("sync", o0, o1, wh[sl][:, 0:W], f"qwr{ji % 2}", b["pid_sb"])
        cx.wait("gpsimd", "qwr0")
        cx.wait("gpsimd", "qwr1")
        cx.write_flag(2 * EV_QNT, 2 * EV_QNT + 1, b["fvals"], b["sh_flags"], b["pid_sb"])
        cx.poll_flag_pair(2 * EV_QNT, b["fvals"], b["spoll"], b["sh_flags"], "qrdy", b["pid_sb"])
        cx.barrier("barQ")

    with (
        nc.sbuf_tensor("qwbf0", [128, QW], BF16) as wbf0,
        nc.sbuf_tensor("qwbf1", [128, QW], BF16) as wbf1,
        nc.sbuf_tensor("qwh0", [128, QW], BF16) as wh0,
        nc.sbuf_tensor("qwh1", [128, QW], BF16) as wh1,
        nc.sbuf_tensor("qgsf", [128, QW // 64], F32) as gsf,
        nc.sbuf_tensor("qscb", [128, QW // 64], BF16) as scb,
        nc.sbuf_tensor("qrsb", [128, QW // 64], F32) as rsb,
        nc.sbuf_tensor("qybf", [128, QW], BF16) as ybf,
        nc.sbuf_tensor("qt1", [128, QW], BF16) as t1,
        nc.sbuf_tensor("qt2", [128, QW], BF16) as t2,
    ):
        run_phase(emit_Q, dict(wbf=[wbf0, wbf1], wh=[wh0, wh1], gsf=gsf,
                               scb=scb, rsb=rsb, ybf=ybf, t1=t1, t2=t2))



def emit_ffn0_phase(nc, cx, block, b, run_phase):
    BR = 512
    NGB = HG // BR
    GPB = BR // 128

    def emit_F(cx, b, pb):
        gsl, usl, psl = pb["gsl"], pb["usl"], pb["psl"]
        hbf, acc, sgf = pb["hbf"], pb["acc"], pb["sgf"]
        fsq = pb["fsq"]
        strow, sbc = pb["strow"], pb["sbc"]
        pg, pu, pss, pw = pb["pg"], pb["pu"], pb["pss"], pb["pw"]
        pbc1 = pb["pbc1"]
        xr, xb = b["xr"], b["xb"]
        cx.wait("sync", "qrdy")
        ngt = [0]
        nacc = [0]
        for th in range(2):
            to = th * 512
            for gb in range(NGB):
                if th > 0 or gb > 0:
                    cx.wait("sync", "fgu_mm", val=ngt[0])
                for dc in range(NC):
                    cx.dma("sync", gsl[:, dc * BR : (dc + 1) * BR],
                           b["qgu_hat"][gb * BR : (gb + 1) * BR, dc * 128 : (dc + 1) * 128],
                           "fld", transpose=True)
                    cx.dma("sync", usl[:, dc * BR : (dc + 1) * BR],
                           b["qgu_hat"][HG + gb * BR : HG + (gb + 1) * BR, dc * 128 : (dc + 1) * 128],
                           "fld", transpose=True)
                cx.wait("tensor", "fld")
                for gt in range(GPB):
                    if ngt[0] >= 1:
                        cx.wait("tensor", "fgu_ev", val=ngt[0])
                    for dc in range(NC):
                        cx.op("tensor", lambda t, _gt=gt, _dc=dc, _to=to: t.matmul(
                            pg[:], gsl[:, _dc * BR + _gt * 128 : _dc * BR + (_gt + 1) * 128],
                            xb[:, _dc * T + _to : _dc * T + _to + 512],
                            start=(_dc == 0), stop=(_dc == NC - 1)))
                        cx.op("tensor", lambda t, _gt=gt, _dc=dc, _to=to: t.matmul(
                            pu[:], usl[:, _dc * BR + _gt * 128 : _dc * BR + (_gt + 1) * 128],
                            xb[:, _dc * T + _to : _dc * T + _to + 512],
                            start=(_dc == 0), stop=(_dc == NC - 1)),
                            sem="fgu_mm" if dc == NC - 1 else None)
                    cx.wait("scalar", "fgu_mm")
                    cx.op("scalar", lambda s: s.activation(sgf[:, 0:512], pg[:], AF.Silu), "fsg")
                    cx.wait("vector", "fsg")
                    cx.op("vector", lambda v, _gt=gt: v.tensor_mul(
                        hbf[:, _gt * 512 : (_gt + 1) * 512], sgf[:, 0:512], pu[:]), "fgu_ev")
                    ngt[0] += 1
                    cx.wait("scalar", "fgu_ev", val=ngt[0])
                    if ngt[0] >= 2:
                        cx.wait("scalar", "fssm", val=ngt[0] - 1)
                    cx.op("scalar", lambda s, _gt=gt: s.activation(
                        fsq[:], hbf[:, _gt * 512 : (_gt + 1) * 512], AF.Square), "fsqd")
                    cx.wait("tensor", "fsqd")
                    cx.op("tensor", lambda t, _gt=gt, _gb=gb, _th=th: t.matmul(
                        pss[0:1, :], b["onesb"][:], fsq[:],
                        start=(_gb == 0 and _gt == 0), stop=(_gb == NGB - 1 and _gt == GPB - 1)),
                        sem="fssm")
                if th > 0 or gb > 0:
                    cx.wait("sync", "fpj_mm", val=(th * NGB + gb) * NC)
                for hc in range(GPB):
                    cx.dma("sync", psl[:, hc * D : (hc + 1) * D],
                           b["qpr_hat"][:, (gb * GPB + hc) * 128 : (gb * GPB + hc + 1) * 128],
                           "fpld", transpose=True)
                cx.wait("tensor", "fpld")
                for dt_ in range(NC):
                    pwc = pw[dt_ % 2]
                    if nacc[0] >= 2:
                        cx.wait("tensor", "facc", val=nacc[0] - 1)
                    for hc in range(GPB):
                        cx.op("tensor", lambda t, _p=pwc, _dt=dt_, _hc=hc: t.matmul(
                            _p[:], psl[:, _hc * D + _dt * 128 : _hc * D + (_dt + 1) * 128],
                            hbf[:, _hc * 512 : (_hc + 1) * 512],
                            start=(_hc == 0), stop=(_hc == GPB - 1)),
                            sem="fpj_mm" if hc == GPB - 1 else None)
                    cx.wait("vector", "fpj_mm")
                    if gb == 0:
                        cx.op("vector", lambda v, _pw=pwc, _dt=dt_: v.tensor_copy(
                            acc[:, _dt * 512 : (_dt + 1) * 512], _pw[:]), "facc")
                    else:
                        cx.op("vector", lambda v, _pw=pwc, _dt=dt_: v.tensor_add(
                            acc[:, _dt * 512 : (_dt + 1) * 512],
                            acc[:, _dt * 512 : (_dt + 1) * 512], _pw[:]), "facc")
                    nacc[0] += 1
            cx.wait("vector", "facc")
            mrow = lambda: strow[0:1, 0:512]
            rrow = lambda: strow[0:1, 512:1024]
            cx.op("vector", lambda v: v.tensor_copy(mrow(), pss[0:1, :]), "frmc")
            cx.wait("vector", "frmc")
            cx.op("vector", lambda v: v.tensor_scalar_mul(mrow(), mrow(), 1.0 / HG), "frm1")
            cx.wait("scalar", "frm1")
            cx.op("scalar", lambda s: s.activation(mrow(), mrow(), AF.Ln, bias=b["epsc"][0:1, 1:2]), "frmd")
            cx.wait("scalar", "frmd")
            cx.op("scalar", lambda s: s.activation(rrow(), mrow(), AF.Exp, scale=-0.5), "frm2")
            cx.wait("tensor", "frm2")
            cx.op("tensor", lambda t: t.matmul(
                pbc1[:], b["onesr"][:], rrow(), start=True, stop=True), "frbc")
            cx.wait("scalar", "frbc")
            cx.op("scalar", lambda s: s.copy(sbc[:, 0:512], pbc1[:]), "frbe")
            cx.wait("vector", "frbe")
            for dt_ in range(NC):
                cx.op("vector", lambda v, _dt=dt_: v.tensor_mul(
                    sgf[:, 0:512], acc[:, _dt * 512 : (_dt + 1) * 512], sbc[:, 0:512]), "frmc")
                cx.wait("vector", "frmc")
                cx.op("vector", lambda v, _dt=dt_, _to=to: v.tensor_add(
                    xr[:, _dt * T + _to : _dt * T + _to + 512],
                    xr[:, _dt * T + _to : _dt * T + _to + 512], sgf[:, 0:512]), "fres")
            cx.wait("tensor", "fres")
        cx.barrier("barF")

    with contextlib.ExitStack() as st_:
        A = st_.enter_context
        gsl = A(nc.sbuf_tensor("fgsl", [128, NC * BR], BF16))
        usl = A(nc.sbuf_tensor("fusl", [128, NC * BR], BF16))
        psl = A(nc.sbuf_tensor("fpsl", [128, GPB * D], BF16))
        hbf = A(nc.sbuf_tensor("fhbf", [128, GPB * 512], BF16))
        acc = A(nc.sbuf_tensor("faccb", [128, NC * 512], F32))
        sgf = A(nc.sbuf_tensor("fsgf", [128, 512], F32))
        fsq = A(nc.sbuf_tensor("ffsq", [128, 512], BF16))
        strow = A(nc.sbuf_tensor("fstrow", [1, 2 * T], F32))
        sbc = A(nc.sbuf_tensor("fsbc", [128, 512], F32))
        pg = A(nc.psum_tensor("fpg", [128, 512], F32))
        pu = A(nc.psum_tensor("fpu", [128, 512], F32))
        pss = A(nc.psum_tensor("fpss", [1, 512], F32))
        pw0 = A(nc.psum_tensor("fpw0", [128, 512], F32))
        pw1 = A(nc.psum_tensor("fpw1", [128, 512], F32))
        pbc1 = A(nc.psum_tensor("fpbc1", [128, 512], F32))
        run_phase(emit_F, dict(gsl=gsl, usl=usl, psl=psl, hbf=hbf, acc=acc,
                               sgf=sgf, fsq=fsq, strow=strow, sbc=sbc, pg=pg, pu=pu,
                               pss=pss, pw=[pw0, pw1], pbc1=pbc1))
    emit_postln_phase(nc, cx, block, b, run_phase, "lnF")



def emit_moe_phase(nc, cx, block, b, run_phase, skip_ln=False):
    dt = b["dt"]

    def emit_R(cx, b, pb):
        gwt, scT, sc, pT = pb["gwt"], pb["scT"], pb["sc"], pb["pT"]
        gbc, sel = pb["gbc"], pb["sel"]
        work, tmp16, mx, msk, wsr = pb["work"], pb["tmp16"], pb["mx"], pb["msk"], pb["wsr"]
        pr, ptp = pb["pr"], pb["ptp"]
        cx.dma("sync", gwt.ap().rearrange("p (c e) -> p c e", e=NE),
               dt["gate_wT"].ap().rearrange("(c p) e -> p c e", p=128), "rld")
        cx.dma("sync", gbc[:], dt["gate_b_col"].ap(), "rld")
        cx.dma("sync", sel[:], dt["sel16"].ap(), "rld")
        cx.wait("tensor", "rld")
        for dc in range(NC):
            for th in range(2):
                cx.op("tensor", lambda t, _dc=dc, _th=th: t.matmul(
                    pr[:, _th * 512 : (_th + 1) * 512],
                    gwt[:, _dc * NE : (_dc + 1) * NE],
                    b["xr"][:, _dc * T + _th * 512 : _dc * T + (_th + 1) * 512],
                    start=(_dc == 0), stop=(_dc == NC - 1)),
                    sem="rmm" if (dc == NC - 1 and th == 1) else None)
        cx.wait("scalar", "rmm")
        cx.op("scalar", lambda s: s.activation(scT[:], pr[0:NE, :], AF.Sigmoid), "rsg")
        cx.wait("vector", "rsg")
        cx.op("vector", lambda v: v.tensor_scalar_add(scT[:], scT[:], gbc[:]), "rsb")
        cx.wait("tensor", "rsb")
        for j in range(8):
            cx.op("tensor", lambda t, _j=j: t.transpose(
                ptp[:, 0:NE], scT[:, _j * 128 : (_j + 1) * 128], b["idf"][0:NE, 0:NE]),
                "rtp")
            cx.wait("vector", "rtp")
            cx.op("vector", lambda v, _j=j: v.tensor_copy(
                sc[:, _j * NE : (_j + 1) * NE], ptp[:, 0:NE]), "rtc")
            cx.wait("tensor", "rtc")
        def vs(fn):
            cx.op("vector", fn, "rvc")
            cx.wait("vector", "rvc")
        for j in range(8):
            scj = sc[:, j * NE : (j + 1) * NE]
            vs(lambda v, _s=scj: v.tensor_copy(work[:], _s))
            for r in range(TOPK):
                vs(lambda v, _r=r: v.reduce_max(
                    mx[:, _r : _r + 1], work[:], axis=AX.X))
                if r < TOPK - 1:
                    vs(lambda v, _r=r: v.tensor_scalar(
                        tmp16[:], work[:], mx[:, _r : _r + 1], -1e30,
                        op0=ALU.is_equal, op1=ALU.mult))
                    vs(lambda v: v.tensor_add(work[:], work[:], tmp16[:]))
            vs(lambda v, _s=scj: v.scalar_tensor_tensor(
                msk[:], _s, mx[:, TOPK - 1 : TOPK], _s, op0=ALU.is_ge, op1=ALU.mult))
            vs(lambda v: v.reduce_sum(wsr[:, 0:1], msk[:], axis=AX.X))
            vs(lambda v: v.reciprocal(wsr[:, 1:2], wsr[:, 0:1]))
            cx.op("vector", lambda v, _j=j: v.tensor_scalar_mul(
                sc[:, _j * NE : (_j + 1) * NE], msk[:], wsr[:, 1:2]), "rpj")
            cx.wait("vector", "rpj")
        cx.wait("tensor", "rpj")
        for j in range(8):
            if j >= 1:
                cx.wait("tensor", "rpe", val=j)
            cx.op("tensor", lambda t, _j=j: t.transpose(
                ptp[0:NE, 0:128],
                sc[:, _j * NE : (_j + 1) * NE], b["idf"][:]), "rtb")
            cx.wait("scalar", "rtb")
            cx.op("scalar", lambda s, _j=j: s.copy(
                pT[:, _j * 128 : (_j + 1) * 128], ptp[0:NE, 0:128]), "rpe")
        cx.barrier("barR")

    with (
        nc.sbuf_tensor("mgwt", [128, NC * NE], F32) as gwt,
        nc.sbuf_tensor("mscT", [NE, T], F32) as scT,
        nc.sbuf_tensor("msc", [128, 8 * NE], F32) as sc,
        nc.sbuf_tensor("mpT", [NE, 8 * 128], F32) as pT_,
        nc.sbuf_tensor("mgbc", [NE, 1], F32) as gbc,
        nc.sbuf_tensor("msel", [NE, NE * 128], F32) as sel,
        nc.sbuf_tensor("mwork", [128, NE], F32) as work,
        nc.sbuf_tensor("mtmp16", [128, NE], F32) as tmp16,
        nc.sbuf_tensor("mmx", [128, TOPK], F32) as mx,
        nc.sbuf_tensor("mmsk", [128, NE], F32) as msk,
        nc.sbuf_tensor("mwsr", [128, 2], F32) as wsr,
    ):
        with contextlib.ExitStack() as rst_:
            pr = rst_.enter_context(nc.psum_tensor("mpr", [NE, T], F32))
            ptp = rst_.enter_context(nc.psum_tensor("mptp", [128, 128], F32))
            run_phase(emit_R, dict(gwt=gwt, scT=scT, sc=sc, pT=pT_, gbc=gbc, sel=sel,
                                   work=work, tmp16=tmp16, mx=mx, msk=msk, wsr=wsr,
                                   pr=pr, ptp=ptp))

        def emit_E(cx, b, pb):
            w1s, w3s, w2s = pb["w1s"], pb["w3s"], pb["w2s"]
            h3s, s1f, peb = pb["h3s"], pb["s1f"], pb["peb"]
            pbe, ph1, ph2, pw = pb["pbe"], pb["ph1"], pb["ph2"], pb["pw"]
            xr, xb = b["xr"], b["xb"]
            nht = [0]
            nw2 = [0]
            for e in range(NE + 1):
                routed = e < NE
                if routed:
                    w1t, w3t = dt["L1_ffn_w1_routed"].ap()[e], dt["L1_ffn_w3_routed"].ap()[e]
                    w2t = dt["L1_ffn_w2_routed"].ap()[e]
                else:
                    w1t, w3t = dt["L1_ffn_w1_shared"].ap(), dt["L1_ffn_w3_shared"].ap()
                    w2t = dt["L1_ffn_w2_shared"].ap()
                if routed:
                    if e >= 1:
                        cx.wait("tensor", "mbev", val=e)
                    for th in range(2):
                        cx.op("tensor", lambda t, _e=e, _th=th: t.matmul(
                            pbe[:, _th * 512 : (_th + 1) * 512],
                            sel[:, _e * 128 : (_e + 1) * 128],
                            pT_[:, _th * 512 : (_th + 1) * 512],
                            start=True, stop=True),
                            sem="mbe" if th == 1 else None)
                    cx.wait("scalar", "mbe")
                    if e >= 1:
                        cx.wait("scalar", "mh3", val=e * HMT)
                    cx.op("scalar", lambda s: s.copy(peb[:], pbe[:]), "mbev")
                if e >= 1:
                    cx.wait("gpsimd", "mw2mm", val=nw2[0])
                for hc in range(HMT):
                    cx.dma("gpsimd", w2s[:, hc * D : (hc + 1) * D],
                           w2t[hc * 128 : (hc + 1) * 128, :], "mw2ld")
                for ht in range(HMT):
                    sl = ht % 2
                    idx = e * HMT + ht
                    if idx >= 2:
                        cx.wait("gpsimd", "mhmm", val=2 * (idx - 1))
                    cx.dma("gpsimd", w1s[sl].ap().rearrange("p (c e) -> p c e", e=128),
                           w1t[:, ht * 128 : (ht + 1) * 128].rearrange("(c p) e -> p c e", p=128),
                           "mwld")
                    cx.dma("gpsimd", w3s[sl].ap().rearrange("p (c e) -> p c e", e=128),
                           w3t[:, ht * 128 : (ht + 1) * 128].rearrange("(c p) e -> p c e", p=128),
                           "mwld")
                    cx.wait("tensor", "mwld")
                    if idx >= 1:
                        cx.wait("tensor", "mh3", val=nht[0])
                    if routed and ht == 0:
                        cx.wait("vector", "mbev", val=e + 1)
                    for dc in range(NC):
                        for th in range(2):
                            cx.op("tensor", lambda t, _s=w1s[sl], _dc=dc, _th=th: t.matmul(
                                ph1[:, _th * 512 : (_th + 1) * 512],
                                _s[:, _dc * 128 : (_dc + 1) * 128],
                                xb[:, _dc * T + _th * 512 : _dc * T + (_th + 1) * 512],
                                start=(_dc == 0), stop=(_dc == NC - 1)))
                            cx.op("tensor", lambda t, _s=w3s[sl], _dc=dc, _th=th: t.matmul(
                                ph2[:, _th * 512 : (_th + 1) * 512],
                                _s[:, _dc * 128 : (_dc + 1) * 128],
                                xb[:, _dc * T + _th * 512 : _dc * T + (_th + 1) * 512],
                                start=(_dc == 0), stop=(_dc == NC - 1)),
                                sem="mhmm" if (dc == NC - 1 and th == 1) else None, k=2)
                    cx.wait("scalar", "mhmm")
                    if routed:
                        cx.op("scalar", lambda s: s.copy(s1f[:], ph1[:]), "mcv1")
                        cx.wait("vector", "mcv1")
                        cx.op("vector", lambda v: v.tensor_mul(s1f[:], s1f[:], ph2[:]), "mcv2")
                        cx.wait("scalar", "mcv2")
                        cx.op("scalar", lambda s: s.activation(s1f[:], s1f[:], AF.Silu), "mcv3")
                        cx.wait("vector", "mcv3")
                        cx.op("vector", lambda v, _ht=ht: v.tensor_mul(
                            h3s[:, _ht * T : (_ht + 1) * T], s1f[:], peb[:]), "mh3")
                    else:
                        cx.op("scalar", lambda s: s.activation(s1f[:], ph1[:], AF.Silu), "mcv1")
                        cx.mark("vector", "mcv2")
                        cx.mark("scalar", "mcv3")
                        cx.wait("vector", "mcv3")
                        cx.op("vector", lambda v, _ht=ht: v.tensor_mul(
                            h3s[:, _ht * T : (_ht + 1) * T], s1f[:], ph2[:]), "mh3")
                    nht[0] += 1
                cx.wait("tensor", "mw2ld")
                cx.wait("tensor", "mh3", val=nht[0])
                for dt_ in range(NC):
                    for th in range(2):
                        pwc = pw[(dt_ * 2 + th) % 2]
                        if nw2[0] >= 2:
                            cx.wait("tensor", "mxw", val=nw2[0] - 1)
                        for hc in range(HMT):
                            cx.op("tensor", lambda t, _p=pwc, _dt=dt_, _th=th, _hc=hc: t.matmul(
                                _p[:], w2s[:, _hc * D + _dt * 128 : _hc * D + (_dt + 1) * 128],
                                h3s[:, _hc * T + _th * 512 : _hc * T + (_th + 1) * 512],
                                start=(_hc == 0), stop=(_hc == HMT - 1)),
                                sem="mw2mm" if hc == HMT - 1 else None)
                        cx.wait("vector", "mw2mm")
                        cx.op("vector", lambda v, _p=pwc, _dt=dt_, _th=th: v.tensor_add(
                            xr[:, _dt * T + _th * 512 : _dt * T + (_th + 1) * 512],
                            xr[:, _dt * T + _th * 512 : _dt * T + (_th + 1) * 512],
                            _p[:]), "mxw")
                        nw2[0] += 1
            cx.barrier("barE")

        with contextlib.ExitStack() as st_:
            A = st_.enter_context
            w1s0 = A(nc.sbuf_tensor("mw1s0", [128, NC * 128], BF16))
            w1s1 = A(nc.sbuf_tensor("mw1s1", [128, NC * 128], BF16))
            w3s0 = A(nc.sbuf_tensor("mw3s0", [128, NC * 128], BF16))
            w3s1 = A(nc.sbuf_tensor("mw3s1", [128, NC * 128], BF16))
            w2s = A(nc.sbuf_tensor("mw2s", [128, HMT * D], BF16))
            h3s = A(nc.sbuf_tensor("mh3s", [128, HMT * T], BF16))
            s1f = A(nc.sbuf_tensor("ms1f", [128, T], F32))
            peb = A(nc.sbuf_tensor("mpeb", [128, T], F32))
            pbe = A(nc.psum_tensor("mpbe", [128, T], F32))
            ph1 = A(nc.psum_tensor("mph1", [128, T], F32))
            ph2 = A(nc.psum_tensor("mph2", [128, T], F32))
            pw0 = A(nc.psum_tensor("mpw0", [128, 512], F32))
            pw1 = A(nc.psum_tensor("mpw1", [128, 512], F32))
            run_phase(emit_E, dict(w1s=[w1s0, w1s1], w3s=[w3s0, w3s1], w2s=w2s,
                                   h3s=h3s, s1f=s1f, peb=peb,
                                   pbe=pbe, ph1=ph1, ph2=ph2, pw=[pw0, pw1]))
    if not skip_ln:
        emit_postln_phase(nc, cx, block, b, run_phase, "lnM")



def own_rows(p):
    return np.concatenate([np.arange(g * 128, (g + 1) * 128) for g in own_blocks(p)])


def _consts(p, nonce):
    d = {}
    d["fvals"] = (nonce + np.arange(NFLAG, dtype=np.int64) % (2**31 - 1)).astype(np.int32).reshape(1, NFLAG)
    d["ident_f32"] = np.eye(128, dtype=np.float32)
    d["ident_bf"] = np.eye(128, dtype=ml_dtypes.bfloat16)
    d["ones_bf"] = np.ones((128, 1), dtype=ml_dtypes.bfloat16)
    d["onesrow_f32"] = np.ones((1, 128), dtype=np.float32)
    d["dmask"] = make_dmask(p)
    return d


def make_in_maps(inputs, phases, extra=None):
    need = set(phases)
    f = lambda k: np.ascontiguousarray(np.asarray(inputs[k], dtype=np.float32))
    nonce = int(np.random.randint(1, 2**30))
    maps = []
    for c in range(8):
        bb, p = c >> 1, c & 1
        m = dict(_consts(p, nonce))
        if "load_x" in need:
            m["x_own"] = np.ascontiguousarray(f("L0_attn_x")[bb][own_rows(p)])
        if "loadT" in need:
            m["xT_in"] = extra[c]
        if "l0attn" in need:
            for k in ("L0_attn_W_Q", "L0_attn_W_K", "L0_attn_W_V", "L0_attn_W_O"):
                m[k] = f(k)
            m["bqT"] = np.ascontiguousarray(f("L0_attn_b_Q").T)
            m["bkT"] = np.ascontiguousarray(f("L0_attn_b_K").T)
            m["bvT"] = np.ascontiguousarray(f("L0_attn_b_V").T)
            m["boT"] = np.ascontiguousarray(f("L0_attn_b_O").reshape(NC, 128).T)
        if "l1attn" in need:
            m["wqkvT"] = np.ascontiguousarray(f("L1_attn_w_qkv").T)
            m["woutT"] = np.ascontiguousarray(f("L1_attn_w_out").T)
            m["bqkvT"] = np.ascontiguousarray(
                f("L1_attn_b_qkv").reshape(3, H, 128).transpose(2, 0, 1).reshape(128, 3 * H))
            m["boutT"] = np.ascontiguousarray(f("L1_attn_b_out").reshape(NC, 128).T)
        if "quant" in need:
            wgu = f("L0_ffn_w_gate_up")
            m["qgu_my"] = np.ascontiguousarray(wgu[0:HG] if p == 0 else wgu[HG:])
            wpr = f("L0_ffn_w_proj")
            m["qpr_my"] = np.ascontiguousarray(wpr[p * 1024 : (p + 1) * 1024])
        if "moe" in need or "moe_noln" in need:
            m["gate_wT"] = np.ascontiguousarray(f("L1_ffn_gate_w").T)
            m["gate_b_col"] = np.ascontiguousarray(f("L1_ffn_gate_b").reshape(NE, 1))
            m["sel16"] = np.ascontiguousarray(np.repeat(np.eye(NE, dtype=np.float32), 128, axis=1))
            for k in ("L1_ffn_w1_shared", "L1_ffn_w2_shared", "L1_ffn_w3_shared",
                      "L1_ffn_w1_routed", "L1_ffn_w2_routed", "L1_ffn_w3_routed"):
                m[k] = f(k)
        maps.append(m)
    return maps


FUSED_PHASES = ("load_x", "l0attn", "quant", "ffn0", "l1attn", "moe", "store")


def kernel(**inputs):
    nc = build_program(list(FUSED_PHASES))
    in_maps = make_in_maps(inputs, FUSED_PHASES)
    res = run_bass_kernel_spmd(nc, in_maps, core_ids=list(range(8)))
    y = np.zeros((B, S, D), np.float32)
    for c in range(8):
        bb, p = c >> 1, c & 1
        y[bb][own_rows(p)] = res.results[c]["y_own"]
    return y
